# version 03
# baseline (speedup 1.0000x reference)
import numpy as np
import ml_dtypes
from contextlib import ExitStack

import concourse.bass as bass
import concourse.mybir as mybir
from concourse.bass_utils import run_bass_kernel_spmd

F32 = mybir.dt.float32
BF16 = mybir.dt.bfloat16
AF = mybir.ActivationFunctionType
ALU = mybir.AluOpType
BFNP = ml_dtypes.bfloat16

NCORES = 8
T = 16384
TC = T // NCORES
DM = 1024
NH = 16
HD = 64
CH = 128
NCH = TC // CH
NFC = DM // 128
NTT = TC // 512
NTG = TC // 128
E = 8
EI = 512
SI = 2048
LNEPS = 1e-5
GNEPS = 1e-5
RSCALE = 2.5

TEN, VEC, SCA, GPS, SYN = "tensor", "vector", "scalar", "gpsimd", "sync"
ENGINES = (TEN, VEC, SCA, GPS, SYN)



def pbcast(a, n):
    ap = [list(x) for x in a.ap]
    ap[0] = [0, n]
    return bass.AP(a.tensor, a.offset, ap)

class Prog:

    def __init__(self):
        self.ops = {e: [] for e in ENGINES}
        self.count = {}
        self.handles = {}

    def val(self, sem):
        return self.count.get(sem, 0)

    def bump(self, sem, n):
        self.count[sem] = self.count.get(sem, 0) + n
        return self.count[sem]

    def op(self, engine, build, waits=(), inc=None, inc_n=1, drain=None):
        waits = tuple((s, v) for (s, v) in waits if v > 0)
        if inc is not None:
            self.bump(inc, inc_n)
        if drain is None:
            drain = engine in (VEC, SCA)

        def fn(eng, _w=waits, _b=build, _i=inc, _n=inc_n, _d=drain):
            if _d:
                eng.drain()
            for s, v in _w:
                eng.wait_ge(self.handles[s], v)
            ins = _b(eng)
            if _i is not None:
                ins.then_inc(self.handles[_i], _n)

        self.ops[engine].append(fn)

    def dma(self, engine, out, in_, waits=(), inc=None):
        self.op(engine, lambda e, o=out, i=in_: e.dma_start(out=o, in_=i),
                waits=waits, inc=inc, inc_n=16)

    def barrier_wait(self, engine, sem, value):
        if value > 0:
            self.ops[engine].append(
                lambda eng, s=sem, v=value: eng.wait_ge(self.handles[s], v))

    def replay(self, block):
        for name in ENGINES:
            def run(eng, n=name):
                for fn in self.ops[n]:
                    fn(eng)
            getattr(block, name)(run)


class Builder:
    def __init__(self, taps=(), ncores=NCORES):
        self.ncores = ncores
        self.nc = bass.Bass("TRN2", target_bir_lowering=False)
        self.P = Prog()
        self.st = ExitStack()
        self.taps = {}
        self.want_taps = set(taps)
        self._declare()

    def din(self, name, shape, dt=F32):
        return self.nc.dram_tensor(name, shape, dt, kind="ExternalInput").ap()

    def dint(self, name, shape, dt):
        return self.nc.dram_tensor(name, shape, dt).ap()

    def sbuf(self, name, shape, dt):
        return self.st.enter_context(self.nc.sbuf_tensor(name, shape, dt))

    def psum(self, name, shape, dt=F32):
        return self.st.enter_context(self.nc.psum_tensor(name, shape, dt))

    def tap(self, name, ap, shape, dt, waits):
        if name not in self.want_taps:
            return
        t = self.nc.dram_tensor("tap_" + name, shape, dt,
                                kind="ExternalOutput").ap()
        self.taps[name] = t
        self.P.dma(SYN, t, ap,
                   waits=list(waits) + [("dT", self.P.val("dT"))], inc="dT")

    def _declare(self):
        nc, P = self.nc, self.P
        din, dint, sb, ps = self.din, self.dint, self.sbuf, self.psum
        self.x0 = din("x0", [DM, TC])
        self.wq = din("wq", [DM, DM]); self.wk = din("wk", [DM, DM])
        self.wv = din("wv", [DM, DM]); self.wg = din("wg", [DM, DM])
        self.wo = din("wo", [DM, DM])
        self.bqkvg = din("bqkvg", [128, 4 * NFC])
        self.bo_c = din("bo_c", [128, NFC])
        self.bab = din("bab", [16, 2])
        self.wa = din("wa", [DM, NH]); self.wb = din("wb", [DM, NH])
        self.convw = din("convw", [128, NFC * 12])
        self.gns = din("gns", [128, NFC]); self.gnb = din("gnb", [128, NFC])
        self.gatew = din("gatew", [DM, E]); self.gateb = din("gateb", [8, 1])
        self.gateb128 = din("gateb128", [128, 8])
        self.upw = din("upw", [E, DM, EI])
        self.downw = din("downw", [NFC, E, 4, 128, 128])
        self.shupw = din("shupw", [DM, SI])
        self.shdownw = din("shdownw", [NFC, 16, 128, 128])
        self.onehot = din("onehot", [8, 1])
        self.h0m = din("h0m", [1, 1])
        self.foldA = din("foldA", [64, 7])
        self.foldB = din("foldB", [64, 7 * 64])
        self.fold8 = din("fold8", [128, 16])
        self.ident = din("ident", [128, 128])
        self.identb = din("identb", [128, 128], BF16)
        self.tri2 = din("tri2", [128, 256])
        self.onescol = din("onescol", [128, 2], BF16)
        self.onescol32 = din("onescol32", [128, 2])
        self.selsum = din("selsum", [128, NFC * 16], BF16)
        self.selsum32 = din("selsum32", [128, NFC * 16])
        self.expand = din("expand", [16, NFC * 128], BF16)
        self.expand32 = din("expand32", [16, NFC * 128])
        self.iota8 = din("iota8", [128, 8])
        self.onesr = din("onesr", [1, 128])
        self.ones4 = din("ones4", [128, 128])
        self.zrow = din("zrow", [16, 512])
        self.epsc = din("epsc", [128, 2])
        self.y = nc.dram_tensor("y", [DM, TC], F32, kind="ExternalOutput").ap()

        self.upb = dint("upb", [E, DM, EI], BF16)
        self.downb = dint("downb", [NFC, E, 4, 128, 128], BF16)
        self.shupb = dint("shupb", [DM, SI], BF16)
        self.shdownb = dint("shdownb", [NFC, 16, 128, 128], BF16)
        self.wqb = dint("wqb", [DM, DM], BF16)
        self.wkb = dint("wkb", [DM, DM], BF16)
        self.wvb = dint("wvb", [DM, DM], BF16)
        self.wgb = dint("wgb", [DM, DM], BF16)
        self.wob = dint("wob", [DM, DM], BF16)
        self.gatewb = dint("gatewb", [DM, E], BF16)
        self.wab = dint("wab", [DM, NH], BF16)
        self.wbb = dint("wbb", [DM, NH], BF16)
        self.qhat_s = dint("qhat_s", [NCH, DM, CH], BF16)
        self.ksil_s = dint("ksil_s", [NCH, DM, CH], BF16)
        self.vsil_s = dint("vsil_s", [NCH, DM, CH], BF16)
        self.og_s = dint("og_s", [DM, TC], BF16)
        self.x1_s = dint("x1_s", [DM, TC], BF16)
        self.z_s = dint("z_s", [NCH, NH, HD, CH], BF16)
        self.o1_s = dint("o1_s", [NCH, NH, CH, HD], BF16)
        self.mc_s = dint("mc_s", [NCH, NH, HD, HD], BF16)
        self.q2_s = dint("q2_s", [NCH, NH, HD, HD], BF16)
        self.hal_in = dint("hal_in", [DM, 3], BF16)
        self.halo_b = dint("halo_b", [NFC, 1, 384], BF16)
        self.hal_out = dint("hal_out", [self.ncores * DM, 3], BF16)
        self.ag2_in = dint("ag2_in", [HD, NH * 2 * HD], BF16)
        self.ag2_out = dint("ag2_out", [self.ncores * HD, NH * 2 * HD], BF16)
        self.ag3_in = dint("ag3_in", [NH, 2], F32)
        self.ag3_out = dint("ag3_out", [self.ncores * NH, 2], F32)

        self.Xb = sb("Xb", [128, NFC * TC], BF16)
        self.T1 = sb("T1", [128, TC], F32)
        self.T3 = sb("T3", [128, TC], F32)
        self.VSTK = sb("VSTK", [128, TC], F32)
        self.STATS = sb("STATS", [128, 2048], F32)
        self.STGF = sb("STGF", [128, 2048], F32)
        self.RB = sb("RB", [128, TC], BF16)
        self.CONVW = sb("CONVW", [128, NFC * 12], F32)
        self.GNS = sb("GNS", [128, NFC], F32)
        self.GNB = sb("GNB", [128, NFC], F32)
        self.BQKVG = sb("BQKVG", [128, 4 * NFC], F32)
        self.BO = sb("BO", [128, NFC], F32)
        self.BAB = sb("BAB", [16, 2], F32)
        self.IDENT = sb("IDENT", [128, 128], F32)
        self.IDENTB = sb("IDENTB", [128, 128], BF16)
        self.TRI2 = sb("TRI2", [128, 256], F32)
        self.ONESC = sb("ONESC", [128, 2], BF16)
        self.ONESC32 = sb("ONESC32", [128, 2], F32)
        self.SELSUM = sb("SELSUM", [128, NFC * 16], BF16)
        self.SELSUM32 = sb("SELSUM32", [128, NFC * 16], F32)
        self.EXPAND = sb("EXPAND", [16, NFC * 128], BF16)
        self.EXPAND32 = sb("EXPAND32", [16, NFC * 128], F32)
        self.IOTA8 = sb("IOTA8", [128, 8], F32)
        self.ONESR = sb("ONESR", [1, 128], F32)
        self.ONES4 = sb("ONES4", [128, 128], F32)
        self.ZROW = sb("ZROW", [16, 512], F32)
        self.EPSC = sb("EPSC", [128, 2], F32)
        self.ONEHOT = sb("ONEHOT", [8, 1], BF16)
        self.H0M = sb("H0M", [1, 1], F32)
        self.FOLDA = sb("FOLDA", [64, 7], F32)
        self.FOLDB = sb("FOLDB", [64, 7 * 64], F32)
        self.FOLD8 = sb("FOLD8", [128, 16], F32)
        self.GCOLS = sb("GCOLS", [128, 4 * NH], F32)
        self.GCREP = sb("GCREP", [64, NH], F32)
        self.SLOC = sb("SLOC", [64, NH * HD], BF16)
        self.MCH = sb("MCH", [64, NH * HD], BF16)
        self.SS = sb("SS", [64, NH * HD], BF16)
        self.STRUE = sb("STRUE", [64, NH * HD], BF16)
        self.GSTATS = sb("GSTATS", [128, 16], F32)
        self.MRS = sb("MRS", [16, 8], F32)
        self.COEF = sb("COEF", [128, 2 * NFC], F32)
        self.R2TM = sb("R2TM", [128, NTG], F32)
        self.GATEB = sb("GATEB", [8, 1], F32)
        self.GATEB128 = sb("GATEB128", [128, 8], F32)
        self.YSTG = sb("YSTG", [128, 1024], F32)
        self.WSUMS = sb("WSUMS", [128, 4160], BF16)
        self.MROWB = sb("MROWB", [128, 2048], BF16)
        self.POOL16 = sb("POOL16", [128, 40512], BF16)
        self.POOL32 = sb("POOL32", [128, 2048], F32)

        self.PBIG = ps("PBIG", [128, 2048])
        self.PE = ps("PE", [128, 512]); self.PF = ps("PF", [128, 512])
        self.PG = ps("PG", [128, 512])
        self.PT = ps("PT", [128, 1024], BF16)

        for s in ("dT", "dG", "dG2", "dW", "dWs", "dUp0", "dUp1", "dDn", "dX", "dHs", "dHb", "dX10", "dX11", "dF0", "dF1", "dSp0", "dSp1", "dSl0", "dSl1", "dZ0", "dZ1", "dMc0", "dMc1", "dQ20", "dQ21", "dO10", "dO11", "Tm", "Vm", "Sm", "Gm", "cc"):
            P.handles[s] = self.st.enter_context(nc.semaphore(s))
        self._rb_free = 0
        self._r2_free = 0
        self._lnq_free = [0, 0, 0, 0]
        self._ffn_q_free = [0, 0, 0, 0]
        self._x1dma = 0
        self._ydma = 0
        self._stg_free = 0
        self._stg_dma = 0
        self._cc_done = 0
        self._hrow_v = 0
        self._hbounce_d = 0
        self._sq_free = 0
        self._ssq_v = 0
        self._ab_v = 0
        self._ab_s = 0
        self._spill_d = 0
        self._pbig_sfree = [0, 0, 0, 0]
        self._pbig_vfree = [0, 0, 0, 0]
        self._p2big_free = [0, 0, 0, 0]
        self._otm_free = 0
        self._p2wo_free = 0
        self._p2wo_sfree = 0
        self._xwo_free = 0
        self._en_qfree = [0, 0, 0, 0]

    def p16(self, off, width):
        assert off >= 0 and off + width <= 40512, (off, width)
        return self.POOL16[:, off:off + width]

    def p32(self, off, width):
        assert off >= 0 and off + width <= 2048, (off, width)
        return self.POOL32[:, off:off + width]

    def emit_loads(self):
        s, p = self, self.P
        for dst, src in [
            (s.IDENT, s.ident), (s.TRI2, s.tri2), (s.ONESC32, s.onescol32),
            (s.IOTA8, s.iota8), (s.H0M, s.h0m), (s.FOLDA, s.foldA), (s.FOLDB, s.foldB),
            (s.FOLD8, s.fold8),
            (s.GNS, s.gns), (s.GNB, s.gnb), (s.CONVW, s.convw),
            (s.SELSUM32, s.selsum32), (s.EXPAND32, s.expand32),
            (s.BQKVG, s.bqkvg), (s.BO, s.bo_c), (s.BAB, s.bab),
            (s.ONESR, s.onesr), (s.GATEB, s.gateb), (s.EPSC, s.epsc),
            (s.GATEB128, s.gateb128), (s.ZROW, s.zrow), (s.ONES4, s.ones4),
        ]:
            p.dma(SYN, dst[:], src, inc="dT")
        for dst, src in [(s.IDENTB, s.identb), (s.ONESC, s.onescol),
                         (s.SELSUM, s.selsum), (s.EXPAND, s.expand)]:
            p.dma(SYN, dst[:], src, inc="dT")
        p.dma(GPS, s.ONEHOT[:], s.onehot, inc="dG")
        for fc in range(NFC):
            p.dma(GPS, s.Xb[:, fc * TC:(fc + 1) * TC],
                  s.x0[fc * 128:(fc + 1) * 128, :], inc="dG")
        s.c_consts = p.val("dT")
        s.c_x = p.val("dG")
        for eng in (TEN, VEC, SCA):
            p.barrier_wait(eng, "dT", s.c_consts)
        p.barrier_wait(TEN, "dG", s.c_x)
        s.wcast_at = {}
        for nm, src, dst in [
            ("gate", s.gatew, s.gatewb), ("shup", s.shupw, s.shupb),
            ("up", s.upw, s.upb), ("down", s.downw, s.downb),
            ("shdown", s.shdownw, s.shdownb),
            ("wq", s.wq, s.wqb), ("wk", s.wk, s.wkb), ("wv", s.wv, s.wvb),
            ("wg", s.wg, s.wgb), ("wa", s.wa, s.wab), ("wb", s.wb, s.wbb),
            ("wo", s.wo, s.wob),
        ]:
            p.dma(GPS, dst, src, waits=[("dW", p.val("dW"))], inc="dW")
            s.wcast_at[nm] = p.val("dW")

        def wsum(dst_row, wsrc, ncols, wait_w):
            for oc in range(0, ncols, 512):
                ow = min(512, ncols - oc)
                for kc in range(NFC):
                    buf = s.p16(kc * 512, 512)
                    p.dma(SYN, buf[:, 0:ow],
                          wsrc[kc * 128:(kc + 1) * 128, oc:oc + ow],
                          waits=[("dW", wait_w),
                                 ("Tm", s._wsum_tm)], inc="dWs")
                dt = p.val("dWs")
                for kc in range(NFC):
                    buf = s.p16(kc * 512, 512)
                    p.op(TEN, lambda e, b=buf, ow=ow, k=kc: e.matmul(
                        s.PG[0:1, 0:ow], s.ONESC[:, 0:1], b[:, 0:ow],
                        start=(k == 0), stop=(k == NFC - 1)),
                        waits=([("dWs", dt), ("Vm", s._wsum_vm)]
                               if kc == 0 else ()),
                        inc=("Tm" if kc == NFC - 1 else None))
                s._wsum_tm = p.val("Tm")
                p.op(VEC, lambda e, d=dst_row, oc=oc, ow=ow: e.tensor_scalar(
                    d[:, oc:oc + ow], s.PG[0:1, 0:ow], -1.0, None, ALU.mult),
                    waits=[("Tm", s._wsum_tm)], inc="Vm")
                s._wsum_vm = p.val("Vm")

        s._wsum_tm = 0
        s._wsum_vm = 0
        wsum(s.WSUMS[32:33, 2048:2056], s.gatewb, E, s.wcast_at["gate"])
        wsum(s.WSUMS[32:33, 0:2048], s.shupb, SI, s.wcast_at["shup"])
        for e_ in range(E):
            wsum(s.WSUMS[0:1, e_ * 512:(e_ + 1) * 512],
                 s.upb[e_], EI, s.wcast_at["up"])
        for i_, nm in enumerate(("wq", "wk", "wv", "wg")):
            wsum(s.WSUMS[64:65, i_ * DM:(i_ + 1) * DM],
                 getattr(s, nm + "b"), DM, s.wcast_at[nm])
        wsum(s.WSUMS[64:65, 4 * DM: 4 * DM + 16], s.wab, NH,
             s.wcast_at["wa"])
        wsum(s.WSUMS[64:65, 4 * DM + 16: 4 * DM + 32], s.wbb, NH,
             s.wcast_at["wb"])
        s.v_wsum = p.val("Vm")
        s.t_wsum = p.val("Tm")

    LN_SQ = 24576

    def emit_ln(self, xwaits=()):
        s, p = self, self.P
        sq_last_use = [0, 0]
        for j in range(NTT):
            js = slice(j * 512, (j + 1) * 512)
            q = s.PBIG[:, js]
            for fc in range(NFC):
                xs = s.Xb[:, fc * TC + j * 512: fc * TC + j * 512 + 512]
                bi = fc % 2
                sq = s.p16(s.LN_SQ + bi * 512, 512)
                w = [("Tm", sq_last_use[bi])]
                if fc == 0:
                    w = w + list(xwaits) + [("Vm", s._lnq_free[j])]
                p.op(SCA, lambda e, x=xs, qq=sq: e.activation(qq[:], x, AF.Square),
                     waits=w, inc="Sm")
                sm = p.val("Sm")
                p.op(TEN, lambda e, x=xs, fc=fc, q=q: e.matmul(
                    q[0:1, :], s.ONESC[:, 0:1], x,
                    start=(fc == 0), stop=(fc == NFC - 1)),
                    waits=(list(xwaits) + [("Vm", s._lnq_free[j])]) if fc == 0 else ())
                p.op(TEN, lambda e, sq=sq, fc=fc, q=q: e.matmul(
                    q[32:33, :], s.ONESC[:, 0:1], sq[:],
                    start=(fc == 0), stop=(fc == NFC - 1)),
                    waits=[("Sm", sm)], inc="Tm")
                sq_last_use[bi] = p.val("Tm")
            tm = p.val("Tm")
            ST = s.STATS
            p.op(VEC, lambda e, js=js, q=q: e.tensor_scalar(
                ST[0:1, js], q[0:1, :], 1.0 / DM, None,
                ALU.mult), waits=[("Tm", tm)], inc="Vm")
            p.op(VEC, lambda e, js=js: e.tensor_scalar(
                s.MROWB[0:1, js], ST[0:1, js], 1.0, None, ALU.mult), inc="Vm")
            p.op(VEC, lambda e, js=js: e.tensor_scalar(
                s.MROWB[32:33, js], ST[0:1, js], 1.0, None, ALU.mult), inc="Vm")
            p.op(VEC, lambda e, js=js: e.tensor_scalar(
                s.MROWB[64:65, js], ST[0:1, js], 1.0, None, ALU.mult), inc="Vm")
            p.op(VEC, lambda e, js=js: e.tensor_tensor(
                ST[32:33, js], ST[0:1, js], ST[0:1, js], ALU.mult), inc="Vm")
            p.op(VEC, lambda e, js=js, q=q: e.scalar_tensor_tensor(
                ST[64:65, js], q[32:33, :], 1.0 / DM, ST[32:33, js],
                ALU.mult, ALU.subtract), inc="Vm")
            s._lnq_free[j] = p.val("Vm")
            vv = p.val("Vm")
            p.op(SCA, lambda e, js=js: e.activation(
                ST[64:65, js], ST[64:65, js], AF.Sqrt, bias=s.EPSC[0:1, 0:1]),
                waits=[("Vm", vv)], inc="Sm")
            sv = p.val("Sm")
            p.op(VEC, lambda e, js=js: e.reciprocal(
                ST[64:65, js], ST[64:65, js]), waits=[("Sm", sv)], inc="Vm")
            p.op(VEC, lambda e, js=js: e.tensor_scalar(
                s.MROWB[96:97, js], ST[64:65, js], 1.0, None, ALU.mult), inc="Vm")
            p.op(VEC, lambda e, js=js: e.tensor_tensor(
                ST[32:33, js], ST[64:65, js], ST[64:65, js], ALU.mult), inc="Vm")
            vv = p.val("Vm")
            p.op(TEN, lambda e, js=js: e.matmul(
                s.PF[0:128, 0:512], s.ONES4[64:65, 0:128], ST[64:65, js],
                start=True, stop=True),
                waits=[("Vm", vv), ("Vm", s._r2_free)], inc="Tm")
            tm_rb = p.val("Tm")
            p.op(VEC, lambda e, js=js: e.tensor_scalar(
                s.RB[:, js], s.PF[0:128, 0:512], 1.0, None, ALU.mult),
                waits=[("Tm", tm_rb)], inc="Vm")
            vv_rb = p.val("Vm")
            for gsub in range(4):
                g = 4 * j + gsub
                p.op(TEN, lambda e, j=j, gs=gsub: e.matmul(
                    s.PF[0:128, 0:1],
                    s.STATS[32:33, j * 512 + gs * 128: j * 512 + gs * 128 + 128],
                    s.IDENT[32:33, 32:33], is_transpose=True, start=True,
                    stop=True), waits=[("Vm", vv_rb), ("Vm", s._r2_free)],
                    inc="Tm")
                tm3 = p.val("Tm")
                p.op(VEC, lambda e, g=g: e.tensor_scalar(
                    s.R2TM[:, g:g + 1], s.PF[0:128, 0:1], 1.0,
                    None, ALU.mult), waits=[("Tm", tm3)], inc="Vm")
                s._r2_free = p.val("Vm")
        s.v_ln = p.val("Vm")
        s.s_ln = p.val("Sm")
        s.t_ln = p.val("Tm")
        s.g_ln = p.val("Gm")

    FF_R2CW = 0
    FF_R2SH = 16384
    FF_UP = 24576
    FF_DOWN = 32768
    FF_GATE = 38912

    def emit_router(self, xwaits=()):
        s, p = self, self.P
        gate_sb = s.p16(s.FF_GATE, 64)
        p.dma(GPS, gate_sb.rearrange("p (kc c) -> p kc c", c=8),
              s.gatewb.rearrange("(kc p) c -> p kc c", p=128),
              waits=[("dW", s.wcast_at["gate"])], inc="dG2")
        dgate = p.val("dG2")
        for j in range(NTT):
            q = s.PBIG[0:8, j * 512:(j + 1) * 512]
            xw = [("dG2", dgate), ("Vm", s._lnq_free[j])] + list(xwaits)
            for kc in range(NFC):
                xs = s.Xb[:, kc * TC + j * 512: kc * TC + j * 512 + 512]
                p.op(TEN, lambda e, q=q, xs=xs, kc=kc: e.matmul(
                    q, gate_sb[:, kc * 8:(kc + 1) * 8], xs,
                    start=(kc == 0), stop=False),
                    waits=(xw if kc == 0 else ()))
            p.op(TEN, lambda e, q=q, j=j: e.matmul(
                q, s.WSUMS[32:33, 2048:2056],
                s.MROWB[32:33, j * 512:(j + 1) * 512],
                start=False, stop=True),
                waits=[("Vm", s.v_wsum), ("Vm", s.v_ln)],
                inc="Tm")
            tm = p.val("Tm")
            p.op(VEC, lambda e, q=q, j=j: e.tensor_tensor(
                s.T1[64:72, j * 512:(j + 1) * 512], q,
                s.RB[0:8, j * 512:(j + 1) * 512], ALU.mult),
                waits=[("Tm", tm)], inc="Vm")
        vv = p.val("Vm")
        s._lnq_free = [p.val("Vm")] * NTT
        p.op(SCA, lambda e: e.activation(
            s.T1[64:72, :], s.T1[64:72, :],
            AF.Sigmoid), waits=[("Vm", vv)], inc="Sm")
        sv = p.val("Sm")
        RTM = self.p32(0, 256).rearrange("p (g k) -> p g k", k=16)
        tfree = [0, 0]
        for g in range(NTG):
            bank = s.PE if g % 2 == 0 else s.PF
            p.op(TEN, lambda e, g=g, bank=bank: e.matmul(
                bank[0:128, 0:8],
                s.T1[64:72, g * 128:(g + 1) * 128],
                s.IDENT[64:72, 64:72], is_transpose=True, start=True, stop=True),
                waits=[("Sm", sv), ("Vm", tfree[g % 2])], inc="Tm")
            tm = p.val("Tm")
            p.op(VEC, lambda e, g=g, bank=bank: e.tensor_scalar(
                self.p32(0, 256)[:, g * 16:g * 16 + 8], bank[0:128, 0:8],
                1.0, None, ALU.mult), waits=[("Tm", tm)], inc="Vm")
            p.op(VEC, lambda e, g=g: e.tensor_tensor(
                self.p32(0, 256)[:, g * 16 + 8:g * 16 + 16],
                self.p32(0, 256)[:, g * 16:g * 16 + 8],
                s.GATEB128[:, :], ALU.add), inc="Vm")
            tfree[g % 2] = p.val("Vm")
        SCR = 256

        def nxt(w):
            nonlocal SCR
            r = self.p32(SCR, w)
            SCR += w
            return r
        M1A = nxt(16); M2A = nxt(16); M1B = nxt(16); M2B = nxt(16)
        GSA = nxt(16); GSB = nxt(16); SELA = nxt(16); SELB = nxt(16)
        TMP64 = nxt(64)
        MSK = nxt(128); TMP128 = nxt(128)
        I1 = nxt(16); I2 = nxt(16); W1R = nxt(16); W2R = nxt(16)
        REC = nxt(16); SCL = nxt(16); IND1 = nxt(128); IND2 = nxt(128)
        IOTB4 = nxt(4); IOTB8 = nxt(8); CWTM = nxt(128)
        v_rt = p.val("Vm")
        V = lambda fn: p.op(VEC, fn, inc="Vm")

        def b16(ap, k):
            return ap.rearrange("p (g o) -> p g o", o=1).broadcast_to([128, 16, k])

        def g3(ap, k):
            return ap.rearrange("p (g k) -> p g k", k=k)

        V(lambda e: e.tensor_scalar(IOTB4, s.IOTA8[:, 0:4], 16.0, None, ALU.add))
        V(lambda e: e.tensor_scalar(IOTB8, s.IOTA8[:, 0:8], 16.0, None, ALU.add))
        sfc3 = g3(self.p32(0, 256), 16)
        for (lo, M1, M2, GS) in ((8, M1A, M2A, GSA), (12, M1B, M2B, GSB)):
            gsl = self.p32(0, 256).rearrange("p (g k) -> p g k", k=16)[:, :, lo:lo + 4]
            V(lambda e, gsl=gsl, M1=M1: e.tensor_reduce(
                M1, gsl, mybir.AxisListType.X, ALU.max))
            V(lambda e, gsl=gsl, M1=M1: e.tensor_tensor(
                g3(TMP64, 4), gsl, b16(M1, 4), ALU.is_equal))
            V(lambda e: e.scalar_tensor_tensor(
                g3(TMP64, 4), g3(TMP64, 4), -16.0,
                IOTB4.rearrange("p (o k) -> p o k", o=1).broadcast_to([128, 16, 4]),
                ALU.mult, ALU.add))
            V(lambda e, M2=M2: e.tensor_reduce(
                M2, g3(TMP64, 4), mybir.AxisListType.X, ALU.min))
            V(lambda e, M2=M2: e.tensor_tensor(
                g3(TMP64, 4),
                s.IOTA8[:, 0:4].rearrange("p (o k) -> p o k", o=1).broadcast_to([128, 16, 4]),
                b16(M2, 4), ALU.is_equal))
            V(lambda e, gsl=gsl: e.scalar_tensor_tensor(
                g3(TMP64, 4), g3(TMP64, 4), -1e9, gsl, ALU.mult, ALU.add))
            V(lambda e, M2=M2: e.tensor_reduce(
                M2, g3(TMP64, 4), mybir.AxisListType.X, ALU.max))
            V(lambda e, M1=M1, M2=M2, GS=GS: e.tensor_tensor(
                GS, M1, M2, ALU.add))
        V(lambda e: e.tensor_tensor(SELA, GSA, GSB, ALU.is_ge))
        V(lambda e: e.tensor_tensor(SELB, GSA, GSB, ALU.is_lt))
        V(lambda e: e.tensor_tensor(
            g3(MSK, 8)[:, :, 0:4], sfc3[:, :, 8:12], b16(SELA, 4), ALU.mult))
        V(lambda e: e.tensor_tensor(
            g3(MSK, 8)[:, :, 4:8], sfc3[:, :, 12:16], b16(SELB, 4), ALU.mult))
        V(lambda e: e.tensor_reduce(I1, g3(MSK, 8), mybir.AxisListType.X, ALU.max))
        V(lambda e: e.tensor_tensor(g3(TMP128, 8), g3(MSK, 8), b16(I1, 8), ALU.is_equal))
        V(lambda e: e.scalar_tensor_tensor(
            g3(TMP128, 8), g3(TMP128, 8), -16.0,
            IOTB8.rearrange("p (o k) -> p o k", o=1).broadcast_to([128, 16, 8]),
            ALU.mult, ALU.add))
        V(lambda e: e.tensor_reduce(I1, g3(TMP128, 8), mybir.AxisListType.X, ALU.min))
        V(lambda e: e.tensor_tensor(
            g3(IND1, 8),
            s.IOTA8.rearrange("p (o k) -> p o k", o=1).broadcast_to([128, 16, 8]),
            b16(I1, 8), ALU.is_equal))
        V(lambda e: e.scalar_tensor_tensor(
            g3(TMP128, 8), g3(IND1, 8), -1e9, g3(MSK, 8), ALU.mult, ALU.add))
        V(lambda e: e.tensor_reduce(I2, g3(TMP128, 8), mybir.AxisListType.X, ALU.max))
        V(lambda e: e.tensor_tensor(g3(MSK, 8), g3(TMP128, 8), b16(I2, 8), ALU.is_equal))
        V(lambda e: e.scalar_tensor_tensor(
            g3(MSK, 8), g3(MSK, 8), -16.0,
            IOTB8.rearrange("p (o k) -> p o k", o=1).broadcast_to([128, 16, 8]),
            ALU.mult, ALU.add))
        V(lambda e: e.tensor_reduce(I2, g3(MSK, 8), mybir.AxisListType.X, ALU.min))
        V(lambda e: e.tensor_tensor(
            g3(IND2, 8),
            s.IOTA8.rearrange("p (o k) -> p o k", o=1).broadcast_to([128, 16, 8]),
            b16(I2, 8), ALU.is_equal))
        V(lambda e: e.tensor_tensor(g3(TMP128, 8), g3(IND1, 8), sfc3[:, :, 0:8], ALU.mult))
        V(lambda e: e.tensor_reduce(W1R, g3(TMP128, 8), mybir.AxisListType.X, ALU.add))
        V(lambda e: e.tensor_tensor(g3(TMP128, 8), g3(IND2, 8), sfc3[:, :, 0:8], ALU.mult))
        V(lambda e: e.tensor_reduce(W2R, g3(TMP128, 8), mybir.AxisListType.X, ALU.add))
        V(lambda e: e.tensor_tensor(REC, W1R, W2R, ALU.add))
        V(lambda e: e.tensor_scalar(REC, REC, 1e-20, None, ALU.add))
        V(lambda e: e.reciprocal(REC, REC))
        V(lambda e: e.tensor_tensor(SCL, REC, s.R2TM[:, :], ALU.mult))
        V(lambda e: e.tensor_scalar(SCL, SCL, RSCALE, None, ALU.mult))
        V(lambda e: e.tensor_tensor(g3(CWTM, 8), g3(IND1, 8), b16(W1R, 8), ALU.mult))
        V(lambda e: e.tensor_tensor(g3(TMP128, 8), g3(IND2, 8), b16(W2R, 8), ALU.mult))
        V(lambda e: e.tensor_tensor(g3(CWTM, 8), g3(CWTM, 8), g3(TMP128, 8), ALU.add))
        V(lambda e: e.tensor_tensor(g3(CWTM, 8), g3(CWTM, 8), b16(SCL, 8), ALU.mult))
        vcw = p.val("Vm")
        tfree = [p.val("Vm"), p.val("Vm")]
        for g in range(NTG):
            bank = s.PE if g % 2 == 0 else s.PF
            p.op(TEN, lambda e, g=g, bank=bank: e.matmul(
                bank[0:8, 0:128], CWTM[:, g * 8:(g + 1) * 8],
                s.IDENT[0:128, 0:128], is_transpose=True, start=True,
                stop=True), waits=[("Vm", vcw), ("Vm", tfree[g % 2])],
                inc="Tm")
            tm = p.val("Tm")
            p.op(VEC, lambda e, g=g, bank=bank: e.tensor_scalar(
                s.T3[96:104, g * 128:(g + 1) * 128],
                bank[0:8, 0:128], 1.0, None, ALU.mult),
                waits=[("Tm", tm)], inc="Vm")
            tfree[g % 2] = p.val("Vm")
        s.v_router = p.val("Vm")

    def emit_ffn(self, stage):
        s, p = self, self.P
        up_free = [self.t_ln, self.t_ln]
        down_free = [0, 0]
        pbig_free = [p.val("Sm")] * 4
        cwb_free = [0, 0]
        dup = [0, 0]
        ddown = [0, 0]
        x_written = []
        for j in range(NTT):
            js = slice(j * 512, (j + 1) * 512)
            mrow = s.MROWB[0:1, j * 512:(j + 1) * 512]
            mrow32 = s.MROWB[32:33, j * 512:(j + 1) * 512]
            for scb in range(4):
                for sci in range(4):
                    sc = 4 * scb + sci
                    slot = sci % 2
                    buf = s.p16(s.FF_UP + slot * 4096, 1024)
                    p.dma(SYN, buf.rearrange("p (kc c) -> p kc c", c=128),
                          s.shupb.rearrange("(kc p) c -> p kc c", p=128)
                          [:, :, sc * 128:(sc + 1) * 128],
                          waits=[("dW", s.wcast_at["shup"]),
                                 ("Tm", up_free[slot])],
                          inc="dUp%d" % slot)
                    dup[slot] = p.val("dUp%d" % slot)
                    q = s.PBIG[:, sci * 512:(sci + 1) * 512]
                    for kc in range(NFC):
                        xs = s.Xb[:, kc * TC + j * 512: kc * TC + j * 512 + 512]
                        p.op(TEN, lambda e, q=q, b=buf, kc=kc, xs=xs: e.matmul(
                            q, b[:, kc * 128:(kc + 1) * 128], xs,
                            start=(kc == 0), stop=False),
                            waits=([("dUp%d" % slot, dup[slot]),
                                    ("Sm", pbig_free[sci])] if kc == 0 else ()))
                    p.op(TEN, lambda e, q=q, sc=sc, m32=mrow32: e.matmul(
                        q, s.WSUMS[32:33, sc * 128:(sc + 1) * 128],
                        m32, start=False, stop=True),
                        waits=[("Vm", s.v_wsum)],
                        inc="Tm")
                    up_free[slot] = p.val("Tm")
                tm = p.val("Tm")
                rsl = s.p16(s.FF_R2SH + scb * 2048, 2048)
                p.op(SCA, lambda e, rsl=rsl: e.activation(
                    rsl, s.PBIG[:, 0:2048], AF.Relu), waits=[("Tm", tm)],
                    inc="Sm")
                pbig_free = [p.val("Sm")] * 4
                sm = p.val("Sm")
                p.op(VEC, lambda e, rsl=rsl, j=j: e.tensor_tensor(
                    rsl.rearrange("p (sc c) -> p sc c", c=512),
                    rsl.rearrange("p (sc c) -> p sc c", c=512),
                    s.RB[:, j * 512:(j + 1) * 512]
                    .rearrange("p (o c) -> p o c", o=1)
                    .broadcast_to([128, 4, 512]), ALU.mult),
                    waits=[("Sm", sm)], inc="Vm")
                p.op(VEC, lambda e, rsl=rsl: e.tensor_tensor(
                    rsl, rsl, rsl, ALU.mult), inc="Vm")
            v_r2sh = p.val("Vm")
            for r_ in range(2):
                p.dma(SYN, s.STGF[32 * r_:32 * r_ + 1, 0:2048]
                      .rearrange("o (u c) -> o u c", c=512),
                      s.T3[96 + 4 * r_:100 + 4 * r_, js],
                      waits=[("Vm", s.v_router), ("Tm", s._stg_free)],
                      inc="dT")
            s._stg_dma = p.val("dT")
            for e_ in range(E):
                slot = e_ % 2
                buf = s.p16(s.FF_UP + slot * 4096, 4096)
                p.dma(SYN, buf.rearrange("p (kc c) -> p kc c", c=EI),
                      s.upb[e_].rearrange("(kc p) c -> p kc c", p=128),
                      waits=[("dW", s.wcast_at["up"]),
                             ("Tm", up_free[slot])], inc="dUp%d" % slot)
                dup[slot] = p.val("dUp%d" % slot)
                cbank = s.PE if e_ % 2 == 0 else s.PF
                p.op(TEN, lambda e, cbank=cbank, e_=e_: e.matmul(
                    cbank[:, 0:512],
                    s.ONES4[32 * (e_ // 4):32 * (e_ // 4) + 1, 0:128],
                    s.STGF[32 * (e_ // 4):32 * (e_ // 4) + 1,
                           (e_ % 4) * 512:(e_ % 4) * 512 + 512],
                    start=True, stop=True),
                    waits=[("dT", s._stg_dma), ("Vm", cwb_free[e_ % 2])],
                    inc="Tm")
                tcb = p.val("Tm")
                cb = s.p16(s.FF_GATE + 64 + (e_ % 2) * 512, 512)
                p.op(VEC, lambda e, cb=cb, cbank=cbank: e.tensor_scalar(
                    cb, cbank[:, 0:512], 1.0, None, ALU.mult),
                    waits=[("Tm", tcb)], inc="Vm")
                t_cwb = p.val("Vm")
                s._stg_free = p.val("Tm")
                for ic in range(4):
                    q = s.PBIG[:, ic * 512:(ic + 1) * 512]
                    for kc in range(NFC):
                        xs = s.Xb[:, kc * TC + j * 512: kc * TC + j * 512 + 512]
                        p.op(TEN, lambda e, q=q, b=buf, kc=kc, ic=ic, xs=xs:
                             e.matmul(q, b[:, kc * EI + ic * 128:
                                           kc * EI + ic * 128 + 128], xs,
                                      start=(kc == 0), stop=False),
                             waits=([("dUp%d" % slot, dup[slot]),
                                     ("Sm", pbig_free[ic])] if kc == 0 else ()))
                    p.op(TEN, lambda e, q=q, e_=e_, ic=ic, mrow=mrow: e.matmul(
                        q, s.WSUMS[0:1, e_ * 512 + ic * 128:
                                   e_ * 512 + ic * 128 + 128],
                        mrow, start=False, stop=True),
                        inc=("Tm" if ic == 3 else None))
                up_free[slot] = p.val("Tm")
                tm = p.val("Tm")
                rsl = s.p16(s.FF_R2CW + e_ * 2048, 2048)
                p.op(SCA, lambda e, rsl=rsl: e.activation(
                    rsl, s.PBIG[:, 0:2048], AF.Relu), waits=[("Tm", tm)],
                    inc="Sm")
                pbig_free = [p.val("Sm")] * 4
                sm = p.val("Sm")
                t1 = s.p32(0, 2048)
                p.op(VEC, lambda e, rsl=rsl, cb=cb: e.tensor_tensor(
                    t1.rearrange("p (ic c) -> p ic c", c=512),
                    rsl.rearrange("p (ic c) -> p ic c", c=512),
                    cb.rearrange("p (o c) -> p o c", o=1)
                    .broadcast_to([128, 4, 512]), ALU.mult),
                    waits=[("Sm", sm), ("Vm", t_cwb)], inc="Vm")
                cwb_free[e_ % 2] = p.val("Vm")
                p.op(VEC, lambda e, rsl=rsl, t1=t1: e.tensor_tensor(
                    rsl, t1, rsl, ALU.mult), inc="Vm")
            v_r2cw = p.val("Vm")
            for oc in range(NFC):
                slot = 0
                dbuf = s.p16(s.FF_DOWN, 6144)
                p.dma(SYN, dbuf[:, 0:4096].rearrange("p (u c) -> p u c", c=128),
                      s.downb[oc].rearrange("e i p c -> p (e i) c"),
                      waits=[("dW", s.wcast_at["down"]),
                             ("Tm", down_free[slot])], inc="dDn")
                p.dma(SYN, dbuf[:, 4096:6144].rearrange("p (u c) -> p u c", c=128),
                      s.shdownb[oc].rearrange("u p c -> p u c"),
                      waits=[("dW", s.wcast_at["shdown"])], inc="dDn")
                ddown[slot] = p.val("dDn")
                q = s.PBIG[:, (oc % 4) * 512:(oc % 4) * 512 + 512]
                first = True
                for e_ in range(E):
                    for ic in range(4):
                        rsl = s.p16(s.FF_R2CW + e_ * 2048 + ic * 512, 512)
                        w = []
                        if first:
                            w = [("dDn", ddown[slot]), ("Vm", v_r2cw),
                                 ("Vm", s._ffn_q_free[oc % 4])]
                        p.op(TEN, lambda e, q=q, d=dbuf, e_=e_, ic=ic, rsl=rsl:
                             e.matmul(q, d[:, (e_ * 4 + ic) * 128:
                                           (e_ * 4 + ic) * 128 + 128], rsl,
                                      start=(e_ == 0 and ic == 0), stop=False),
                             waits=w)
                        first = False
                for sc in range(16):
                    rsl = s.p16(s.FF_R2SH + sc * 512, 512)
                    p.op(TEN, lambda e, q=q, d=dbuf, sc=sc, rsl=rsl: e.matmul(
                        q, d[:, 4096 + sc * 128: 4096 + sc * 128 + 128], rsl,
                        start=False, stop=(sc == 15)),
                        waits=([("Vm", v_r2sh)] if sc == 0 else ()),
                        inc=("Tm" if sc == 15 else None))
                down_free[slot] = p.val("Tm")
                tm = p.val("Tm")
                xs = s.Xb[:, oc * TC + j * 512: oc * TC + j * 512 + 512]
                if stage == 1:
                    ot = s.p16(s.FF_GATE + 1088, 512)
                    p.op(VEC, lambda e, q=q, xs=xs, ot=ot:
                         e.scalar_tensor_tensor(ot, q, 0.5, xs, ALU.mult,
                                                ALU.add),
                         waits=[("Tm", tm), ("dX", s._x1dma)], inc="Vm")
                    s._ffn_q_free[oc % 4] = p.val("Vm")
                    vq = p.val("Vm")
                    p.dma(SYN, s.x1_s[oc * 128:(oc + 1) * 128, js], ot,
                          waits=[("Vm", vq)], inc="dX")
                    s._x1dma = p.val("dX")
                else:
                    ot = s.YSTG[:, (oc % 2) * 512:(oc % 2) * 512 + 512]
                    p.op(VEC, lambda e, q=q, xs=xs, ot=ot:
                         e.scalar_tensor_tensor(ot, q, 0.5, xs, ALU.mult,
                                                ALU.add),
                         waits=[("Tm", tm), ("dX", s._ydma)], inc="Vm")
                    s._ffn_q_free[oc % 4] = p.val("Vm")
                    vq = p.val("Vm")
                    p.dma(SYN, s.y[oc * 128:(oc + 1) * 128, js], ot,
                          waits=[("Vm", vq)], inc="dX")
                    s._ydma = p.val("dX")
        s.v_ffn = p.val("Vm")
        s.d_ffn = p.val("dT")

    AT_W = 0
    AT_PRE = 16384
    AT_CONV = 32800
    AT_SQ = 36896
    AT_HALG = 38944

    def emit_attn_proj(self):
        s, p = self, self.P
        p.op(VEC, lambda e: e.memset(s.VSTK[:, :], 0.0), inc="Vm")
        p.op(VEC, lambda e: e.memset(s.T1[:, :], 0.0), inc="Vm")
        for fc in range(NFC):
            p.dma(GPS, s.Xb[:, fc * TC:(fc + 1) * TC],
                  s.x1_s[fc * 128:(fc + 1) * 128, :],
                  waits=[("dX", s._x1dma), ("Vm", s.v_ffn)], inc="dG2")
        s.c_x1 = p.val("dG2")
        self.emit_ln(xwaits=[("dG2", s.c_x1)])

        s._pbig_sfree = [p.val("Sm")] * NTT
        s._pbig_vfree = [p.val("Vm")] * NTT
        wslots = [s.p16(s.AT_W, 8192), s.p16(s.AT_W + 8192, 8192)]
        wsem = ["dUp0", "dUp1"]
        wfree = [p.val("Tm"), p.val("Tm")]
        dw = [0, 0]
        projs = [("q", s.wqb, 0, s.qhat_s), ("k", s.wkb, 1, s.ksil_s),
                 ("v", s.wvb, 2, s.vsil_s), ("g", s.wgb, 3, None)]
        pre = s.p16(s.AT_PRE, 16416)
        prev_spill = {0: 0, 1: 0}
        s._d_spill = 0
        s._v_sumsq = 0
        s._halg_free = 0
        conv_free = [0, 0]
        sumsq_rows = {"q": 32, "k": 64}
        for pi, (pname, wdram, pidx, spill) in enumerate(projs):
            slot = pi % 2
            wv = wslots[slot]
            p.dma(SYN, wv.rearrange("p (kc c) -> p kc c", c=DM),
                  wdram.rearrange("(kc p) c -> p kc c", p=128),
                  waits=[("dW", s.wcast_at["w" + pname]),
                         ("Tm", wfree[slot])], inc=wsem[slot])
            dw[slot] = p.val(wsem[slot])
            if pname in ("q", "k"):
                r0 = sumsq_rows[pname]
                p.op(VEC, lambda e, r0=r0: e.memset(s.T3[r0:r0 + 16, :], 0.0),
                     inc="Vm")
            for oc in range(NFC):
                for j in range(NTT):
                    q_ = s.PBIG[:, j * 512:(j + 1) * 512]
                    for kc in range(NFC):
                        xs = s.Xb[:, kc * TC + j * 512: kc * TC + j * 512 + 512]
                        p.op(TEN, lambda e, q=q_, w=wv, kc=kc, oc=oc, xs=xs:
                             e.matmul(q, w[:, kc * DM + oc * 128:
                                           kc * DM + oc * 128 + 128], xs,
                                      start=(kc == 0), stop=False),
                             waits=([("dG2", s.c_x1), (wsem[slot], dw[slot]),
                                     ("Sm", s._pbig_sfree[j]),
                                     ("Vm", s._pbig_vfree[j])]
                                    if kc == 0 else ()))
                    p.op(TEN, lambda e, q=q_, oc=oc, j=j, pidx=pidx: e.matmul(
                        q, s.WSUMS[64:65, pidx * DM + oc * 128:
                                   pidx * DM + oc * 128 + 128],
                        s.MROWB[64:65, j * 512:(j + 1) * 512],
                        start=False, stop=True),
                        waits=[("Vm", s.v_wsum)],
                        inc=("Tm" if j == NTT - 1 else None))
                tm = p.val("Tm")
                tbuf = s.p16(s.AT_SQ, 2048)
                p.op(VEC, lambda e, tbuf=tbuf: e.tensor_tensor(
                    tbuf.rearrange("p (j c) -> p j c", c=512),
                    s.PBIG.rearrange("p (j c) -> p j c", c=512),
                    s.RB.rearrange("p (j c) -> p j c", c=512),
                    ALU.mult), waits=[("Tm", tm), ("Vm", s.v_ln),
                                      ("Tm", s._sq_free)], inc="Vm")
                s._pbig_vfree = [p.val("Vm")] * NTT
                p.op(VEC, lambda e, tbuf=tbuf, oc=oc, pidx=pidx: e.tensor_scalar(
                    pre[:, oc * 2052 + 3: oc * 2052 + 2051],
                    tbuf[:, 0:2048], s.BQKVG[:, pidx * NFC + oc:
                                             pidx * NFC + oc + 1],
                    None, ALU.add), inc="Vm")
                s._pre_v = p.val("Vm")
            if pname == "g":
                dhalo = p.val("dT")
                for oc in range(NFC):
                    cbuf = s.p16(s.AT_CONV + (oc % 2) * 2048, 2048)
                    p.op(SCA, lambda e, cbuf=cbuf, oc=oc: e.activation(
                        cbuf, pre[:, oc * 2052 + 3: oc * 2052 + 2051],
                        AF.Silu), waits=[("Vm", s._pre_v),
                                         ("dSp%d" % (oc % 2),
                                          conv_free[oc % 2])], inc="Sm")
                    sv = p.val("Sm")
                    p.dma(SYN, s.og_s[oc * 128:(oc + 1) * 128, :], cbuf,
                          waits=[("Sm", sv)], inc="dSp%d" % (oc % 2))
                    conv_free[oc % 2] = p.val("dSp%d" % (oc % 2))
                wfree[slot] = p.val("Tm")
                continue
            for oc in range(NFC):
                p.dma(SYN, s.hal_in[oc * 128:(oc + 1) * 128, 0:3],
                      pre[:, oc * 2052 + 2048: oc * 2052 + 2051],
                      waits=[("Vm", s._pre_v), ("cc", s._cc_done)], inc="dHs")
            dstage = p.val("dHs")
            p.op(GPS, lambda e: e.collective_compute(
                "AllGather", ALU.bypass, replica_groups=[list(range(self.ncores))],
                ins=[s.hal_in.opt()], outs=[s.hal_out.opt()]),
                waits=[("dHs", dstage)], inc="cc")
            s._cc_done = p.val("cc")
            hal_flat = s.hal_out.rearrange("(r f) c -> r (f c)", r=self.ncores)
            for oc in range(NFC):
                hg = s.p16(s.AT_HALG, 512)
                p.dma(GPS, hg[0:self.ncores, 0:384], hal_flat[:, oc * 384:(oc + 1) * 384],
                      waits=[("cc", s._cc_done), ("Tm", s._halg_free)],
                      inc="dG2")
                dh = p.val("dG2")
                p.op(TEN, lambda e, hg=hg: e.matmul(
                    s.PE[0:1, 0:384], s.ONEHOT[0:self.ncores, 0:1],
                    hg[0:self.ncores, 0:384],
                    start=True, stop=True),
                    waits=[("dG2", dh), ("Vm", s._hrow_v)], inc="Tm")
                s._halg_free = p.val("Tm")
                th = p.val("Tm")
                hrow = s.p16(s.AT_HALG + 512, 512)
                p.op(VEC, lambda e, hrow=hrow: e.tensor_scalar(
                    hrow[0:1, 0:384], s.PE[0:1, 0:384], s.H0M[0:1, 0:1], None,
                    ALU.mult), waits=[("Tm", th), ("dHb", s._hbounce_d)],
                    inc="Vm")
                vh = p.val("Vm")
                s._hrow_v = vh
                p.dma(SYN, s.halo_b[oc], hrow[0:1, 0:384],
                      waits=[("Vm", vh)], inc="dHb")
                db = p.val("dHb")
                p.dma(SYN, pre[:, oc * 2052: oc * 2052 + 3],
                      s.halo_b[oc].rearrange("o (q c) -> (o q) c", c=3),
                      waits=[("dHb", db)], inc="dHb")
                s._hbounce_d = p.val("dHb")
            dhalo = p.val("dHb")
            for oc in range(NFC):
                cbuf = s.p16(s.AT_CONV + (oc % 2) * 2048, 2048)
                for tap in range(4):
                    wcol = s.CONVW[:, oc * 12 + pidx * 4 + tap:
                                   oc * 12 + pidx * 4 + tap + 1]
                    if tap == 0:
                        p.op(VEC, lambda e, cbuf=cbuf, oc=oc, w=wcol:
                             e.tensor_scalar(cbuf, pre[:, oc * 2052:
                                                       oc * 2052 + 2048],
                                             w, None, ALU.mult),
                             waits=[("dHb", dhalo),
                                    ("dSp%d" % (oc % 2), conv_free[oc % 2])],
                             inc="Vm")
                    else:
                        p.op(VEC, lambda e, cbuf=cbuf, oc=oc, tap=tap, w=wcol:
                             e.scalar_tensor_tensor(
                                 cbuf, pre[:, oc * 2052 + tap:
                                           oc * 2052 + 2048 + tap], w,
                                 cbuf, ALU.mult, ALU.add), inc="Vm")
                p.op(SCA, lambda e, cbuf=cbuf: e.activation(
                    cbuf, cbuf, AF.Silu), waits=[("Vm", p.val("Vm"))], inc="Sm")
                sv = p.val("Sm")
                p.dma(SYN, spill.rearrange("n (f p) c -> p n f c", p=128)
                      [:, :, oc, :], cbuf.rearrange("p (n c) -> p n c", c=CH),
                      waits=[("Sm", sv)], inc="dSp%d" % (oc % 2))
                if pname in ("q", "k"):
                    sq = s.p16(s.AT_SQ, 2048)
                    p.op(VEC, lambda e, cbuf=cbuf, sq=sq: e.tensor_tensor(
                        sq, cbuf, cbuf, ALU.mult),
                        waits=[("Sm", sv), ("Tm", s._sq_free)], inc="Vm")
                    vsq = p.val("Vm")
                    r0 = sumsq_rows[pname]
                    for j in range(NTT):
                        p.op(TEN, lambda e, sq=sq, j=j, oc=oc: e.matmul(
                            s.PF[0:16, 0:512],
                            s.SELSUM[:, oc * 16:(oc + 1) * 16],
                            sq[:, j * 512:(j + 1) * 512],
                            start=True, stop=True),
                            waits=[("Vm", vsq), ("Vm", s._ssq_v)], inc="Tm")
                        s._sq_free = p.val("Tm")
                        tms = p.val("Tm")
                        p.op(VEC, lambda e, r0=r0, j=j: e.tensor_tensor(
                            s.T3[r0:r0 + 16, j * 512:(j + 1) * 512],
                            s.T3[r0:r0 + 16, j * 512:(j + 1) * 512],
                            s.PF[0:16, 0:512], ALU.add),
                            waits=[("Tm", tms)], inc="Vm")
                        s._ssq_v = p.val("Vm")
                conv_free[oc % 2] = p.val("dSp%d" % (oc % 2))
            wfree[slot] = p.val("Tm")
        s.v_proj = p.val("Vm")

        wab_sb = s.p16(s.AT_W, 256)
        p.dma(SYN, wab_sb[:, 0:128].rearrange("p (kc c) -> p kc c", c=16),
              s.wab.rearrange("(kc p) c -> p kc c", p=128),
              waits=[("dW", s.wcast_at["wa"]), ("Tm", wfree[0])], inc="dUp0")
        p.dma(SYN, wab_sb[:, 128:256].rearrange("p (kc c) -> p kc c", c=16),
              s.wbb.rearrange("(kc p) c -> p kc c", p=128),
              waits=[("dW", s.wcast_at["wb"])], inc="dUp0")
        dab = p.val("dUp0")
        for ab in range(2):
            dst = s.T1[0:16, :] if ab == 0 else s.T1[32:48, :]
            for j in range(NTT):
                q_ = s.PE[0:16, 0:512]
                for kc in range(NFC):
                    xs = s.Xb[:, kc * TC + j * 512: kc * TC + j * 512 + 512]
                    p.op(TEN, lambda e, q=q_, kc=kc, ab=ab, xs=xs: e.matmul(
                        q, wab_sb[:, ab * 128 + kc * 16: ab * 128 + kc * 16 + 16],
                        xs, start=(kc == 0), stop=False),
                        waits=([("dUp0", dab), ("Vm", s._ab_v)] if kc == 0 else ()))
                p.op(TEN, lambda e, q=q_, ab=ab, j=j: e.matmul(
                    q, s.WSUMS[64:65, 4 * DM + ab * 16: 4 * DM + ab * 16 + 16],
                    s.MROWB[64:65, j * 512:(j + 1) * 512],
                    start=False, stop=True), inc="Tm")
                tm = p.val("Tm")
                p.op(VEC, lambda e, j=j: e.tensor_tensor(
                    s.T3[96:112, j * 512:(j + 1) * 512], q_,
                    s.RB[0:16, j * 512:(j + 1) * 512], ALU.mult),
                    waits=[("Tm", tm), ("Sm", s._ab_s)], inc="Vm")
                s._ab_v = p.val("Vm")
                vv = p.val("Vm")
                p.op(SCA, lambda e, dst=dst, j=j, ab=ab: e.activation(
                    dst[:, j * 512:(j + 1) * 512],
                    s.T3[96:112, j * 512:(j + 1) * 512], AF.Sigmoid,
                    bias=s.BAB[:, ab:ab + 1]), waits=[("Vm", vv)], inc="Sm")
                s._ab_s = p.val("Sm")
        s.s_ab = p.val("Sm")
        s._ab_tm = p.val("Tm")

        V = lambda fn, w=(): p.op(VEC, fn, waits=w, inc="Vm")
        S_ = lambda fn, w=(): p.op(SCA, fn, waits=w, inc="Sm")
        S_(lambda e: e.activation(s.T3[96:112, :], s.T1[0:16, :], AF.Ln),
           w=[("Sm", s.s_ab), ("Vm", p.val("Vm"))])
        sv = p.val("Sm")
        V(lambda e: e.tensor_scalar(s.T3[96:112, :], s.T3[96:112, :], -1.0,
                                    None, ALU.mult), w=[("Sm", sv)])
        for j_ in range(NTT):
            ini = (0.0 if j_ == 0 else
                   s.VSTK[96:112, j_ * 512 - 1: j_ * 512])
            V(lambda e, j_=j_, ini=ini: e.tensor_tensor_scan(
                s.VSTK[96:112, j_ * 512:(j_ + 1) * 512],
                s.T3[96:112, j_ * 512:(j_ + 1) * 512],
                s.T1[96:112, 0:512], ini, ALU.add, ALU.add),
              w=[("dT", s.c_consts)])
        S_(lambda e: e.activation(s.T3[32:48, :], s.T3[32:48, :], AF.Sqrt),
           w=[("Vm", s.v_proj)])
        sv = p.val("Sm")
        V(lambda e: e.tensor_scalar(s.T3[32:48, :], s.T3[32:48, :], 1e-6,
                                    None, ALU.add), w=[("Sm", sv)])
        S_(lambda e: e.activation(s.T3[32:48, :], s.T3[32:48, :], AF.Ln),
           w=[("Vm", p.val("Vm"))])
        sv = p.val("Sm")
        V(lambda e: e.tensor_scalar(s.VSTK[32:48, :], s.VSTK[96:112, :], 1.0,
                                    None, ALU.mult), w=[("Sm", sv)])
        V(lambda e: e.tensor_tensor(s.VSTK[64:80, :], s.T3[32:48, :],
                                    s.VSTK[32:48, :], ALU.add))
        V(lambda e: e.tensor_scalar(s.VSTK[64:80, :], s.VSTK[64:80, :], -1.0,
                                    None, ALU.mult))
        S_(lambda e: e.activation(s.T3[64:80, :], s.T3[64:80, :], AF.Sqrt),
           w=[("Vm", s.v_proj)])
        sv = p.val("Sm")
        V(lambda e: e.tensor_scalar(s.T3[64:80, :], s.T3[64:80, :], 1e-6,
                                    None, ALU.add), w=[("Sm", sv)])
        V(lambda e: e.reciprocal(s.T3[0:16, :], s.T3[64:80, :]))
        vv = p.val("Vm")
        S_(lambda e: e.activation(s.T3[64:80, :], s.T3[64:80, :], AF.Ln),
           w=[("Vm", vv)])
        s_lnsk = p.val("Sm")
        V(lambda e: e.tensor_scalar(s.T3[96:112, :], s.T3[64:80, :], 1.0,
                                    None, ALU.mult), w=[("Sm", s_lnsk)])
        V(lambda e: e.tensor_tensor(s.VSTK[0:16, :], s.VSTK[96:112, :],
                                    s.T3[96:112, :], ALU.subtract))
        S_(lambda e: e.activation(s.T1[96:112, :], s.T1[32:48, :], AF.Ln),
           w=[("Vm", p.val("Vm"))])
        s_lnb = p.val("Sm")
        V(lambda e: e.tensor_tensor(s.VSTK[32:48, :], s.T1[96:112, :],
                                    s.T3[96:112, :], ALU.subtract),
          w=[("Sm", s_lnb)])
        V(lambda e: e.tensor_scalar(s.T3[32:48, :], s.VSTK[96:112, :], 1.0,
                                    None, ALU.mult))
        V(lambda e: e.tensor_tensor(s.VSTK[32:48, :], s.VSTK[32:48, :],
                                    s.T3[32:48, :], ALU.subtract))
        s.v_rows = p.val("Vm")
        s.s_rows = p.val("Sm")
        s._qs_sp0 = p.val("dSp0")
        s._qs_sp1 = p.val("dSp1")

    def finish(self):
        p = self.P
        p.barrier_wait(SYN, "dT", p.val("dT"))
        p.barrier_wait(SYN, "dWs", p.val("dWs"))
        p.barrier_wait(SYN, "dUp0", p.val("dUp0"))
        p.barrier_wait(SYN, "dUp1", p.val("dUp1"))
        p.barrier_wait(GPS, "dG2", p.val("dG2"))
        p.barrier_wait(SYN, "dDn", p.val("dDn"))
        p.barrier_wait(SYN, "dX", p.val("dX"))
        p.barrier_wait(SYN, "dHs", p.val("dHs"))
        p.barrier_wait(SYN, "dHb", p.val("dHb"))
        p.barrier_wait(SYN, "dX10", p.val("dX10"))
        p.barrier_wait(SYN, "dX11", p.val("dX11"))
        p.barrier_wait(SYN, "dF0", p.val("dF0"))
        p.barrier_wait(SYN, "dF1", p.val("dF1"))
        p.barrier_wait(SYN, "dSp0", p.val("dSp0"))
        p.barrier_wait(SYN, "dSp1", p.val("dSp1"))
        p.barrier_wait(SYN, "dSl0", p.val("dSl0"))
        p.barrier_wait(SYN, "dSl1", p.val("dSl1"))
        p.barrier_wait(SYN, "dF0", p.val("dF0"))
        p.barrier_wait(SYN, "dF1", p.val("dF1"))
        p.barrier_wait(SYN, "dX10", p.val("dX10"))
        p.barrier_wait(SYN, "dX11", p.val("dX11"))
        p.barrier_wait(SYN, "dZ0", p.val("dZ0"))
        p.barrier_wait(SYN, "dZ1", p.val("dZ1"))
        p.barrier_wait(SYN, "dMc0", p.val("dMc0"))
        p.barrier_wait(SYN, "dMc1", p.val("dMc1"))
        p.barrier_wait(SYN, "dQ20", p.val("dQ20"))
        p.barrier_wait(SYN, "dQ21", p.val("dQ21"))
        p.barrier_wait(SYN, "dO10", p.val("dO10"))
        p.barrier_wait(SYN, "dO11", p.val("dO11"))
        p.barrier_wait(GPS, "dG", p.val("dG"))
        p.barrier_wait(GPS, "Gm", p.val("Gm"))
        p.barrier_wait(GPS, "dW", p.val("dW"))
        p.barrier_wait(TEN, "Tm", p.val("Tm"))
        p.barrier_wait(VEC, "Vm", p.val("Vm"))
        p.barrier_wait(SCA, "Sm", p.val("Sm"))

    def build(self):
        with self.nc.Block() as block:
            self.P.replay(block)
        self.st.close()
        return self.nc


    SC_SLAB = 0
    SC_WS = 6144
    SC_KBAR = 14336
    SC_MC = 14848
    SC_Q2 = 15360
    SC_Z = 15872
    SC_O1 = 16896

    def emit_scan(self):
        s, p = self, self.P
        p.op(VEC, lambda e: e.memset(s.SLOC[:, :], 0.0), inc="Vm")
        for h in range(NH):
            p.op(VEC, lambda e, h=h: e.tensor_scalar(
                s.MCH[:, h * HD:(h + 1) * HD], s.IDENTB[0:64, 0:64], 1.0,
                None, ALU.mult), waits=[("dT", s.c_consts)], inc="Vm")
        slab_free = [s._ab_tm, s._ab_tm]
        dslab = [0, 0]
        ws_free = [s.v_rows, s.v_rows]
        stg_free = 0
        gc_free = 0
        v_chain = p.val("Vm")
        t_chain = 0
        pe_free = s._ab_v
        pf_free = 0
        pg_free = 0
        pt_free = 0
        pbig_free = [p.val("Vm")] * 4
        spl = {"z": [0, 0], "mc": [0, 0], "q2": [0, 0], "o1": [0, 0]}
        for c in range(NCH):
            cs = slice(c * CH, (c + 1) * CH)
            par = c % 2
            slab = s.p16(s.SC_SLAB + par * 3072, 3072)
            for i_, src in enumerate((s.qhat_s, s.ksil_s, s.vsil_s)):
                p.dma(SYN, slab[:, i_ * 1024:(i_ + 1) * 1024]
                      .rearrange("p (f c) -> p f c", c=CH),
                      src[c].rearrange("(f p) c -> p f c", p=128),
                      waits=[("dSp0", s._qs_sp0), ("dSp1", s._qs_sp1),
                             ("Tm", slab_free[par])],
                      inc="dSl%d" % par)
            dslab[par] = p.val("dSl%d" % par)
            p.dma(SYN, s.STGF[0:1, 0:2048].rearrange("o (u c) -> o u c", c=CH),
                  s.VSTK[32:48, cs], waits=[("Vm", s.v_rows),
                                            ("Tm", stg_free)], inc="dT")
            p.dma(SYN, s.STATS[0:1, 0:2048].rearrange("o (u c) -> o u c", c=CH),
                  s.VSTK[64:80, cs], inc="dT")
            if c == 0:
                p.dma(SYN, s.STGF[64:65, 0:16], s.ZROW[:, 0:1], inc="dT")
            else:
                p.dma(SYN, s.STGF[64:65, 0:16],
                      s.VSTK[96:112, c * CH - 1: c * CH], inc="dT")
            p.dma(SYN, s.STGF[64:65, 16:32],
                  s.VSTK[96:112, c * CH + CH - 1: c * CH + CH], inc="dT")
            dstg = p.val("dT")
            CT = s.p32(0, 128)
            CT2 = s.p32(128, 128)
            p.op(TEN, lambda e, cs=cs: e.matmul(
                s.PF[0:128, 0:128], s.VSTK[:, cs], s.IDENT[:, :],
                is_transpose=True, start=True, stop=True),
                waits=[("Vm", s.v_rows), ("Vm", gc_free), ("Vm", pf_free)],
                inc="Tm")
            tm = p.val("Tm")
            p.op(VEC, lambda e, CT=CT: e.tensor_scalar(
                CT, s.PF[0:128, 0:128], 1.0, None, ALU.mult),
                waits=[("Tm", tm)], inc="Vm")
            p.op(TEN, lambda e, cs=cs: e.matmul(
                s.PF[0:128, 0:128], s.T1[:, cs], s.IDENT[:, :],
                is_transpose=True, start=True, stop=True),
                waits=[("Vm", p.val("Vm"))], inc="Tm")
            tm = p.val("Tm")
            p.op(VEC, lambda e, CT2=CT2: e.tensor_scalar(
                CT2, s.PF[0:128, 0:128], 1.0, None, ALU.mult),
                waits=[("Tm", tm)], inc="Vm")
            pf_free = p.val("Vm")
            vct = p.val("Vm")
            p.op(TEN, lambda e: e.matmul(
                s.PE[0:128, 0:16], s.ONES4[64:65, 0:128], s.STGF[64:65, 0:16],
                start=True, stop=True),
                waits=[("dT", dstg), ("Vm", pe_free)], inc="Tm")
            p.op(TEN, lambda e: e.matmul(
                s.PE[0:128, 16:32], s.ONES4[64:65, 0:128],
                s.STGF[64:65, 16:32], start=True, stop=True), inc="Tm")
            tm = p.val("Tm")
            GC = s.GCOLS
            p.op(VEC, lambda e, CT=CT: e.tensor_tensor(
                GC[:, 0:16], s.PE[0:128, 0:16], CT[:, 96:112], ALU.subtract),
                waits=[("Tm", tm), ("Vm", vct), ("Vm", gc_free)], inc="Vm")
            p.op(VEC, lambda e, CT=CT: e.tensor_tensor(
                GC[:, 16:32], CT[:, 64:80], s.PE[0:128, 0:16], ALU.add),
                inc="Vm")
            p.op(VEC, lambda e, CT=CT: e.tensor_tensor(
                GC[:, 32:48], CT[:, 0:16], s.PE[0:128, 16:32], ALU.subtract),
                inc="Vm")
            p.op(VEC, lambda e, CT=CT: e.tensor_tensor(
                GC[:, 48:64], CT[:, 32:48], s.PE[0:128, 0:16], ALU.add),
                inc="Vm")
            vv = p.val("Vm")
            p.op(SCA, lambda e, GC=GC: e.activation(GC[:, :], GC[:, :], AF.Exp),
                 waits=[("Vm", vv)], inc="Sm")
            sgc = p.val("Sm")
            p.op(VEC, lambda e: e.tensor_tensor(
                s.STGF[64:65, 32:48], s.STGF[64:65, 0:16],
                s.STGF[64:65, 16:32], ALU.subtract),
                waits=[("dT", dstg)], inc="Vm")
            vv = p.val("Vm")
            p.op(SCA, lambda e: e.activation(
                s.STGF[64:65, 32:48], s.STGF[64:65, 32:48], AF.Exp),
                waits=[("Vm", vv)], inc="Sm")
            sv = p.val("Sm")
            p.op(TEN, lambda e: e.matmul(
                s.PE[0:64, 32:48], s.ONES4[64:65, 0:64], s.STGF[64:65, 32:48],
                start=True, stop=True), waits=[("Sm", sv)], inc="Tm")
            tm = p.val("Tm")
            p.op(VEC, lambda e: e.tensor_scalar(
                s.GCREP[:, :], s.PE[0:64, 32:48], 1.0, None, ALU.mult),
                waits=[("Tm", tm)], inc="Vm")
            pe_free = p.val("Vm")
            v_cols = p.val("Vm")
            for hq in range(4):
                wpar = (c * 4 + hq) % 2
                ws = s.p16(s.SC_WS + wpar * 4096, 4096)
                W1A = ws[:, 0:1024]
                PB_ = ws[:, 1024:2048]
                QB_ = ws[:, 2048:3072]
                RB_ = ws[:, 3072:3584]
                heads = [4 * hq + i for i in range(4)]
                stage_lim = getattr(self, "scan_stage", 99)
                if stage_lim == 0:
                    continue

                def hslabs(h):
                    base = 64 * (h % 2)
                    fc = h // 2
                    qsl = slab[base:base + 64, fc * 128: fc * 128 + 128]
                    ksl = slab[base:base + 64, 1024 + fc * 128:
                               1024 + fc * 128 + 128]
                    vsl = slab[base:base + 64, 2048 + fc * 128:
                               2048 + fc * 128 + 128]
                    return qsl, ksl, vsl

                for i, h in enumerate(heads):
                    qsl, ksl, _ = hslabs(h)
                    qq = s.PBIG[:, i * 512:(i + 1) * 512]
                    p.op(TEN, lambda e, qq=qq, ksl=ksl: e.matmul(
                        qq[:, 0:128], ksl, ksl, start=True, stop=True),
                        waits=[("dSl%d" % par, dslab[par]),
                               ("Vm", pbig_free[i])])
                    if stage_lim == 10:
                        p.op(TEN, lambda e, qq=qq, ksl=ksl, qsl=qsl: e.matmul(
                            qq[:, 128:256], ksl, qsl, start=True, stop=True),
                            inc="Tm")
                        continue
                    p.op(TEN, lambda e, qq=qq, ksl=ksl, qsl=qsl: e.matmul(
                        qq[:, 128:256], ksl, qsl, start=True, stop=True))
                    p.op(TEN, lambda e, qq=qq, h=h: e.matmul(
                        qq[:, 256:384], s.ONES4[0:1, 0:128],
                        s.STGF[0:1, h * 128: h * 128 + 128],
                        start=True, stop=True), waits=[("dT", dstg)])
                    p.op(TEN, lambda e, qq=qq, h=h: e.matmul(
                        qq[:, 384:512], s.ONES4[0:1, 0:128],
                        s.STATS[0:1, h * 128: h * 128 + 128],
                        start=True, stop=True), inc="Tm")
                stg_free = p.val("Tm")
                tgram = p.val("Tm")
                if stage_lim in (10, 11):
                    continue
                EArg = s.p32(256, 1024)
                for i, h in enumerate(heads):
                    p.op(VEC, lambda e, i=i, h=h, CT=CT: e.tensor_scalar(
                        EArg[:, i * 256:(i + 1) * 256],
                        s.PBIG[:, i * 512 + 256: i * 512 + 512],
                        CT[:, h:h + 1], None, ALU.add),
                        waits=[("Tm", tgram), ("Sm", sgc)], inc="Vm")
                p.op(VEC, lambda e, EArg=EArg: e.tensor_tensor(
                    EArg.rearrange("p (i c) -> p i c", c=256),
                    EArg.rearrange("p (i c) -> p i c", c=256),
                    s.TRI2.rearrange("p (o c) -> p o c", o=1)
                    .broadcast_to([128, 4, 256]), ALU.add), inc="Vm")
                vv = p.val("Vm")
                p.op(SCA, lambda e, EArg=EArg: e.activation(
                    EArg, EArg, AF.Exp), waits=[("Vm", vv)], inc="Sm")
                se = p.val("Sm")
                if stage_lim == 12:
                    continue
                p.op(VEC, lambda e, W1A=W1A, EArg=EArg: e.tensor_tensor(
                    W1A.rearrange("p (i c) -> p i c", c=256),
                    s.PBIG.rearrange("p (i c) -> p i c", c=512)[:, :, 0:256],
                    EArg.rearrange("p (i c) -> p i c", c=256), ALU.mult),
                    waits=[("Sm", se), ("Vm", ws_free[wpar])], inc="Vm")
                vW1A = p.val("Vm")
                pbig_free = [p.val("Vm")] * 4
                if stage_lim == 1:
                    continue
                for f2 in range(2):
                    fc = heads[2 * f2] // 2
                    vfull = slab[0:128, 2048 + fc * 128: 2048 + fc * 128 + 128]
                    kfull = slab[0:128, 1024 + fc * 128: 1024 + fc * 128 + 128]
                    p.op(TEN, lambda e, f2=f2, vfull=vfull: e.matmul(
                        s.PT[:, f2 * 256: f2 * 256 + 128], vfull,
                        s.IDENTB[:, :], is_transpose=True, start=True,
                        stop=True),
                        waits=([("Vm", pt_free)] if f2 == 0 else ()))
                    p.op(TEN, lambda e, f2=f2, kfull=kfull: e.matmul(
                        s.PT[:, f2 * 256 + 128: f2 * 256 + 256], kfull,
                        s.IDENTB[:, :], is_transpose=True, start=True,
                        stop=True),
                        inc=("Tm" if f2 == 1 else None))
                tpt = p.val("Tm")
                kbar = s.p16(s.SC_KBAR + wpar * 256, 256)
                for i, h in enumerate(heads):
                    vcol = (i // 2) * 256 + (h % 2) * 64
                    kcol = (i // 2) * 256 + 128 + (h % 2) * 64
                    p.op(VEC, lambda e, i=i, h=h, RB_=RB_, CT2=CT2, vcol=vcol:
                         e.tensor_scalar(
                             RB_[:, i * 128: i * 128 + 64],
                             s.PT[:, vcol: vcol + 64],
                             CT2[:, 32 + h: 33 + h], None, ALU.mult),
                         waits=[("Tm", tpt)] if i == 0 else (), inc="Vm")
                    p.op(VEC, lambda e, i=i, h=h, RB_=RB_, kcol=kcol:
                         e.tensor_scalar(
                             RB_[:, i * 128 + 64: i * 128 + 128],
                             s.PT[:, kcol: kcol + 64],
                             s.GCOLS[:, 48 + h: 49 + h], -1.0, ALU.mult,
                             ALU.mult),
                         inc="Vm")
                    p.op(VEC, lambda e, i=i, h=h, kbar=kbar, kcol=kcol:
                         e.tensor_scalar(
                             kbar[:, i * 64:(i + 1) * 64],
                             s.PT[:, kcol: kcol + 64],
                             s.GCOLS[:, 32 + h: 33 + h], None, ALU.mult),
                         inc="Vm")
                pt_free = p.val("Vm")
                vR0 = p.val("Vm")
                for i in range(4):
                    p.op(TEN, lambda e, i=i, W1A=W1A: e.matmul(
                        s.PT[:, i * 128:(i + 1) * 128],
                        W1A[:, i * 256: i * 256 + 128], s.IDENTB[:, :],
                        is_transpose=True, start=True, stop=True),
                        waits=([("Vm", vW1A), ("Vm", pt_free)] if i == 0
                               else ()),
                        inc=("Tm" if i == 3 else None))
                tq0 = p.val("Tm")
                p.op(VEC, lambda e, QB_=QB_: e.tensor_scalar(
                    QB_[:, 0:512], s.PT[:, 0:1024:2][:, 0:512] if False else
                    s.PT[:, 0:512], 1.0, None, ALU.mult),
                    waits=[("Tm", tq0)], inc="Vm")
                pt_free = p.val("Vm")
                vq0 = p.val("Vm")
                if stage_lim == 2:
                    continue
                vr = vR0
                vpq = vq0
                sq_wait = 0
                pg_free_sm = 0
                for j in range(7):
                    for i in range(4):
                        lsl = (W1A[:, i * 256: i * 256 + 128] if j == 0 else
                               PB_[:, (j % 2) * 512 + i * 128:
                                   (j % 2) * 512 + i * 128 + 128])
                        p.op(TEN, lambda e, i=i, lsl=lsl, RB_=RB_: e.matmul(
                            s.PE[:, i * 128:(i + 1) * 128], lsl,
                            RB_[:, i * 128:(i + 1) * 128],
                            start=True, stop=True),
                            waits=([("Vm", vr), ("Vm", vpq),
                                    ("Vm", pe_free)] if i == 0 else ()),
                            inc=("Tm" if i == 3 else None))
                    ta = p.val("Tm")
                    op_ = ALU.subtract if j == 0 else ALU.add
                    p.op(VEC, lambda e, RB_=RB_, op_=op_: e.tensor_tensor(
                        RB_, RB_, s.PE[:, 0:512], op_),
                        waits=[("Tm", ta)], inc="Vm")
                    vr = p.val("Vm")
                    pe_free = vr
                    if j < 6:
                        for i in range(4):
                            psl = (W1A[:, i * 256: i * 256 + 128] if j == 0
                                   else PB_[:, (j % 2) * 512 + i * 128:
                                            (j % 2) * 512 + i * 128 + 128])
                            qsl2 = QB_[:, (j % 2) * 512 + i * 128:
                                       (j % 2) * 512 + i * 128 + 128]
                            p.op(TEN, lambda e, i=i, psl=psl, qsl2=qsl2:
                                 e.matmul(s.PF[:, i * 128:(i + 1) * 128],
                                          qsl2, psl, start=True, stop=True),
                                 waits=([("Vm", pf_free), ("Sm", sq_wait)]
                                        if i == 0 else ()),
                                 inc=("Tm" if (i == 3 and j == 5) else None))
                            if j < 5:
                                p.op(TEN, lambda e, i=i, psl=psl, qsl2=qsl2:
                                     e.matmul(s.PG[:, i * 128:(i + 1) * 128],
                                              psl, qsl2, start=True,
                                              stop=True),
                                     waits=([("Vm", pg_free)] if i == 0
                                            else ()),
                                     inc=("Tm" if i == 3 else None))
                        tsq = p.val("Tm")
                        np_ = PB_[:, ((j + 1) % 2) * 512:
                                  ((j + 1) % 2) * 512 + 512]
                        p.op(VEC, lambda e, np_=np_: e.tensor_scalar(
                            np_, s.PF[:, 0:512], 1.0, None, ALU.mult),
                            waits=[("Tm", tsq)], inc="Vm")
                        pf_free = p.val("Vm")
                        vpq = p.val("Vm")
                        if j < 5:
                            nq_ = QB_[:, ((j + 1) % 2) * 512:
                                      ((j + 1) % 2) * 512 + 512]
                            p.op(SCA, lambda e, nq_=nq_: e.activation(
                                nq_, s.PG[:, 0:512], AF.Identity),
                                waits=[("Tm", tsq)], inc="Sm")
                            sq_wait = p.val("Sm")
                            pg_free_sm = sq_wait
                if stage_lim == 3:
                    continue
                for f2 in range(2):
                    fc = heads[2 * f2] // 2
                    qfull = slab[0:128, fc * 128: fc * 128 + 128]
                    p.op(TEN, lambda e, f2=f2, qfull=qfull: e.matmul(
                        s.PT[:, f2 * 128:(f2 + 1) * 128], qfull,
                        s.IDENTB[:, :], is_transpose=True, start=True,
                        stop=True),
                        waits=([("Vm", pt_free)] if f2 == 0 else ()),
                        inc=("Tm" if f2 == 1 else None))
                tqz = p.val("Tm")
                GQ = s.p32(1280, 256)
                for i, h in enumerate(heads):
                    qcol = (i // 2) * 128 + (h % 2) * 64
                    p.op(VEC, lambda e, i=i, h=h, GQ=GQ, qcol=qcol:
                         e.tensor_scalar(
                             GQ[:, i * 64:(i + 1) * 64],
                             s.PT[:, qcol: qcol + 64],
                             s.GCOLS[:, 16 + h: 17 + h], None, ALU.mult),
                         waits=([("Tm", tqz)] if i == 0 else ()), inc="Vm")
                pt_free = p.val("Vm")
                vgq = p.val("Vm")
                for i in range(4):
                    p.op(TEN, lambda e, i=i, GQ=GQ: e.matmul(
                        s.PG[0:64, i * 128:(i + 1) * 128], GQ[:, i * 64:(i + 1) * 64],
                        s.IDENT[:, :], is_transpose=True, start=True,
                        stop=False), waits=([("Vm", vgq), ("Vm", pg_free),
                                             ("Sm", pg_free_sm)]
                                            if i == 0 else ()))
                    p.op(TEN, lambda e, i=i, RB_=RB_, W1A=W1A: e.matmul(
                        s.PG[0:64, i * 128:(i + 1) * 128],
                        RB_[:, i * 128 + 64: i * 128 + 128],
                        W1A[:, i * 256 + 128: i * 256 + 256],
                        start=False, stop=True),
                        inc=("Tm" if i == 3 else None))
                tz = p.val("Tm")
                zb = s.p16(s.SC_Z + wpar * 512, 512)
                p.op(VEC, lambda e, zb=zb: e.tensor_scalar(
                    zb[0:64, :], s.PG[0:64, 0:512], 1.0, None, ALU.mult),
                    waits=[("Tm", tz), ("dZ%d" % wpar, spl["z"][wpar])],
                    inc="Vm")
                vz = p.val("Vm")
                pg_free = vz
                p.dma(SYN, s.z_s[c, 4 * hq:4 * hq + 4]
                      .rearrange("i p c -> p i c"),
                      zb[0:64, :].rearrange("p (i c) -> p i c", c=CH),
                      waits=[("Vm", vz)], inc="dZ%d" % wpar)
                spl["z"][wpar] = p.val("dZ%d" % wpar)
                if stage_lim == 4:
                    continue
                for i in range(4):
                    p.op(TEN, lambda e, i=i, RB_=RB_, kbar=kbar: e.matmul(
                        s.PE[0:64, i * 64 + 0:(i + 1) * 64],
                        RB_[:, i * 128 + 64: i * 128 + 128],
                        kbar[:, i * 64:(i + 1) * 64],
                        start=True, stop=True),
                        waits=([("Vm", pe_free)] if i == 0 else ()))
                    p.op(TEN, lambda e, i=i, RB_=RB_, kbar=kbar: e.matmul(
                        s.PE[0:64, 256 + i * 64: 256 + (i + 1) * 64],
                        kbar[:, i * 64:(i + 1) * 64],
                        RB_[:, i * 128: i * 128 + 64],
                        start=True, stop=True),
                        inc=("Tm" if i == 3 else None))
                tmc = p.val("Tm")
                mcb = s.p16(s.SC_MC + wpar * 256, 256)
                q2b = s.p16(s.SC_Q2 + wpar * 256, 256)
                for i, h in enumerate(heads):
                    p.op(VEC, lambda e, i=i, h=h, mcb=mcb:
                         e.scalar_tensor_tensor(
                             mcb[0:64, i * 64:(i + 1) * 64],
                             s.IDENTB[0:64, 0:64], s.GCREP[:, h:h + 1],
                             s.PE[0:64, i * 64:(i + 1) * 64],
                             ALU.mult, ALU.add),
                         waits=([("Tm", tmc), ("Vm", v_cols),
                                 ("dMc%d" % wpar, spl["mc"][wpar])]
                                if i == 0 else ()),
                         inc="Vm")
                p.op(VEC, lambda e, q2b=q2b: e.tensor_scalar(
                    q2b[0:64, :], s.PE[0:64, 256:512], 1.0, None, ALU.mult),
                    waits=[("dQ2%d" % wpar, spl["q2"][wpar])], inc="Vm")
                vmc = p.val("Vm")
                pe_free = vmc
                p.dma(SYN, s.mc_s[c, 4 * hq:4 * hq + 4]
                      .rearrange("i p c -> p i c"),
                      mcb[0:64, :].rearrange("p (i c) -> p i c", c=HD),
                      waits=[("Vm", vmc)], inc="dMc%d" % wpar)
                spl["mc"][wpar] = p.val("dMc%d" % wpar)
                p.dma(SYN, s.q2_s[c, 4 * hq:4 * hq + 4]
                      .rearrange("i p c -> p i c"),
                      q2b[0:64, :].rearrange("p (i c) -> p i c", c=HD),
                      waits=[("Vm", vmc)], inc="dQ2%d" % wpar)
                spl["q2"][wpar] = p.val("dQ2%d" % wpar)
                if stage_lim == 5:
                    continue
                for i in range(4):
                    p.op(TEN, lambda e, i=i, W1A=W1A, RB_=RB_: e.matmul(
                        s.PF[:, i * 64: i * 64 + 64],
                        W1A[:, i * 256 + 128: i * 256 + 256],
                        RB_[:, i * 128: i * 128 + 64],
                        start=True, stop=True),
                        waits=([("Vm", pf_free)] if i == 0 else ()),
                        inc=("Tm" if i == 3 else None))
                to1 = p.val("Tm")
                o1b = s.p16(s.SC_O1 + wpar * 1024, 1024)
                p.op(VEC, lambda e, o1b=o1b: e.tensor_scalar(
                    o1b[:, 0:256], s.PF[:, 0:256], 1.0, None, ALU.mult),
                    waits=[("Tm", to1), ("dO1%d" % wpar, spl["o1"][wpar])],
                    inc="Vm")
                vo1 = p.val("Vm")
                pf_free = vo1
                p.dma(SYN, s.o1_s[c, 4 * hq:4 * hq + 4]
                      .rearrange("i p c -> p i c"),
                      o1b[:, 0:256].rearrange("p (i c) -> p i c", c=HD),
                      waits=[("Vm", vo1)], inc="dO1%d" % wpar)
                spl["o1"][wpar] = p.val("dO1%d" % wpar)
                for i, h in enumerate(heads):
                    p.op(TEN, lambda e, i=i, h=h, mcb=mcb: e.matmul(
                        s.PG[0:64, i * 64:(i + 1) * 64],
                        mcb[0:64, i * 64:(i + 1) * 64],
                        s.SLOC[:, h * HD:(h + 1) * HD],
                        start=True, stop=True),
                        waits=([("Vm", vmc), ("Vm", v_chain),
                                ("Vm", pg_free)] if i == 0 else ()))
                    p.op(TEN, lambda e, i=i, h=h, mcb=mcb: e.matmul(
                        s.PG[0:64, 256 + i * 64: 256 + (i + 1) * 64],
                        mcb[0:64, i * 64:(i + 1) * 64],
                        s.MCH[:, h * HD:(h + 1) * HD],
                        start=True, stop=True),
                        inc=("Tm" if i == 3 else None))
                tch = p.val("Tm")
                p.op(VEC, lambda e, q2b=q2b, hq=hq: e.tensor_tensor(
                    s.SLOC[:, 4 * hq * HD: (4 * hq + 4) * HD],
                    q2b[0:64, :], s.PG[0:64, 0:256], ALU.add),
                    waits=[("Tm", tch)], inc="Vm")
                p.op(VEC, lambda e, hq=hq: e.tensor_scalar(
                    s.MCH[:, 4 * hq * HD: (4 * hq + 4) * HD],
                    s.PG[0:64, 256:512], 1.0, None, ALU.mult), inc="Vm")
                v_chain = p.val("Vm")
                pg_free = v_chain
                ws_free[wpar] = p.val("Vm")
            slab_free[par] = p.val("Tm")
            gc_free = p.val("Vm")
        s.t_scan = p.val("Tm")
        s.v_scan = p.val("Vm")
        s.spl = spl

    P2_OFM = 0
    P2_MC = 16384
    P2_Q2 = 18432
    P2_Z = 20480
    P2_O1 = 24576
    P2_FG = 26624
    P2_MT = 30720
    EN_WO = 16384
    EN_OG = 33280
    EN_XWO = 34304
    EN_X1 = 38400

    def emit_fold(self):
        s, p = self, self.P
        p.dma(SYN, s.ag2_in.rearrange("p (h two c) -> p h two c", two=2, c=HD)
              [:, :, 0, :], s.MCH[:, :].rearrange("p (h c) -> p h c", c=HD),
              waits=[("Vm", s.v_scan)], inc="dT")
        p.dma(SYN, s.ag2_in.rearrange("p (h two c) -> p h two c", two=2, c=HD)
              [:, :, 1, :], s.SLOC[:, :].rearrange("p (h c) -> p h c", c=HD),
              inc="dT")
        d2 = p.val("dT")
        p.op(GPS, lambda e: e.collective_compute(
            "AllGather", ALU.bypass, replica_groups=[list(range(self.ncores))],
            ins=[s.ag2_in.opt()], outs=[s.ag2_out.opt()]),
            waits=[("dT", d2), ("cc", s._cc_done)], inc="cc")
        s._cc_done = p.val("cc")
        p.op(VEC, lambda e: e.memset(s.SS[:, :], 0.0), inc="Vm")
        fg_d = [0, 0]
        fg_free = [0, 0]
        fg_vfree = [0, 0]
        v_ss = p.val("Vm")
        pe_free = 0
        mt_tmfree = 0
        for j in range(min(7, self.ncores - 1) if self.ncores > 1 else 0):
            par = j % 2
            fg = s.p16(s.P2_FG + par * 2048, 2048)
            p.dma(SYN, fg[0:64, :].rearrange("p (h c) -> p h c", c=2 * HD),
                  s.ag2_out[j * HD:(j + 1) * HD, :]
                  .rearrange("p (h c) -> p h c", c=2 * HD),
                  waits=[("cc", s._cc_done), ("Tm", fg_free[par]),
                         ("Vm", fg_vfree[par])],
                  inc="dF%d" % par)
            fg_d[par] = p.val("dF%d" % par)
            mt = s.p16(s.P2_MT, 1024)
            p.op(VEC, lambda e, fg=fg, j=j: e.scalar_tensor_tensor(
                mt[0:64, :].rearrange("p (h c) -> p h c", c=HD),
                fg[0:64, :].rearrange("p (h c) -> p h c", c=2 * HD)[:, :, 0:HD],
                s.FOLDA[:, j:j + 1],
                s.FOLDB.rearrange("p (j c) -> p j c", c=HD)
                [:, j:j + 1, :].broadcast_to([64, NH, HD]),
                ALU.mult, ALU.add),
                waits=[("dF%d" % par, fg_d[par]), ("Tm", mt_tmfree)],
                inc="Vm")
            vmt = p.val("Vm")
            for h in range(NH):
                p.op(TEN, lambda e, h=h, mt=mt: e.matmul(
                    s.PT[0:64, h * 64:(h + 1) * 64],
                    mt[0:64, h * 64:(h + 1) * 64], s.IDENTB[0:64, 0:64],
                    is_transpose=True, start=True, stop=True),
                    waits=([("Vm", vmt)] if h == 0 else ()),
                    inc=("Tm" if h == NH - 1 else None))
            tmt = p.val("Tm")
            mtf = s.p16(s.P2_MT, 1024)
            p.op(VEC, lambda e, mtf=mtf: e.tensor_scalar(
                mtf[0:64, :], s.PT[0:64, 0:1024], 1.0, None, ALU.mult),
                waits=[("Tm", tmt)], inc="Vm")
            vmtf = p.val("Vm")
            for h in range(NH):
                p.op(TEN, lambda e, h=h, mtf=mtf: e.matmul(
                    s.PE[0:64, h * 64:(h + 1) * 64] if h < 8 else
                    s.PF[0:64, (h - 8) * 64:(h - 7) * 64],
                    mtf[0:64, h * 64:(h + 1) * 64],
                    s.SS[:, h * HD:(h + 1) * HD], start=True, stop=True),
                    waits=([("Vm", vmtf), ("Vm", v_ss), ("Vm", pe_free)]
                           if h == 0 else ()),
                    inc=("Tm" if h == NH - 1 else None))
            tss = p.val("Tm")
            p.op(VEC, lambda e, fg=fg, j=j: e.scalar_tensor_tensor(
                s.SS[:, 0:8 * HD].rearrange("p (h c) -> p h c", c=HD),
                fg[0:64, :].rearrange("p (h c) -> p h c", c=2 * HD)
                [:, 0:8, HD:2 * HD],
                s.FOLDA[:, j:j + 1],
                s.PE[0:64, 0:512].rearrange("p (h c) -> p h c", c=HD),
                ALU.mult, ALU.add), waits=[("Tm", tss)], inc="Vm")
            p.op(VEC, lambda e, fg=fg, j=j: e.scalar_tensor_tensor(
                s.SS[:, 8 * HD:].rearrange("p (h c) -> p h c", c=HD),
                fg[0:64, :].rearrange("p (h c) -> p h c", c=2 * HD)
                [:, 8:16, HD:2 * HD],
                s.FOLDA[:, j:j + 1],
                s.PF[0:64, 0:512].rearrange("p (h c) -> p h c", c=HD),
                ALU.mult, ALU.add), inc="Vm")
            v_ss = p.val("Vm")
            pe_free = v_ss
            mt_tmfree = tss
            fg_free[par] = p.val("Tm")
            fg_vfree[par] = p.val("Vm")
        s.v_ss = p.val("Vm")

    def emit_pass2(self):
        s, p = self, self.P
        p.op(VEC, lambda e: e.tensor_scalar(
            s.STRUE[:, :], s.SS[:, :], 1.0, None, ALU.mult),
            waits=[("Vm", s.v_ss)], inc="Vm")
        v_st = p.val("Vm")
        rings = {}
        dring = {"mc": [0, 0], "q2": [0, 0], "z": [0, 0], "o1": [0, 0]}
        rfree = {"mc": [0, 0], "q2": [0, 0], "z": [0, 0], "o1": [0, 0]}
        rvfree = {"mc": [0, 0], "q2": [0, 0], "z": [0, 0], "o1": [0, 0]}
        ofm = s.p16(s.P2_OFM, 16384)
        pe_free = 0; pf_free = 0; pt_free = 0
        v_ofm = 0
        for c in range(NCH):
            par = c % 2
            mcr = s.p16(s.P2_MC + par * 1024, 1024)
            q2r = s.p16(s.P2_Q2 + par * 1024, 1024)
            zr = s.p16(s.P2_Z + par * 2048, 2048)
            o1r = s.p16(s.P2_O1 + par * 1024, 1024)
            for nm, dst, src, w in (("mc", mcr, s.mc_s, HD),
                                    ("q2", q2r, s.q2_s, HD),
                                    ("z", zr, s.z_s, CH),
                                    ("o1", o1r, s.o1_s, HD)):
                p.dma(SYN, dst[0:128 if nm == "o1" else 64, :]
                      .rearrange("p (h c) -> p h c", c=w),
                      src[c].rearrange("h p c -> p h c"),
                      waits=[("Tm", rfree[nm][par]), ("Vm", rvfree[nm][par]),
                             ("dZ0", s.spl["z"][0]), ("dZ1", s.spl["z"][1]),
                             ("dMc0", s.spl["mc"][0]), ("dMc1", s.spl["mc"][1]),
                             ("dQ20", s.spl["q2"][0]), ("dQ21", s.spl["q2"][1]),
                             ("dO10", s.spl["o1"][0]), ("dO11", s.spl["o1"][1])],
                      inc="dSl%d" % par)
            dr = p.val("dSl%d" % par)
            for h in range(NH):
                bank = s.PBIG[:, (h % 4) * 512:(h % 4) * 512 + 512]
                p.op(TEN, lambda e, h=h, zr=zr, bank=bank: e.matmul(
                    bank[:, 0:64], zr[0:64, h * CH:(h + 1) * CH],
                    s.STRUE[:, h * HD:(h + 1) * HD], start=True, stop=True),
                    waits=[("dSl%d" % par, dr), ("Vm", v_st),
                           ("Vm", s._p2big_free[h % 4])], inc="Tm")
                t_o = p.val("Tm")
                otm = s.p32(1536, 64)
                p.op(VEC, lambda e, h=h, o1r=o1r, bank=bank, otm=otm:
                     e.tensor_tensor(otm, bank[:, 0:64],
                                     o1r[:, h * HD:(h + 1) * HD], ALU.add),
                     waits=[("Tm", t_o), ("Tm", s._otm_free)], inc="Vm")
                v_otm = p.val("Vm")
                p.op(TEN, lambda e, otm=otm: e.matmul(
                    s.PF[0:64, 0:128], otm, s.IDENT[:, :],
                    is_transpose=True, start=True, stop=True),
                    waits=[("Vm", v_otm), ("Vm", pf_free)], inc="Tm")
                s._otm_free = p.val("Tm")
                t_t = p.val("Tm")
                fc = h // 2
                rb = 64 * (h % 2)
                p.op(VEC, lambda e, fc=fc, rb=rb, c=c, ofm=ofm: e.tensor_scalar(
                    ofm[rb:rb + 64, fc * TC + c * CH: fc * TC + c * CH + CH],
                    s.PF[0:64, 0:128], 1.0, None, ALU.mult),
                    waits=[("Tm", t_t)], inc="Vm")
                pf_free = p.val("Vm")
                v_ofm = p.val("Vm")
                p.op(TEN, lambda e, h=h, mcr=mcr: e.matmul(
                    bank[:, 64:128][0:64, :] if False else
                    s.PE[0:64, (h % 8) * 64:(h % 8) * 64 + 64],
                    mcr[0:64, h * HD:(h + 1) * HD],
                    s.STRUE[:, h * HD:(h + 1) * HD], start=True, stop=True),
                    waits=[("Vm", pe_free)], inc="Tm")
                t_s = p.val("Tm")
                p.op(VEC, lambda e, h=h, q2r=q2r: e.tensor_tensor(
                    s.STRUE[:, h * HD:(h + 1) * HD],
                    s.PE[0:64, (h % 8) * 64:(h % 8) * 64 + 64],
                    q2r[0:64, h * HD:(h + 1) * HD], ALU.add),
                    waits=[("Tm", t_s)], inc="Vm")
                v_st = p.val("Vm")
                pe_free = v_st
                s._p2big_free[h % 4] = p.val("Vm")
            for nm in rfree:
                rfree[nm][par] = p.val("Tm")
                rvfree[nm][par] = p.val("Vm")
        s.v_ofm = p.val("Vm")
        scr = s.p16(s.P2_MC, 2048)
        for fc in range(NFC):
            p.op(SCA, lambda e, fc=fc, ofm=ofm, scr=scr: e.activation(
                scr, ofm[:, fc * TC:(fc + 1) * TC], AF.Identity,
                accum_out=s.GSTATS[:, fc:fc + 1]),
                waits=[("Vm", s.v_ofm)], inc="Sm")
            p.op(SCA, lambda e, fc=fc, ofm=ofm, scr=scr: e.activation(
                scr, ofm[:, fc * TC:(fc + 1) * TC], AF.Square,
                accum_out=s.GSTATS[:, 8 + fc:9 + fc]), inc="Sm")
        sg = p.val("Sm")
        for fc in range(NFC):
            p.op(TEN, lambda e, fc=fc: e.matmul(
                s.PE[0:16, 0:1], s.SELSUM32[:, fc * 16:(fc + 1) * 16],
                s.GSTATS[:, fc:fc + 1], start=(fc == 0), stop=(fc == NFC - 1)),
                waits=([("Sm", sg), ("Vm", pe_free)] if fc == 0 else ()))
            p.op(TEN, lambda e, fc=fc: e.matmul(
                s.PF[0:16, 0:1], s.SELSUM32[:, fc * 16:(fc + 1) * 16],
                s.GSTATS[:, 8 + fc:9 + fc], start=(fc == 0),
                stop=(fc == NFC - 1)),
                waits=([("Vm", pf_free)] if fc == 0 else ()),
                inc=("Tm" if fc == NFC - 1 else None))
        tg = p.val("Tm")
        p.op(VEC, lambda e: e.tensor_scalar(
            s.MRS[:, 0:1], s.PE[0:16, 0:1], 1.0, None, ALU.mult),
            waits=[("Tm", tg)], inc="Vm")
        p.op(VEC, lambda e: e.tensor_scalar(
            s.MRS[:, 1:2], s.PF[0:16, 0:1], 1.0, None, ALU.mult), inc="Vm")
        vg = p.val("Vm")
        p.dma(SYN, s.ag3_in, s.MRS[:, 0:2], waits=[("Vm", vg)], inc="dT")
        d3 = p.val("dT")
        p.op(GPS, lambda e: e.collective_compute(
            "AllGather", ALU.bypass, replica_groups=[list(range(self.ncores))],
            ins=[s.ag3_in.opt()], outs=[s.ag3_out.opt()]),
            waits=[("dT", d3), ("cc", s._cc_done)], inc="cc")
        s._cc_done = p.val("cc")
        s._p2wo_free = p.val("Tm")
        s._p2wo_sfree = p.val("Sm")

    def emit_attn_end(self):
        s, p = self, self.P
        nr = self.ncores
        gsb = s.p32(1600, 2)
        p.dma(GPS, gsb[0:nr * 16, :], s.ag3_out,
              waits=[("cc", s._cc_done)], inc="dG2")
        dg = p.val("dG2")
        p.op(TEN, lambda e: e.matmul(
            s.PE[0:16, 0:2], s.FOLD8[0:16 * nr, 0:16], gsb[0:16 * nr, :],
            start=True, stop=True), waits=[("dG2", dg)], inc="Tm")
        tg = p.val("Tm")
        N_ = float(TC * self.ncores * HD)
        V = lambda fn, w=(): p.op(VEC, fn, waits=w, inc="Vm")
        S_ = lambda fn, w=(): p.op(SCA, fn, waits=w, inc="Sm")
        V(lambda e: e.tensor_scalar(s.MRS[:, 2:4], s.PE[0:16, 0:2], 1.0 / N_,
                                    None, ALU.mult), w=[("Tm", tg)])
        V(lambda e: e.tensor_tensor(s.MRS[:, 4:5], s.MRS[:, 2:3],
                                    s.MRS[:, 2:3], ALU.mult))
        V(lambda e: e.tensor_tensor(s.MRS[:, 3:4], s.MRS[:, 3:4],
                                    s.MRS[:, 4:5], ALU.subtract))
        vv = p.val("Vm")
        S_(lambda e: e.activation(s.MRS[:, 3:4], s.MRS[:, 3:4], AF.Sqrt,
                                  bias=s.EPSC[0:16, 1:2]), w=[("Vm", vv)])
        sv = p.val("Sm")
        V(lambda e: e.reciprocal(s.MRS[:, 3:4], s.MRS[:, 3:4]), w=[("Sm", sv)])
        V(lambda e: e.tensor_tensor(s.MRS[:, 4:5], s.MRS[:, 2:3],
                                    s.MRS[:, 3:4], ALU.mult))
        vmrs = p.val("Vm")
        for fc in range(NFC):
            p.op(TEN, lambda e, fc=fc: e.matmul(
                s.PE[0:128, 4 + 2 * fc: 6 + 2 * fc],
                s.EXPAND32[:, fc * 128:(fc + 1) * 128], s.MRS[:, 3:5],
                start=True, stop=True), waits=[("Vm", vmrs)],
                inc=("Tm" if fc == NFC - 1 else None))
        te = p.val("Tm")
        for fc in range(NFC):
            V(lambda e, fc=fc: e.tensor_tensor(
                s.COEF[:, fc:fc + 1], s.GNS[:, fc:fc + 1],
                s.PE[0:128, 4 + 2 * fc: 5 + 2 * fc], ALU.mult),
              w=([("Tm", te), ("dT", s.c_consts)] if fc == 0 else ()))
            V(lambda e, fc=fc: e.tensor_tensor(
                s.COEF[:, 8 + fc: 9 + fc], s.GNS[:, fc:fc + 1],
                s.PE[0:128, 5 + 2 * fc: 6 + 2 * fc], ALU.mult))
            V(lambda e, fc=fc: e.tensor_tensor(
                s.COEF[:, 8 + fc: 9 + fc], s.GNB[:, fc:fc + 1],
                s.COEF[:, 8 + fc: 9 + fc], ALU.subtract))
        v_coef = p.val("Vm")
        wslots = [s.p16(s.EN_WO, 8192), s.p16(s.EN_WO + 8192, 8192)]
        p.dma(SYN, wslots[0].rearrange("p (kc c) -> p kc c", c=DM),
              s.wob.rearrange("(kc p) c -> p kc c", p=128),
              waits=[("dW", s.wcast_at["wo"]), ("Tm", s._p2wo_free),
                     ("Sm", s._p2wo_sfree), ("dUp0", p.val("dUp0")),
                     ("dUp1", p.val("dUp1"))],
              inc="dUp0")
        dwo = p.val("dUp0")
        ofm = s.p16(s.P2_OFM, 16384)
        xwo = s.p16(s.EN_XWO, 4096)
        dog = [0, 0]
        og_free = [0, 0]
        dx1 = [0, 0]
        x1_free = [0, 0]
        for j in range(NTT):
            js = slice(j * 512, (j + 1) * 512)
            for fc in range(NFC):
                par = fc % 2
                ogr = s.p16(s.EN_OG + par * 512, 512)
                p.dma(SYN, ogr, s.og_s[fc * 128:(fc + 1) * 128, js],
                      waits=[("dSp0", s._qs_sp0), ("dSp1", s._qs_sp1),
                             ("Vm", og_free[par])], inc="dO1%d" % par)
                dog[par] = p.val("dO1%d" % par)
                p.op(VEC, lambda e, fc=fc, j=j, ofm=ofm, xwo=xwo: e.tensor_scalar(
                    xwo[:, fc * 512:(fc + 1) * 512],
                    ofm[:, fc * TC + j * 512: fc * TC + j * 512 + 512],
                    s.COEF[:, fc:fc + 1], s.COEF[:, 8 + fc: 9 + fc],
                    ALU.mult, ALU.add),
                    waits=[("Vm", v_coef), ("Vm", s.v_ofm),
                           ("Tm", s._xwo_free)], inc="Vm")
                p.op(VEC, lambda e, fc=fc, ogr=ogr, xwo=xwo: e.tensor_tensor(
                    xwo[:, fc * 512:(fc + 1) * 512],
                    xwo[:, fc * 512:(fc + 1) * 512], ogr, ALU.mult),
                    waits=[("dO1%d" % par, dog[par])], inc="Vm")
                og_free[par] = p.val("Vm")
            v_xwo = p.val("Vm")
            for oc in range(NFC):
                q_ = s.PBIG[:, (oc % 4) * 512:(oc % 4) * 512 + 512]
                for kc in range(NFC):
                    p.op(TEN, lambda e, q=q_, kc=kc, oc=oc, xwo=xwo: e.matmul(
                        q, wslots[0][:, kc * DM + oc * 128:
                                     kc * DM + oc * 128 + 128],
                        xwo[:, kc * 512:(kc + 1) * 512],
                        start=(kc == 0), stop=(kc == NFC - 1)),
                        waits=([("dUp0", dwo), ("Vm", v_xwo),
                                ("Vm", s._en_qfree[oc % 4])] if kc == 0
                               else ()),
                        inc=("Tm" if kc == NFC - 1 else None))
                tm = p.val("Tm")
                s._xwo_free = tm
                par = oc % 2
                x1r = s.p16(s.EN_X1 + par * 512, 512)
                p.dma(SYN, x1r, s.x1_s[oc * 128:(oc + 1) * 128, js],
                      waits=[("Vm", x1_free[par])], inc="dX1%d" % par)
                dx = p.val("dX1%d" % par)
                p.op(VEC, lambda e, q=q_, oc=oc, j=j, x1r=x1r:
                     e.scalar_tensor_tensor(
                         s.Xb[:, oc * TC + j * 512: oc * TC + j * 512 + 512],
                         q, s.BO[:, oc:oc + 1], x1r, ALU.add, ALU.add),
                     waits=[("Tm", tm), ("dX1%d" % par, dx)], inc="Vm")
                s._en_qfree[oc % 4] = p.val("Vm")
                x1_free[par] = p.val("Vm")
        s.v_x2 = p.val("Vm")

    def stage_barrier(self):
        p = self.P
        cnt = {x: p.val(x) for x in ("Tm", "Vm", "Sm", "Gm")}
        dma = {x: p.val(x) for x in ("dT", "dHb", "dHs", "dX",
                                     "dX10", "dX11", "dF0", "dF1",
                                     "dSp0", "dSp1", "dSl0", "dSl1",
                                     "dZ0", "dZ1", "dMc0", "dMc1",
                                     "dQ20", "dQ21", "dO10", "dO11",
                                     "dUp0", "dUp1", "dDn")}
        for eng in (TEN, VEC, SCA, GPS):
            for sem, v in cnt.items():
                p.barrier_wait(eng, sem, v)
            for sem, v in dma.items():
                p.barrier_wait(eng, sem, v)

    def emit_all(self):
        self.emit_loads()
        self.emit_ln(xwaits=[("dG", self.c_x)])
        self.emit_router(xwaits=[("dG", self.c_x)])
        self.emit_ffn(stage=1)
        self.stage_barrier()
        self.emit_attn_proj()
        self.stage_barrier()
        self.emit_scan()
        self.stage_barrier()
        self.emit_fold()
        self.stage_barrier()
        self.emit_pass2()
        self.stage_barrier()
        self.emit_attn_end()
        self.stage_barrier()
        self._lnq_free = [self.P.val("Vm")] * NTT
        self.emit_ln(xwaits=[("Vm", self.v_x2)])
        self.emit_router()
        self.emit_ffn(stage=2)
        self.finish()


_NC_CACHE = {}


def _build_full():
    if "nc" not in _NC_CACHE:
        b = Builder()
        b.emit_all()
        _NC_CACHE["nc"] = b.build()
    return _NC_CACHE["nc"]


def kernel(**inputs):
    nc = _build_full()
    in_maps = [_prep_core_inputs(inputs, core) for core in range(NCORES)]
    res = run_bass_kernel_spmd(nc, in_maps, core_ids=list(range(NCORES)))
    out = np.concatenate([np.asarray(r["y"]).T for r in res.results], axis=0)
    return out.reshape(1, T, DM).astype(np.float32)


def _prep_core_inputs(full, core):
    f32 = np.float32
    t0 = core * TC
    x = np.ascontiguousarray(full["L0_attn_x"][0, t0:t0 + TC, :].T).astype(f32)
    conv = np.concatenate([full["L0_attn_q_conv"], full["L0_attn_k_conv"],
                           full["L0_attn_v_conv"]], axis=1)
    convw = np.zeros((128, NFC * 12), f32)
    for fc in range(NFC):
        convw[:, fc * 12:(fc + 1) * 12] = conv[fc * 128:(fc + 1) * 128, :]
    bqkvg = np.zeros((128, 4 * NFC), f32)
    for i, b in enumerate(["bq", "bk", "bv", "bg"]):
        bqkvg[:, i * NFC:(i + 1) * NFC] = np.asarray(
            full["L0_attn_" + b]).reshape(NFC, 128).T
    bo_c = np.asarray(full["L0_attn_bo"]).reshape(NFC, 128).T.astype(f32)
    bab = np.stack([full["L0_attn_ba"], full["L0_attn_bb"]], axis=1).astype(f32)
    gns = full["L0_attn_gn_scale"].reshape(NFC, 128).T.astype(f32)
    gnb = full["L0_attn_gn_bias"].reshape(NFC, 128).T.astype(f32)
    dwn = full["L0_ffn_down_weights"]
    dwn_t = np.transpose(dwn, (1, 2, 0))
    down_r = np.zeros((NFC, E, 4, 128, 128), f32)
    for oc in range(NFC):
        for e_ in range(E):
            blk = dwn[e_, oc * 128:(oc + 1) * 128, :]
            for ic in range(4):
                down_r[oc, e_, ic] = blk[:, ic * 128:(ic + 1) * 128].T
    shd = full["L0_ffn_shared_down_weight"]
    shdown_r = np.zeros((NFC, 16, 128, 128), f32)
    for oc in range(NFC):
        blk = shd[oc * 128:(oc + 1) * 128, :]
        for sc in range(16):
            shdown_r[oc, sc] = blk[:, sc * 128:(sc + 1) * 128].T
    tri2 = np.zeros((128, 256), f32)
    sidx = np.arange(128)[:, None]
    tidx = np.arange(128)[None, :]
    tri2[:, 0:128] = np.where(sidx < tidx, 0.0, -60000.0)
    tri2[:, 128:256] = np.where(sidx <= tidx, 0.0, -60000.0)
    onescol = np.zeros((128, 2), f32); onescol[:, 0] = 1.0
    selsum = np.zeros((128, NFC * 16), f32)
    expand = np.zeros((16, NFC * 128), f32)
    for fc in range(NFC):
        selsum[0:64, fc * 16 + 2 * fc] = 1.0
        selsum[64:128, fc * 16 + 2 * fc + 1] = 1.0
        expand[2 * fc, fc * 128: fc * 128 + 64] = 1.0
        expand[2 * fc + 1, fc * 128 + 64: fc * 128 + 128] = 1.0
    iota8 = np.tile(np.arange(8, dtype=f32)[None, :], (128, 1))
    onehot = np.zeros((8, 1), f32)
    if core > 0:
        onehot[core - 1, 0] = 1.0
    foldA = np.zeros((64, 7), f32)
    foldA[:, :core] = 1.0
    foldB = np.zeros((64, 7 * 64), f32)
    for j in range(7):
        if j >= core:
            foldB[:, j * 64:(j + 1) * 64] = np.eye(64, dtype=f32)
    fold8 = np.zeros((128, 16), f32)
    for r_ in range(8):
        fold8[r_ * 16 + np.arange(16), np.arange(16)] = 1.0
    d = {
        "x0": x,
        "wq": full["L0_attn_Wq"].astype(f32), "wk": full["L0_attn_Wk"].astype(f32),
        "wv": full["L0_attn_Wv"].astype(f32), "wg": full["L0_attn_Wg"].astype(f32),
        "wo": full["L0_attn_Wo"].astype(f32),
        "bqkvg": bqkvg, "bo_c": bo_c, "bab": bab,
        "wa": full["L0_attn_Wa"].astype(f32), "wb": full["L0_attn_Wb"].astype(f32),
        "convw": convw, "gns": gns, "gnb": gnb,
        "gatew": np.ascontiguousarray(full["L0_ffn_gate_weight"].T).astype(f32),
        "gateb": full["L0_ffn_gate_bias"].reshape(8, 1).astype(f32),
        "upw": np.ascontiguousarray(np.transpose(
            full["L0_ffn_up_weights"], (0, 2, 1))).astype(f32),
        "downw": down_r, "shdownw": shdown_r,
        "shupw": np.ascontiguousarray(full["L0_ffn_shared_up_weight"].T).astype(f32),
        "onehot": onehot, "h0m": np.array([[0.0 if core == 0 else 1.0]], f32),
        "foldA": foldA, "foldB": foldB, "fold8": fold8,
        "ident": np.eye(128, dtype=f32),
        "identb": np.eye(128, dtype=f32).astype(BFNP),
        "tri2": tri2, "onescol": onescol.astype(BFNP), "onescol32": onescol,
        "selsum": selsum.astype(BFNP), "expand": expand.astype(BFNP),
        "selsum32": selsum, "expand32": expand,
        "iota8": iota8, "onesr": np.ones((1, 128), f32), "ones4": _ones4(),
        "epsc": np.tile(np.array([[LNEPS, GNEPS]], f32), (128, 1)),
        "gateb128": np.tile(np.asarray(full["L0_ffn_gate_bias"],
                                       dtype=f32)[None, :], (128, 1)),
        "zrow": np.zeros((16, 512), f32),
    }
    return d


def _bias_row(v, n):
    return v.reshape(1, n)


def _ones4():
    o = np.zeros((128, 128), np.float32)
    for r in (0, 32, 64, 96):
        o[r, :] = 1.0
    return o




# version 04
# speedup vs baseline: 1.0119x; 1.0119x over previous
import numpy as np
import ml_dtypes
from contextlib import ExitStack

import concourse.bass as bass
import concourse.mybir as mybir
from concourse.bass_utils import run_bass_kernel_spmd

F32 = mybir.dt.float32
BF16 = mybir.dt.bfloat16
AF = mybir.ActivationFunctionType
ALU = mybir.AluOpType
BFNP = ml_dtypes.bfloat16

NCORES = 8
T = 16384
TC = T // NCORES
DM = 1024
NH = 16
HD = 64
CH = 128
NCH = TC // CH
NFC = DM // 128
NTT = TC // 512
NTG = TC // 128
E = 8
EI = 512
SI = 2048
LNEPS = 1e-5
GNEPS = 1e-5
RSCALE = 2.5

TEN, VEC, SCA, GPS, SYN = "tensor", "vector", "scalar", "gpsimd", "sync"
ENGINES = (TEN, VEC, SCA, GPS, SYN)



def pbcast(a, n):
    ap = [list(x) for x in a.ap]
    ap[0] = [0, n]
    return bass.AP(a.tensor, a.offset, ap)

class Prog:

    def __init__(self):
        self.ops = {e: [] for e in ENGINES}
        self.count = {}
        self.handles = {}

    def val(self, sem):
        return self.count.get(sem, 0)

    def bump(self, sem, n):
        self.count[sem] = self.count.get(sem, 0) + n
        return self.count[sem]

    def op(self, engine, build, waits=(), inc=None, inc_n=1, drain=None):
        waits = tuple((s, v) for (s, v) in waits if v > 0)
        if inc is not None:
            self.bump(inc, inc_n)
        if drain is None:
            drain = engine in (VEC, SCA)

        def fn(eng, _w=waits, _b=build, _i=inc, _n=inc_n, _d=drain):
            if _d:
                eng.drain()
            for s, v in _w:
                eng.wait_ge(self.handles[s], v)
            ins = _b(eng)
            if _i is not None:
                ins.then_inc(self.handles[_i], _n)

        self.ops[engine].append(fn)

    def dma(self, engine, out, in_, waits=(), inc=None):
        self.op(engine, lambda e, o=out, i=in_: e.dma_start(out=o, in_=i),
                waits=waits, inc=inc, inc_n=16)

    def barrier_wait(self, engine, sem, value):
        if value > 0:
            self.ops[engine].append(
                lambda eng, s=sem, v=value: eng.wait_ge(self.handles[s], v))

    def replay(self, block):
        for name in ENGINES:
            def run(eng, n=name):
                for fn in self.ops[n]:
                    fn(eng)
            getattr(block, name)(run)


class Builder:
    def __init__(self, taps=(), ncores=NCORES):
        self.ncores = ncores
        self.nc = bass.Bass("TRN2", target_bir_lowering=False)
        self.P = Prog()
        self.st = ExitStack()
        self.taps = {}
        self.want_taps = set(taps)
        self._declare()

    def din(self, name, shape, dt=F32):
        return self.nc.dram_tensor(name, shape, dt, kind="ExternalInput").ap()

    def dint(self, name, shape, dt):
        return self.nc.dram_tensor(name, shape, dt).ap()

    def sbuf(self, name, shape, dt):
        return self.st.enter_context(self.nc.sbuf_tensor(name, shape, dt))

    def psum(self, name, shape, dt=F32):
        return self.st.enter_context(self.nc.psum_tensor(name, shape, dt))

    def tap(self, name, ap, shape, dt, waits):
        if name not in self.want_taps:
            return
        t = self.nc.dram_tensor("tap_" + name, shape, dt,
                                kind="ExternalOutput").ap()
        self.taps[name] = t
        self.P.dma(SYN, t, ap,
                   waits=list(waits) + [("dT", self.P.val("dT"))], inc="dT")

    def _declare(self):
        nc, P = self.nc, self.P
        din, dint, sb, ps = self.din, self.dint, self.sbuf, self.psum
        self.x0 = din("x0", [DM, TC])
        self.wq = din("wq", [DM, DM]); self.wk = din("wk", [DM, DM])
        self.wv = din("wv", [DM, DM]); self.wg = din("wg", [DM, DM])
        self.wo = din("wo", [DM, DM])
        self.bqkvg = din("bqkvg", [128, 4 * NFC])
        self.bo_c = din("bo_c", [128, NFC])
        self.bab = din("bab", [16, 2])
        self.wa = din("wa", [DM, NH]); self.wb = din("wb", [DM, NH])
        self.convw = din("convw", [128, NFC * 12])
        self.gns = din("gns", [128, NFC]); self.gnb = din("gnb", [128, NFC])
        self.gatew = din("gatew", [DM, E]); self.gateb = din("gateb", [8, 1])
        self.gateb128 = din("gateb128", [128, 8])
        self.upw = din("upw", [E, DM, EI])
        self.downw = din("downw", [NFC, E, 4, 128, 128])
        self.shupw = din("shupw", [DM, SI])
        self.shdownw = din("shdownw", [NFC, 16, 128, 128])
        self.onehot = din("onehot", [8, 1])
        self.h0m = din("h0m", [1, 1])
        self.foldA = din("foldA", [64, 7])
        self.foldB = din("foldB", [64, 7 * 64])
        self.fold8 = din("fold8", [128, 16])
        self.ident = din("ident", [128, 128])
        self.identb = din("identb", [128, 128], BF16)
        self.tri2 = din("tri2", [128, 256])
        self.onescol = din("onescol", [128, 2], BF16)
        self.onescol32 = din("onescol32", [128, 2])
        self.selsum = din("selsum", [128, NFC * 16], BF16)
        self.selsum32 = din("selsum32", [128, NFC * 16])
        self.expand = din("expand", [16, NFC * 128], BF16)
        self.expand32 = din("expand32", [16, NFC * 128])
        self.iota8 = din("iota8", [128, 8])
        self.onesr = din("onesr", [1, 128])
        self.ones4 = din("ones4", [128, 128])
        self.zrow = din("zrow", [16, 512])
        self.epsc = din("epsc", [128, 2])
        self.y = nc.dram_tensor("y", [DM, TC], F32, kind="ExternalOutput").ap()

        self.upb = dint("upb", [E, DM, EI], BF16)
        self.downb = dint("downb", [NFC, E, 4, 128, 128], BF16)
        self.shupb = dint("shupb", [DM, SI], BF16)
        self.shdownb = dint("shdownb", [NFC, 16, 128, 128], BF16)
        self.wqb = dint("wqb", [DM, DM], BF16)
        self.wkb = dint("wkb", [DM, DM], BF16)
        self.wvb = dint("wvb", [DM, DM], BF16)
        self.wgb = dint("wgb", [DM, DM], BF16)
        self.wob = dint("wob", [DM, DM], BF16)
        self.gatewb = dint("gatewb", [DM, E], BF16)
        self.wab = dint("wab", [DM, NH], BF16)
        self.wbb = dint("wbb", [DM, NH], BF16)
        self.qhat_s = dint("qhat_s", [NCH, DM, CH], BF16)
        self.ksil_s = dint("ksil_s", [NCH, DM, CH], BF16)
        self.vsil_s = dint("vsil_s", [NCH, DM, CH], BF16)
        self.og_s = dint("og_s", [DM, TC], BF16)
        self.x1_s = dint("x1_s", [DM, TC], BF16)
        self.z_s = dint("z_s", [NCH, NH, HD, CH], BF16)
        self.o1_s = dint("o1_s", [NCH, NH, CH, HD], BF16)
        self.mc_s = dint("mc_s", [NCH, NH, HD, HD], BF16)
        self.q2_s = dint("q2_s", [NCH, NH, HD, HD], BF16)
        self.hal_in = dint("hal_in", [DM, 3], BF16)
        self.halo_b = dint("halo_b", [NFC, 1, 384], BF16)
        self.hal_out = dint("hal_out", [self.ncores * DM, 3], BF16)
        self.ag2_in = dint("ag2_in", [HD, NH * 2 * HD], BF16)
        self.ag2_out = dint("ag2_out", [self.ncores * HD, NH * 2 * HD], BF16)
        self.ag3_in = dint("ag3_in", [NH, 2], F32)
        self.ag3_out = dint("ag3_out", [self.ncores * NH, 2], F32)

        self.Xb = sb("Xb", [128, NFC * TC], BF16)
        self.T1 = sb("T1", [128, TC], F32)
        self.T3 = sb("T3", [128, TC], F32)
        self.VSTK = sb("VSTK", [128, TC], F32)
        self.STATS = sb("STATS", [128, 2048], F32)
        self.STGF = sb("STGF", [128, 2048], F32)
        self.RB = sb("RB", [128, TC], BF16)
        self.CONVW = sb("CONVW", [128, NFC * 12], F32)
        self.GNS = sb("GNS", [128, NFC], F32)
        self.GNB = sb("GNB", [128, NFC], F32)
        self.BQKVG = sb("BQKVG", [128, 4 * NFC], F32)
        self.BO = sb("BO", [128, NFC], F32)
        self.BAB = sb("BAB", [16, 2], F32)
        self.IDENT = sb("IDENT", [128, 128], F32)
        self.IDENTB = sb("IDENTB", [128, 128], BF16)
        self.TRI2 = sb("TRI2", [128, 256], F32)
        self.ONESC = sb("ONESC", [128, 2], BF16)
        self.ONESC32 = sb("ONESC32", [128, 2], F32)
        self.SELSUM = sb("SELSUM", [128, NFC * 16], BF16)
        self.SELSUM32 = sb("SELSUM32", [128, NFC * 16], F32)
        self.EXPAND = sb("EXPAND", [16, NFC * 128], BF16)
        self.EXPAND32 = sb("EXPAND32", [16, NFC * 128], F32)
        self.IOTA8 = sb("IOTA8", [128, 8], F32)
        self.ONESR = sb("ONESR", [1, 128], F32)
        self.ONES4 = sb("ONES4", [128, 128], F32)
        self.ZROW = sb("ZROW", [16, 512], F32)
        self.EPSC = sb("EPSC", [128, 2], F32)
        self.ONEHOT = sb("ONEHOT", [8, 1], BF16)
        self.H0M = sb("H0M", [1, 1], F32)
        self.FOLDA = sb("FOLDA", [64, 7], F32)
        self.FOLDB = sb("FOLDB", [64, 7 * 64], F32)
        self.FOLD8 = sb("FOLD8", [128, 16], F32)
        self.GCOLS = sb("GCOLS", [128, 4 * NH], F32)
        self.GCREP = sb("GCREP", [64, NH], F32)
        self.SLOC = sb("SLOC", [64, NH * HD], BF16)
        self.MCH = sb("MCH", [64, NH * HD], BF16)
        self.SS = sb("SS", [64, NH * HD], BF16)
        self.STRUE = sb("STRUE", [64, NH * HD], BF16)
        self.GSTATS = sb("GSTATS", [128, 16], F32)
        self.MRS = sb("MRS", [16, 8], F32)
        self.COEF = sb("COEF", [128, 2 * NFC], F32)
        self.R2TM = sb("R2TM", [128, NTG], F32)
        self.GATEB = sb("GATEB", [8, 1], F32)
        self.GATEB128 = sb("GATEB128", [128, 8], F32)
        self.YSTG = sb("YSTG", [128, 1024], F32)
        self.WSUMS = sb("WSUMS", [128, 4160], BF16)
        self.MROWB = sb("MROWB", [128, 2048], BF16)
        self.POOL16 = sb("POOL16", [128, 40512], BF16)
        self.POOL32 = sb("POOL32", [128, 2048], F32)

        self.PBIG = ps("PBIG", [128, 2048])
        self.PE = ps("PE", [128, 512]); self.PF = ps("PF", [128, 512])
        self.PG = ps("PG", [128, 512])
        self.PT = ps("PT", [128, 1024], BF16)

        for s in ("dT", "dG", "dG2", "dW", "dWs", "dUp0", "dUp1", "dDn", "dX", "dHs", "dHb", "dX10", "dX11", "dF0", "dF1", "dSp0", "dSp1", "dSl0", "dSl1", "dZ0", "dZ1", "dMc0", "dMc1", "dQ20", "dQ21", "dO10", "dO11", "Tm", "Vm", "Sm", "Gm", "cc"):
            P.handles[s] = self.st.enter_context(nc.semaphore(s))
        self._rb_free = 0
        self._r2_free = 0
        self._lnq_free = [0, 0, 0, 0]
        self._ffn_q_free = [0, 0, 0, 0]
        self._x1dma = 0
        self._ydma = 0
        self._stg_free = 0
        self._stg_dma = 0
        self._cc_done = 0
        self._hrow_v = 0
        self._hbounce_d = 0
        self._sq_free = 0
        self._ssq_v = 0
        self._ab_v = 0
        self._ab_s = 0
        self._spill_d = 0
        self._pbig_sfree = [0, 0, 0, 0]
        self._pbig_vfree = [0, 0, 0, 0]
        self._p2big_free = [0, 0, 0, 0]
        self._otm_free = 0
        self._p2wo_free = 0
        self._p2wo_sfree = 0
        self._xwo_free = 0
        self._en_qfree = [0, 0, 0, 0]

    def p16(self, off, width):
        assert off >= 0 and off + width <= 40512, (off, width)
        return self.POOL16[:, off:off + width]

    def p32(self, off, width):
        assert off >= 0 and off + width <= 2048, (off, width)
        return self.POOL32[:, off:off + width]

    def emit_loads(self):
        s, p = self, self.P
        for dst, src in [
            (s.IDENT, s.ident), (s.TRI2, s.tri2), (s.ONESC32, s.onescol32),
            (s.IOTA8, s.iota8), (s.H0M, s.h0m), (s.FOLDA, s.foldA), (s.FOLDB, s.foldB),
            (s.FOLD8, s.fold8),
            (s.GNS, s.gns), (s.GNB, s.gnb), (s.CONVW, s.convw),
            (s.SELSUM32, s.selsum32), (s.EXPAND32, s.expand32),
            (s.BQKVG, s.bqkvg), (s.BO, s.bo_c), (s.BAB, s.bab),
            (s.ONESR, s.onesr), (s.GATEB, s.gateb), (s.EPSC, s.epsc),
            (s.GATEB128, s.gateb128), (s.ZROW, s.zrow), (s.ONES4, s.ones4),
        ]:
            p.dma(SYN, dst[:], src, inc="dT")
        for dst, src in [(s.IDENTB, s.identb), (s.ONESC, s.onescol),
                         (s.SELSUM, s.selsum), (s.EXPAND, s.expand)]:
            p.dma(SYN, dst[:], src, inc="dT")
        p.dma(GPS, s.ONEHOT[:], s.onehot, inc="dG")
        for fc in range(NFC):
            p.dma(GPS, s.Xb[:, fc * TC:(fc + 1) * TC],
                  s.x0[fc * 128:(fc + 1) * 128, :], inc="dG")
        s.c_consts = p.val("dT")
        s.c_x = p.val("dG")
        for eng in (TEN, VEC, SCA):
            p.barrier_wait(eng, "dT", s.c_consts)
        p.barrier_wait(TEN, "dG", s.c_x)
        s.wcast_at = {}
        for nm, src, dst in [
            ("gate", s.gatew, s.gatewb), ("shup", s.shupw, s.shupb),
            ("up", s.upw, s.upb), ("down", s.downw, s.downb),
            ("shdown", s.shdownw, s.shdownb),
            ("wq", s.wq, s.wqb), ("wk", s.wk, s.wkb), ("wv", s.wv, s.wvb),
            ("wg", s.wg, s.wgb), ("wa", s.wa, s.wab), ("wb", s.wb, s.wbb),
            ("wo", s.wo, s.wob),
        ]:
            p.dma(GPS, dst, src, waits=[("dW", p.val("dW"))], inc="dW")
            s.wcast_at[nm] = p.val("dW")

        def wsum(dst_row, wsrc, ncols, wait_w):
            for oc in range(0, ncols, 512):
                ow = min(512, ncols - oc)
                for kc in range(NFC):
                    buf = s.p16(kc * 512, 512)
                    p.dma(SYN, buf[:, 0:ow],
                          wsrc[kc * 128:(kc + 1) * 128, oc:oc + ow],
                          waits=[("dW", wait_w),
                                 ("Tm", s._wsum_tm)], inc="dWs")
                dt = p.val("dWs")
                for kc in range(NFC):
                    buf = s.p16(kc * 512, 512)
                    p.op(TEN, lambda e, b=buf, ow=ow, k=kc: e.matmul(
                        s.PG[0:1, 0:ow], s.ONESC[:, 0:1], b[:, 0:ow],
                        start=(k == 0), stop=(k == NFC - 1)),
                        waits=([("dWs", dt), ("Vm", s._wsum_vm)]
                               if kc == 0 else ()),
                        inc=("Tm" if kc == NFC - 1 else None))
                s._wsum_tm = p.val("Tm")
                p.op(VEC, lambda e, d=dst_row, oc=oc, ow=ow: e.tensor_scalar(
                    d[:, oc:oc + ow], s.PG[0:1, 0:ow], -1.0, None, ALU.mult),
                    waits=[("Tm", s._wsum_tm)], inc="Vm")
                s._wsum_vm = p.val("Vm")

        s._wsum_tm = 0
        s._wsum_vm = 0
        wsum(s.WSUMS[32:33, 2048:2056], s.gatewb, E, s.wcast_at["gate"])
        wsum(s.WSUMS[32:33, 0:2048], s.shupb, SI, s.wcast_at["shup"])
        for e_ in range(E):
            wsum(s.WSUMS[0:1, e_ * 512:(e_ + 1) * 512],
                 s.upb[e_], EI, s.wcast_at["up"])
        for i_, nm in enumerate(("wq", "wk", "wv", "wg")):
            wsum(s.WSUMS[64:65, i_ * DM:(i_ + 1) * DM],
                 getattr(s, nm + "b"), DM, s.wcast_at[nm])
        wsum(s.WSUMS[64:65, 4 * DM: 4 * DM + 16], s.wab, NH,
             s.wcast_at["wa"])
        wsum(s.WSUMS[64:65, 4 * DM + 16: 4 * DM + 32], s.wbb, NH,
             s.wcast_at["wb"])
        s.v_wsum = p.val("Vm")
        s.t_wsum = p.val("Tm")

    LN_SQ = 24576

    def emit_ln(self, xwaits=()):
        s, p = self, self.P
        sq_last_use = [0, 0]
        for j in range(NTT):
            js = slice(j * 512, (j + 1) * 512)
            q = s.PBIG[:, js]
            for fc in range(NFC):
                xs = s.Xb[:, fc * TC + j * 512: fc * TC + j * 512 + 512]
                bi = fc % 2
                sq = s.p16(s.LN_SQ + bi * 512, 512)
                w = [("Tm", sq_last_use[bi])]
                if fc == 0:
                    w = w + list(xwaits) + [("Vm", s._lnq_free[j])]
                p.op(SCA, lambda e, x=xs, qq=sq: e.activation(qq[:], x, AF.Square),
                     waits=w, inc="Sm")
                sm = p.val("Sm")
                p.op(TEN, lambda e, x=xs, fc=fc, q=q: e.matmul(
                    q[0:1, :], s.ONESC[:, 0:1], x,
                    start=(fc == 0), stop=(fc == NFC - 1)),
                    waits=(list(xwaits) + [("Vm", s._lnq_free[j])]) if fc == 0 else ())
                p.op(TEN, lambda e, sq=sq, fc=fc, q=q: e.matmul(
                    q[32:33, :], s.ONESC[:, 0:1], sq[:],
                    start=(fc == 0), stop=(fc == NFC - 1)),
                    waits=[("Sm", sm)], inc="Tm")
                sq_last_use[bi] = p.val("Tm")
            tm = p.val("Tm")
            ST = s.STATS
            p.op(VEC, lambda e, js=js, q=q: e.tensor_scalar(
                ST[0:1, js], q[0:1, :], 1.0 / DM, None,
                ALU.mult), waits=[("Tm", tm)], inc="Vm")
            p.op(VEC, lambda e, js=js: e.tensor_scalar(
                s.MROWB[0:1, js], ST[0:1, js], 1.0, None, ALU.mult), inc="Vm")
            p.op(VEC, lambda e, js=js: e.tensor_scalar(
                s.MROWB[32:33, js], ST[0:1, js], 1.0, None, ALU.mult), inc="Vm")
            p.op(VEC, lambda e, js=js: e.tensor_scalar(
                s.MROWB[64:65, js], ST[0:1, js], 1.0, None, ALU.mult), inc="Vm")
            p.op(VEC, lambda e, js=js: e.tensor_tensor(
                ST[32:33, js], ST[0:1, js], ST[0:1, js], ALU.mult), inc="Vm")
            p.op(VEC, lambda e, js=js, q=q: e.scalar_tensor_tensor(
                ST[64:65, js], q[32:33, :], 1.0 / DM, ST[32:33, js],
                ALU.mult, ALU.subtract), inc="Vm")
            s._lnq_free[j] = p.val("Vm")
            vv = p.val("Vm")
            p.op(SCA, lambda e, js=js: e.activation(
                ST[64:65, js], ST[64:65, js], AF.Sqrt, bias=s.EPSC[0:1, 0:1]),
                waits=[("Vm", vv)], inc="Sm")
            sv = p.val("Sm")
            p.op(VEC, lambda e, js=js: e.reciprocal(
                ST[64:65, js], ST[64:65, js]), waits=[("Sm", sv)], inc="Vm")
            p.op(VEC, lambda e, js=js: e.tensor_scalar(
                s.MROWB[96:97, js], ST[64:65, js], 1.0, None, ALU.mult), inc="Vm")
            p.op(VEC, lambda e, js=js: e.tensor_tensor(
                ST[32:33, js], ST[64:65, js], ST[64:65, js], ALU.mult), inc="Vm")
            vv = p.val("Vm")
            p.op(TEN, lambda e, js=js: e.matmul(
                s.PF[0:128, 0:512], s.ONES4[64:65, 0:128], ST[64:65, js],
                start=True, stop=True),
                waits=[("Vm", vv), ("Vm", s._r2_free)], inc="Tm")
            tm_rb = p.val("Tm")
            p.op(VEC, lambda e, js=js: e.tensor_scalar(
                s.RB[:, js], s.PF[0:128, 0:512], 1.0, None, ALU.mult),
                waits=[("Tm", tm_rb)], inc="Vm")
            vv_rb = p.val("Vm")
            for gsub in range(4):
                g = 4 * j + gsub
                p.op(TEN, lambda e, j=j, gs=gsub: e.matmul(
                    s.PF[0:128, 0:1],
                    s.STATS[32:33, j * 512 + gs * 128: j * 512 + gs * 128 + 128],
                    s.IDENT[32:33, 32:33], is_transpose=True, start=True,
                    stop=True), waits=[("Vm", vv_rb), ("Vm", s._r2_free)],
                    inc="Tm")
                tm3 = p.val("Tm")
                p.op(VEC, lambda e, g=g: e.tensor_scalar(
                    s.R2TM[:, g:g + 1], s.PF[0:128, 0:1], 1.0,
                    None, ALU.mult), waits=[("Tm", tm3)], inc="Vm")
                s._r2_free = p.val("Vm")
        s.v_ln = p.val("Vm")
        s.s_ln = p.val("Sm")
        s.t_ln = p.val("Tm")
        s.g_ln = p.val("Gm")

    FF_R2CW = 0
    FF_R2SH = 16384
    FF_UP = 24576
    FF_DOWN = 32768
    FF_GATE = 38912

    def emit_router(self, xwaits=()):
        s, p = self, self.P
        gate_sb = s.p16(s.FF_GATE, 64)
        p.dma(GPS, gate_sb.rearrange("p (kc c) -> p kc c", c=8),
              s.gatewb.rearrange("(kc p) c -> p kc c", p=128),
              waits=[("dW", s.wcast_at["gate"])], inc="dG2")
        dgate = p.val("dG2")
        for j in range(NTT):
            q = s.PBIG[0:8, j * 512:(j + 1) * 512]
            xw = [("dG2", dgate), ("Vm", s._lnq_free[j])] + list(xwaits)
            for kc in range(NFC):
                xs = s.Xb[:, kc * TC + j * 512: kc * TC + j * 512 + 512]
                p.op(TEN, lambda e, q=q, xs=xs, kc=kc: e.matmul(
                    q, gate_sb[:, kc * 8:(kc + 1) * 8], xs,
                    start=(kc == 0), stop=False),
                    waits=(xw if kc == 0 else ()))
            p.op(TEN, lambda e, q=q, j=j: e.matmul(
                q, s.WSUMS[32:33, 2048:2056],
                s.MROWB[32:33, j * 512:(j + 1) * 512],
                start=False, stop=True),
                waits=[("Vm", s.v_wsum), ("Vm", s.v_ln)],
                inc="Tm")
            tm = p.val("Tm")
            p.op(VEC, lambda e, q=q, j=j: e.tensor_tensor(
                s.T1[64:72, j * 512:(j + 1) * 512], q,
                s.RB[0:8, j * 512:(j + 1) * 512], ALU.mult),
                waits=[("Tm", tm)], inc="Vm")
        vv = p.val("Vm")
        s._lnq_free = [p.val("Vm")] * NTT
        p.op(SCA, lambda e: e.activation(
            s.T1[64:72, :], s.T1[64:72, :],
            AF.Sigmoid), waits=[("Vm", vv)], inc="Sm")
        sv = p.val("Sm")
        RTM = self.p32(0, 256).rearrange("p (g k) -> p g k", k=16)
        tfree = [0, 0]
        for g in range(NTG):
            bank = s.PE if g % 2 == 0 else s.PF
            p.op(TEN, lambda e, g=g, bank=bank: e.matmul(
                bank[0:128, 0:8],
                s.T1[64:72, g * 128:(g + 1) * 128],
                s.IDENT[64:72, 64:72], is_transpose=True, start=True, stop=True),
                waits=[("Sm", sv), ("Vm", tfree[g % 2])], inc="Tm")
            tm = p.val("Tm")
            p.op(VEC, lambda e, g=g, bank=bank: e.tensor_scalar(
                self.p32(0, 256)[:, g * 16:g * 16 + 8], bank[0:128, 0:8],
                1.0, None, ALU.mult), waits=[("Tm", tm)], inc="Vm")
            p.op(VEC, lambda e, g=g: e.tensor_tensor(
                self.p32(0, 256)[:, g * 16 + 8:g * 16 + 16],
                self.p32(0, 256)[:, g * 16:g * 16 + 8],
                s.GATEB128[:, :], ALU.add), inc="Vm")
            tfree[g % 2] = p.val("Vm")
        SCR = 256

        def nxt(w):
            nonlocal SCR
            r = self.p32(SCR, w)
            SCR += w
            return r
        M1A = nxt(16); M2A = nxt(16); M1B = nxt(16); M2B = nxt(16)
        GSA = nxt(16); GSB = nxt(16); SELA = nxt(16); SELB = nxt(16)
        TMP64 = nxt(64)
        MSK = nxt(128); TMP128 = nxt(128)
        I1 = nxt(16); I2 = nxt(16); W1R = nxt(16); W2R = nxt(16)
        REC = nxt(16); SCL = nxt(16); IND1 = nxt(128); IND2 = nxt(128)
        IOTB4 = nxt(4); IOTB8 = nxt(8); CWTM = nxt(128)
        v_rt = p.val("Vm")
        V = lambda fn: p.op(VEC, fn, inc="Vm")

        def b16(ap, k):
            return ap.rearrange("p (g o) -> p g o", o=1).broadcast_to([128, 16, k])

        def g3(ap, k):
            return ap.rearrange("p (g k) -> p g k", k=k)

        V(lambda e: e.tensor_scalar(IOTB4, s.IOTA8[:, 0:4], 16.0, None, ALU.add))
        V(lambda e: e.tensor_scalar(IOTB8, s.IOTA8[:, 0:8], 16.0, None, ALU.add))
        sfc3 = g3(self.p32(0, 256), 16)
        for (lo, M1, M2, GS) in ((8, M1A, M2A, GSA), (12, M1B, M2B, GSB)):
            gsl = self.p32(0, 256).rearrange("p (g k) -> p g k", k=16)[:, :, lo:lo + 4]
            V(lambda e, gsl=gsl, M1=M1: e.tensor_reduce(
                M1, gsl, mybir.AxisListType.X, ALU.max))
            V(lambda e, gsl=gsl, M1=M1: e.tensor_tensor(
                g3(TMP64, 4), gsl, b16(M1, 4), ALU.is_equal))
            V(lambda e: e.scalar_tensor_tensor(
                g3(TMP64, 4), g3(TMP64, 4), -16.0,
                IOTB4.rearrange("p (o k) -> p o k", o=1).broadcast_to([128, 16, 4]),
                ALU.mult, ALU.add))
            V(lambda e, M2=M2: e.tensor_reduce(
                M2, g3(TMP64, 4), mybir.AxisListType.X, ALU.min))
            V(lambda e, M2=M2: e.tensor_tensor(
                g3(TMP64, 4),
                s.IOTA8[:, 0:4].rearrange("p (o k) -> p o k", o=1).broadcast_to([128, 16, 4]),
                b16(M2, 4), ALU.is_equal))
            V(lambda e, gsl=gsl: e.scalar_tensor_tensor(
                g3(TMP64, 4), g3(TMP64, 4), -1e9, gsl, ALU.mult, ALU.add))
            V(lambda e, M2=M2: e.tensor_reduce(
                M2, g3(TMP64, 4), mybir.AxisListType.X, ALU.max))
            V(lambda e, M1=M1, M2=M2, GS=GS: e.tensor_tensor(
                GS, M1, M2, ALU.add))
        V(lambda e: e.tensor_tensor(SELA, GSA, GSB, ALU.is_ge))
        V(lambda e: e.tensor_tensor(SELB, GSA, GSB, ALU.is_lt))
        V(lambda e: e.tensor_tensor(
            g3(MSK, 8)[:, :, 0:4], sfc3[:, :, 8:12], b16(SELA, 4), ALU.mult))
        V(lambda e: e.tensor_tensor(
            g3(MSK, 8)[:, :, 4:8], sfc3[:, :, 12:16], b16(SELB, 4), ALU.mult))
        V(lambda e: e.tensor_reduce(I1, g3(MSK, 8), mybir.AxisListType.X, ALU.max))
        V(lambda e: e.tensor_tensor(g3(TMP128, 8), g3(MSK, 8), b16(I1, 8), ALU.is_equal))
        V(lambda e: e.scalar_tensor_tensor(
            g3(TMP128, 8), g3(TMP128, 8), -16.0,
            IOTB8.rearrange("p (o k) -> p o k", o=1).broadcast_to([128, 16, 8]),
            ALU.mult, ALU.add))
        V(lambda e: e.tensor_reduce(I1, g3(TMP128, 8), mybir.AxisListType.X, ALU.min))
        V(lambda e: e.tensor_tensor(
            g3(IND1, 8),
            s.IOTA8.rearrange("p (o k) -> p o k", o=1).broadcast_to([128, 16, 8]),
            b16(I1, 8), ALU.is_equal))
        V(lambda e: e.scalar_tensor_tensor(
            g3(TMP128, 8), g3(IND1, 8), -1e9, g3(MSK, 8), ALU.mult, ALU.add))
        V(lambda e: e.tensor_reduce(I2, g3(TMP128, 8), mybir.AxisListType.X, ALU.max))
        V(lambda e: e.tensor_tensor(g3(MSK, 8), g3(TMP128, 8), b16(I2, 8), ALU.is_equal))
        V(lambda e: e.scalar_tensor_tensor(
            g3(MSK, 8), g3(MSK, 8), -16.0,
            IOTB8.rearrange("p (o k) -> p o k", o=1).broadcast_to([128, 16, 8]),
            ALU.mult, ALU.add))
        V(lambda e: e.tensor_reduce(I2, g3(MSK, 8), mybir.AxisListType.X, ALU.min))
        V(lambda e: e.tensor_tensor(
            g3(IND2, 8),
            s.IOTA8.rearrange("p (o k) -> p o k", o=1).broadcast_to([128, 16, 8]),
            b16(I2, 8), ALU.is_equal))
        V(lambda e: e.tensor_tensor(g3(TMP128, 8), g3(IND1, 8), sfc3[:, :, 0:8], ALU.mult))
        V(lambda e: e.tensor_reduce(W1R, g3(TMP128, 8), mybir.AxisListType.X, ALU.add))
        V(lambda e: e.tensor_tensor(g3(TMP128, 8), g3(IND2, 8), sfc3[:, :, 0:8], ALU.mult))
        V(lambda e: e.tensor_reduce(W2R, g3(TMP128, 8), mybir.AxisListType.X, ALU.add))
        V(lambda e: e.tensor_tensor(REC, W1R, W2R, ALU.add))
        V(lambda e: e.tensor_scalar(REC, REC, 1e-20, None, ALU.add))
        V(lambda e: e.reciprocal(REC, REC))
        V(lambda e: e.tensor_tensor(SCL, REC, s.R2TM[:, :], ALU.mult))
        V(lambda e: e.tensor_scalar(SCL, SCL, RSCALE, None, ALU.mult))
        V(lambda e: e.tensor_tensor(g3(CWTM, 8), g3(IND1, 8), b16(W1R, 8), ALU.mult))
        V(lambda e: e.tensor_tensor(g3(TMP128, 8), g3(IND2, 8), b16(W2R, 8), ALU.mult))
        V(lambda e: e.tensor_tensor(g3(CWTM, 8), g3(CWTM, 8), g3(TMP128, 8), ALU.add))
        V(lambda e: e.tensor_tensor(g3(CWTM, 8), g3(CWTM, 8), b16(SCL, 8), ALU.mult))
        vcw = p.val("Vm")
        tfree = [p.val("Vm"), p.val("Vm")]
        for g in range(NTG):
            bank = s.PE if g % 2 == 0 else s.PF
            p.op(TEN, lambda e, g=g, bank=bank: e.matmul(
                bank[0:8, 0:128], CWTM[:, g * 8:(g + 1) * 8],
                s.IDENT[0:128, 0:128], is_transpose=True, start=True,
                stop=True), waits=[("Vm", vcw), ("Vm", tfree[g % 2])],
                inc="Tm")
            tm = p.val("Tm")
            p.op(VEC, lambda e, g=g, bank=bank: e.tensor_scalar(
                s.T3[96:104, g * 128:(g + 1) * 128],
                bank[0:8, 0:128], 1.0, None, ALU.mult),
                waits=[("Tm", tm)], inc="Vm")
            tfree[g % 2] = p.val("Vm")
        s.v_router = p.val("Vm")

    def emit_ffn(self, stage):
        s, p = self, self.P
        up_free = [self.t_ln, self.t_ln]
        down_free = [0, 0]
        pbig_free = [p.val("Sm")] * 4
        cwb_free = [0, 0]
        dup = [0, 0]
        ddown = [0, 0]
        x_written = []
        for j in range(NTT):
            js = slice(j * 512, (j + 1) * 512)
            mrow = s.MROWB[0:1, j * 512:(j + 1) * 512]
            mrow32 = s.MROWB[32:33, j * 512:(j + 1) * 512]
            for scb in range(4):
                for sci in range(4):
                    sc = 4 * scb + sci
                    slot = sci % 2
                    buf = s.p16(s.FF_UP + slot * 4096, 1024)
                    p.dma(GPS, buf.rearrange("p (kc c) -> p kc c", c=128),
                          s.shupb.rearrange("(kc p) c -> p kc c", p=128)
                          [:, :, sc * 128:(sc + 1) * 128],
                          waits=[("dW", s.wcast_at["shup"]),
                                 ("Tm", up_free[slot])],
                          inc="dUp%d" % slot)
                    dup[slot] = p.val("dUp%d" % slot)
                    q = s.PBIG[:, sci * 512:(sci + 1) * 512]
                    for kc in range(NFC):
                        xs = s.Xb[:, kc * TC + j * 512: kc * TC + j * 512 + 512]
                        p.op(TEN, lambda e, q=q, b=buf, kc=kc, xs=xs: e.matmul(
                            q, b[:, kc * 128:(kc + 1) * 128], xs,
                            start=(kc == 0), stop=False),
                            waits=([("dUp%d" % slot, dup[slot]),
                                    ("Sm", pbig_free[sci]),
                                    ("Vm", s._ffn_q_free[sci])]
                                   if kc == 0 else ()))
                    p.op(TEN, lambda e, q=q, sc=sc, m32=mrow32: e.matmul(
                        q, s.WSUMS[32:33, sc * 128:(sc + 1) * 128],
                        m32, start=False, stop=True),
                        waits=[("Vm", s.v_wsum)],
                        inc="Tm")
                    up_free[slot] = p.val("Tm")
                tm = p.val("Tm")
                rsl = s.p16(s.FF_R2SH + scb * 2048, 2048)
                p.op(SCA, lambda e, rsl=rsl: e.activation(
                    rsl, s.PBIG[:, 0:2048], AF.Relu), waits=[("Tm", tm)],
                    inc="Sm")
                pbig_free = [p.val("Sm")] * 4
                sm = p.val("Sm")
                p.op(VEC, lambda e, rsl=rsl, j=j: e.tensor_tensor(
                    rsl.rearrange("p (sc c) -> p sc c", c=512),
                    rsl.rearrange("p (sc c) -> p sc c", c=512),
                    s.RB[:, j * 512:(j + 1) * 512]
                    .rearrange("p (o c) -> p o c", o=1)
                    .broadcast_to([128, 4, 512]), ALU.mult),
                    waits=[("Sm", sm)], inc="Vm")
                p.op(VEC, lambda e, rsl=rsl: e.tensor_tensor(
                    rsl, rsl, rsl, ALU.mult), inc="Vm")
            v_r2sh = p.val("Vm")
            for r_ in range(2):
                p.dma(SYN, s.STGF[32 * r_:32 * r_ + 1, 0:2048]
                      .rearrange("o (u c) -> o u c", c=512),
                      s.T3[96 + 4 * r_:100 + 4 * r_, js],
                      waits=[("Vm", s.v_router), ("Tm", s._stg_free)],
                      inc="dT")
            s._stg_dma = p.val("dT")
            for e_ in range(E):
                slot = e_ % 2
                buf = s.p16(s.FF_UP + slot * 4096, 4096)
                p.dma(GPS, buf.rearrange("p (kc c) -> p kc c", c=EI),
                      s.upb[e_].rearrange("(kc p) c -> p kc c", p=128),
                      waits=[("dW", s.wcast_at["up"]),
                             ("Tm", up_free[slot])], inc="dUp%d" % slot)
                dup[slot] = p.val("dUp%d" % slot)
                cbank = s.PE if e_ % 2 == 0 else s.PF
                p.op(TEN, lambda e, cbank=cbank, e_=e_: e.matmul(
                    cbank[:, 0:512],
                    s.ONES4[32 * (e_ // 4):32 * (e_ // 4) + 1, 0:128],
                    s.STGF[32 * (e_ // 4):32 * (e_ // 4) + 1,
                           (e_ % 4) * 512:(e_ % 4) * 512 + 512],
                    start=True, stop=True),
                    waits=[("dT", s._stg_dma), ("Vm", cwb_free[e_ % 2])],
                    inc="Tm")
                tcb = p.val("Tm")
                cb = s.p16(s.FF_GATE + 64 + (e_ % 2) * 512, 512)
                p.op(VEC, lambda e, cb=cb, cbank=cbank: e.tensor_scalar(
                    cb, cbank[:, 0:512], 1.0, None, ALU.mult),
                    waits=[("Tm", tcb)], inc="Vm")
                t_cwb = p.val("Vm")
                s._stg_free = p.val("Tm")
                for ic in range(4):
                    q = s.PBIG[:, ic * 512:(ic + 1) * 512]
                    for kc in range(NFC):
                        xs = s.Xb[:, kc * TC + j * 512: kc * TC + j * 512 + 512]
                        p.op(TEN, lambda e, q=q, b=buf, kc=kc, ic=ic, xs=xs:
                             e.matmul(q, b[:, kc * EI + ic * 128:
                                           kc * EI + ic * 128 + 128], xs,
                                      start=(kc == 0), stop=False),
                             waits=([("dUp%d" % slot, dup[slot]),
                                     ("Sm", pbig_free[ic]),
                                     ("Vm", s._ffn_q_free[ic])]
                                    if kc == 0 else ()))
                    p.op(TEN, lambda e, q=q, e_=e_, ic=ic, mrow=mrow: e.matmul(
                        q, s.WSUMS[0:1, e_ * 512 + ic * 128:
                                   e_ * 512 + ic * 128 + 128],
                        mrow, start=False, stop=True),
                        inc=("Tm" if ic == 3 else None))
                up_free[slot] = p.val("Tm")
                tm = p.val("Tm")
                rsl = s.p16(s.FF_R2CW + e_ * 2048, 2048)
                p.op(SCA, lambda e, rsl=rsl: e.activation(
                    rsl, s.PBIG[:, 0:2048], AF.Relu), waits=[("Tm", tm)],
                    inc="Sm")
                pbig_free = [p.val("Sm")] * 4
                sm = p.val("Sm")
                t1 = s.p32(0, 2048)
                p.op(VEC, lambda e, rsl=rsl, cb=cb: e.tensor_tensor(
                    t1.rearrange("p (ic c) -> p ic c", c=512),
                    rsl.rearrange("p (ic c) -> p ic c", c=512),
                    cb.rearrange("p (o c) -> p o c", o=1)
                    .broadcast_to([128, 4, 512]), ALU.mult),
                    waits=[("Sm", sm), ("Vm", t_cwb)], inc="Vm")
                cwb_free[e_ % 2] = p.val("Vm")
                p.op(VEC, lambda e, rsl=rsl, t1=t1: e.tensor_tensor(
                    rsl, t1, rsl, ALU.mult), inc="Vm")
            v_r2cw = p.val("Vm")
            for oc in range(NFC):
                slot = 0
                dbuf = s.p16(s.FF_DOWN, 6144)
                p.dma(GPS, dbuf[:, 0:4096].rearrange("p (u c) -> p u c", c=128),
                      s.downb[oc].rearrange("e i p c -> p (e i) c"),
                      waits=[("dW", s.wcast_at["down"]),
                             ("Tm", down_free[slot])], inc="dDn")
                p.dma(GPS, dbuf[:, 4096:6144].rearrange("p (u c) -> p u c", c=128),
                      s.shdownb[oc].rearrange("u p c -> p u c"),
                      waits=[("dW", s.wcast_at["shdown"])], inc="dDn")
                ddown[slot] = p.val("dDn")
                q = s.PBIG[:, (oc % 4) * 512:(oc % 4) * 512 + 512]
                first = True
                for e_ in range(E):
                    for ic in range(4):
                        rsl = s.p16(s.FF_R2CW + e_ * 2048 + ic * 512, 512)
                        w = []
                        if first:
                            w = [("dDn", ddown[slot]), ("Vm", v_r2cw),
                                 ("Vm", s._ffn_q_free[oc % 4])]
                        p.op(TEN, lambda e, q=q, d=dbuf, e_=e_, ic=ic, rsl=rsl:
                             e.matmul(q, d[:, (e_ * 4 + ic) * 128:
                                           (e_ * 4 + ic) * 128 + 128], rsl,
                                      start=(e_ == 0 and ic == 0), stop=False),
                             waits=w)
                        first = False
                for sc in range(16):
                    rsl = s.p16(s.FF_R2SH + sc * 512, 512)
                    p.op(TEN, lambda e, q=q, d=dbuf, sc=sc, rsl=rsl: e.matmul(
                        q, d[:, 4096 + sc * 128: 4096 + sc * 128 + 128], rsl,
                        start=False, stop=(sc == 15)),
                        waits=([("Vm", v_r2sh)] if sc == 0 else ()),
                        inc=("Tm" if sc == 15 else None))
                down_free[slot] = p.val("Tm")
                tm = p.val("Tm")
                xs = s.Xb[:, oc * TC + j * 512: oc * TC + j * 512 + 512]
                if stage == 1:
                    ot = s.p16(s.FF_GATE + 1088, 512)
                    p.op(VEC, lambda e, q=q, xs=xs, ot=ot:
                         e.scalar_tensor_tensor(ot, q, 0.5, xs, ALU.mult,
                                                ALU.add),
                         waits=[("Tm", tm), ("dX", s._x1dma)], inc="Vm")
                    s._ffn_q_free[oc % 4] = p.val("Vm")
                    vq = p.val("Vm")
                    p.dma(SYN, s.x1_s[oc * 128:(oc + 1) * 128, js], ot,
                          waits=[("Vm", vq)], inc="dX")
                    s._x1dma = p.val("dX")
                else:
                    ot = s.YSTG[:, (oc % 2) * 512:(oc % 2) * 512 + 512]
                    p.op(VEC, lambda e, q=q, xs=xs, ot=ot:
                         e.scalar_tensor_tensor(ot, q, 0.5, xs, ALU.mult,
                                                ALU.add),
                         waits=[("Tm", tm), ("dX", s._ydma)], inc="Vm")
                    s._ffn_q_free[oc % 4] = p.val("Vm")
                    vq = p.val("Vm")
                    p.dma(SYN, s.y[oc * 128:(oc + 1) * 128, js], ot,
                          waits=[("Vm", vq)], inc="dX")
                    s._ydma = p.val("dX")
        s.v_ffn = p.val("Vm")
        s.d_ffn = p.val("dT")

    AT_W = 0
    AT_PRE = 16384
    AT_CONV = 32800
    AT_SQ = 36896
    AT_HALG = 38944

    def emit_attn_proj(self):
        s, p = self, self.P
        p.op(VEC, lambda e: e.memset(s.VSTK[:, :], 0.0), inc="Vm")
        p.op(VEC, lambda e: e.memset(s.T1[:, :], 0.0), inc="Vm")
        for fc in range(NFC):
            p.dma(GPS, s.Xb[:, fc * TC:(fc + 1) * TC],
                  s.x1_s[fc * 128:(fc + 1) * 128, :],
                  waits=[("dX", s._x1dma), ("Vm", s.v_ffn)], inc="dG2")
        s.c_x1 = p.val("dG2")
        self.emit_ln(xwaits=[("dG2", s.c_x1)])

        s._pbig_sfree = [p.val("Sm")] * NTT
        s._pbig_vfree = [p.val("Vm")] * NTT
        wslots = [s.p16(s.AT_W, 8192), s.p16(s.AT_W + 8192, 8192)]
        wsem = ["dUp0", "dUp1"]
        wfree = [p.val("Tm"), p.val("Tm")]
        dw = [0, 0]
        projs = [("q", s.wqb, 0, s.qhat_s), ("k", s.wkb, 1, s.ksil_s),
                 ("v", s.wvb, 2, s.vsil_s), ("g", s.wgb, 3, None)]
        pre = s.p16(s.AT_PRE, 16416)
        prev_spill = {0: 0, 1: 0}
        s._d_spill = 0
        s._v_sumsq = 0
        s._halg_free = 0
        conv_free = [0, 0]
        sumsq_rows = {"q": 32, "k": 64}
        for pi, (pname, wdram, pidx, spill) in enumerate(projs):
            slot = pi % 2
            wv = wslots[slot]
            p.dma(GPS, wv.rearrange("p (kc c) -> p kc c", c=DM),
                  wdram.rearrange("(kc p) c -> p kc c", p=128),
                  waits=[("dW", s.wcast_at["w" + pname]),
                         ("Tm", wfree[slot])], inc=wsem[slot])
            dw[slot] = p.val(wsem[slot])
            if pname in ("q", "k"):
                r0 = sumsq_rows[pname]
                p.op(VEC, lambda e, r0=r0: e.memset(s.T3[r0:r0 + 16, :], 0.0),
                     inc="Vm")
            for oc in range(NFC):
                for j in range(NTT):
                    q_ = s.PBIG[:, j * 512:(j + 1) * 512]
                    for kc in range(NFC):
                        xs = s.Xb[:, kc * TC + j * 512: kc * TC + j * 512 + 512]
                        p.op(TEN, lambda e, q=q_, w=wv, kc=kc, oc=oc, xs=xs:
                             e.matmul(q, w[:, kc * DM + oc * 128:
                                           kc * DM + oc * 128 + 128], xs,
                                      start=(kc == 0), stop=False),
                             waits=([("dG2", s.c_x1), (wsem[slot], dw[slot]),
                                     ("Sm", s._pbig_sfree[j]),
                                     ("Vm", s._pbig_vfree[j])]
                                    if kc == 0 else ()))
                    p.op(TEN, lambda e, q=q_, oc=oc, j=j, pidx=pidx: e.matmul(
                        q, s.WSUMS[64:65, pidx * DM + oc * 128:
                                   pidx * DM + oc * 128 + 128],
                        s.MROWB[64:65, j * 512:(j + 1) * 512],
                        start=False, stop=True),
                        waits=[("Vm", s.v_wsum)],
                        inc=("Tm" if j == NTT - 1 else None))
                tm = p.val("Tm")
                tbuf = s.p16(s.AT_SQ, 2048)
                p.op(VEC, lambda e, tbuf=tbuf: e.tensor_tensor(
                    tbuf.rearrange("p (j c) -> p j c", c=512),
                    s.PBIG.rearrange("p (j c) -> p j c", c=512),
                    s.RB.rearrange("p (j c) -> p j c", c=512),
                    ALU.mult), waits=[("Tm", tm), ("Vm", s.v_ln),
                                      ("Tm", s._sq_free)], inc="Vm")
                s._pbig_vfree = [p.val("Vm")] * NTT
                p.op(VEC, lambda e, tbuf=tbuf, oc=oc, pidx=pidx: e.tensor_scalar(
                    pre[:, oc * 2052 + 3: oc * 2052 + 2051],
                    tbuf[:, 0:2048], s.BQKVG[:, pidx * NFC + oc:
                                             pidx * NFC + oc + 1],
                    None, ALU.add), inc="Vm")
                s._pre_v = p.val("Vm")
            if pname == "g":
                dhalo = p.val("dT")
                for oc in range(NFC):
                    cbuf = s.p16(s.AT_CONV + (oc % 2) * 2048, 2048)
                    p.op(SCA, lambda e, cbuf=cbuf, oc=oc: e.activation(
                        cbuf, pre[:, oc * 2052 + 3: oc * 2052 + 2051],
                        AF.Silu), waits=[("Vm", s._pre_v),
                                         ("dSp%d" % (oc % 2),
                                          conv_free[oc % 2])], inc="Sm")
                    sv = p.val("Sm")
                    p.dma(SYN, s.og_s[oc * 128:(oc + 1) * 128, :], cbuf,
                          waits=[("Sm", sv)], inc="dSp%d" % (oc % 2))
                    conv_free[oc % 2] = p.val("dSp%d" % (oc % 2))
                wfree[slot] = p.val("Tm")
                continue
            for oc in range(NFC):
                p.dma(SYN, s.hal_in[oc * 128:(oc + 1) * 128, 0:3],
                      pre[:, oc * 2052 + 2048: oc * 2052 + 2051],
                      waits=[("Vm", s._pre_v), ("cc", s._cc_done)], inc="dHs")
            dstage = p.val("dHs")
            p.op(GPS, lambda e: e.collective_compute(
                "AllGather", ALU.bypass, replica_groups=[list(range(self.ncores))],
                ins=[s.hal_in.opt()], outs=[s.hal_out.opt()]),
                waits=[("dHs", dstage)], inc="cc")
            s._cc_done = p.val("cc")
            hal_flat = s.hal_out.rearrange("(r f) c -> r (f c)", r=self.ncores)
            for oc in range(NFC):
                hg = s.p16(s.AT_HALG, 512)
                p.dma(GPS, hg[0:self.ncores, 0:384], hal_flat[:, oc * 384:(oc + 1) * 384],
                      waits=[("cc", s._cc_done), ("Tm", s._halg_free)],
                      inc="dG2")
                dh = p.val("dG2")
                p.op(TEN, lambda e, hg=hg: e.matmul(
                    s.PE[0:1, 0:384], s.ONEHOT[0:self.ncores, 0:1],
                    hg[0:self.ncores, 0:384],
                    start=True, stop=True),
                    waits=[("dG2", dh), ("Vm", s._hrow_v)], inc="Tm")
                s._halg_free = p.val("Tm")
                th = p.val("Tm")
                hrow = s.p16(s.AT_HALG + 512, 512)
                p.op(VEC, lambda e, hrow=hrow: e.tensor_scalar(
                    hrow[0:1, 0:384], s.PE[0:1, 0:384], s.H0M[0:1, 0:1], None,
                    ALU.mult), waits=[("Tm", th), ("dHb", s._hbounce_d)],
                    inc="Vm")
                vh = p.val("Vm")
                s._hrow_v = vh
                p.dma(SYN, s.halo_b[oc], hrow[0:1, 0:384],
                      waits=[("Vm", vh)], inc="dHb")
                db = p.val("dHb")
                p.dma(SYN, pre[:, oc * 2052: oc * 2052 + 3],
                      s.halo_b[oc].rearrange("o (q c) -> (o q) c", c=3),
                      waits=[("dHb", db)], inc="dHb")
                s._hbounce_d = p.val("dHb")
            dhalo = p.val("dHb")
            for oc in range(NFC):
                cbuf = s.p16(s.AT_CONV + (oc % 2) * 2048, 2048)
                for tap in range(4):
                    wcol = s.CONVW[:, oc * 12 + pidx * 4 + tap:
                                   oc * 12 + pidx * 4 + tap + 1]
                    if tap == 0:
                        p.op(VEC, lambda e, cbuf=cbuf, oc=oc, w=wcol:
                             e.tensor_scalar(cbuf, pre[:, oc * 2052:
                                                       oc * 2052 + 2048],
                                             w, None, ALU.mult),
                             waits=[("dHb", dhalo),
                                    ("dSp%d" % (oc % 2), conv_free[oc % 2])],
                             inc="Vm")
                    else:
                        p.op(VEC, lambda e, cbuf=cbuf, oc=oc, tap=tap, w=wcol:
                             e.scalar_tensor_tensor(
                                 cbuf, pre[:, oc * 2052 + tap:
                                           oc * 2052 + 2048 + tap], w,
                                 cbuf, ALU.mult, ALU.add), inc="Vm")
                p.op(SCA, lambda e, cbuf=cbuf: e.activation(
                    cbuf, cbuf, AF.Silu), waits=[("Vm", p.val("Vm"))], inc="Sm")
                sv = p.val("Sm")
                p.dma(SYN, spill.rearrange("n (f p) c -> p n f c", p=128)
                      [:, :, oc, :], cbuf.rearrange("p (n c) -> p n c", c=CH),
                      waits=[("Sm", sv)], inc="dSp%d" % (oc % 2))
                if pname in ("q", "k"):
                    sq = s.p16(s.AT_SQ, 2048)
                    p.op(VEC, lambda e, cbuf=cbuf, sq=sq: e.tensor_tensor(
                        sq, cbuf, cbuf, ALU.mult),
                        waits=[("Sm", sv), ("Tm", s._sq_free)], inc="Vm")
                    vsq = p.val("Vm")
                    r0 = sumsq_rows[pname]
                    for j in range(NTT):
                        p.op(TEN, lambda e, sq=sq, j=j, oc=oc: e.matmul(
                            s.PF[0:16, 0:512],
                            s.SELSUM[:, oc * 16:(oc + 1) * 16],
                            sq[:, j * 512:(j + 1) * 512],
                            start=True, stop=True),
                            waits=[("Vm", vsq), ("Vm", s._ssq_v)], inc="Tm")
                        s._sq_free = p.val("Tm")
                        tms = p.val("Tm")
                        p.op(VEC, lambda e, r0=r0, j=j: e.tensor_tensor(
                            s.T3[r0:r0 + 16, j * 512:(j + 1) * 512],
                            s.T3[r0:r0 + 16, j * 512:(j + 1) * 512],
                            s.PF[0:16, 0:512], ALU.add),
                            waits=[("Tm", tms)], inc="Vm")
                        s._ssq_v = p.val("Vm")
                conv_free[oc % 2] = p.val("dSp%d" % (oc % 2))
            wfree[slot] = p.val("Tm")
        s.v_proj = p.val("Vm")

        wab_sb = s.p16(s.AT_W, 256)
        p.dma(GPS, wab_sb[:, 0:128].rearrange("p (kc c) -> p kc c", c=16),
              s.wab.rearrange("(kc p) c -> p kc c", p=128),
              waits=[("dW", s.wcast_at["wa"]), ("Tm", wfree[0])], inc="dUp0")
        p.dma(GPS, wab_sb[:, 128:256].rearrange("p (kc c) -> p kc c", c=16),
              s.wbb.rearrange("(kc p) c -> p kc c", p=128),
              waits=[("dW", s.wcast_at["wb"])], inc="dUp0")
        dab = p.val("dUp0")
        for ab in range(2):
            dst = s.T1[0:16, :] if ab == 0 else s.T1[32:48, :]
            for j in range(NTT):
                q_ = s.PE[0:16, 0:512]
                for kc in range(NFC):
                    xs = s.Xb[:, kc * TC + j * 512: kc * TC + j * 512 + 512]
                    p.op(TEN, lambda e, q=q_, kc=kc, ab=ab, xs=xs: e.matmul(
                        q, wab_sb[:, ab * 128 + kc * 16: ab * 128 + kc * 16 + 16],
                        xs, start=(kc == 0), stop=False),
                        waits=([("dUp0", dab), ("Vm", s._ab_v)] if kc == 0 else ()))
                p.op(TEN, lambda e, q=q_, ab=ab, j=j: e.matmul(
                    q, s.WSUMS[64:65, 4 * DM + ab * 16: 4 * DM + ab * 16 + 16],
                    s.MROWB[64:65, j * 512:(j + 1) * 512],
                    start=False, stop=True), inc="Tm")
                tm = p.val("Tm")
                p.op(VEC, lambda e, j=j: e.tensor_tensor(
                    s.T3[96:112, j * 512:(j + 1) * 512], q_,
                    s.RB[0:16, j * 512:(j + 1) * 512], ALU.mult),
                    waits=[("Tm", tm), ("Sm", s._ab_s)], inc="Vm")
                s._ab_v = p.val("Vm")
                vv = p.val("Vm")
                p.op(SCA, lambda e, dst=dst, j=j, ab=ab: e.activation(
                    dst[:, j * 512:(j + 1) * 512],
                    s.T3[96:112, j * 512:(j + 1) * 512], AF.Sigmoid,
                    bias=s.BAB[:, ab:ab + 1]), waits=[("Vm", vv)], inc="Sm")
                s._ab_s = p.val("Sm")
        s.s_ab = p.val("Sm")
        s._ab_tm = p.val("Tm")

        V = lambda fn, w=(): p.op(VEC, fn, waits=w, inc="Vm")
        S_ = lambda fn, w=(): p.op(SCA, fn, waits=w, inc="Sm")
        S_(lambda e: e.activation(s.T3[96:112, :], s.T1[0:16, :], AF.Ln),
           w=[("Sm", s.s_ab), ("Vm", p.val("Vm"))])
        sv = p.val("Sm")
        V(lambda e: e.tensor_scalar(s.T3[96:112, :], s.T3[96:112, :], -1.0,
                                    None, ALU.mult), w=[("Sm", sv)])
        for j_ in range(NTT):
            ini = (0.0 if j_ == 0 else
                   s.VSTK[96:112, j_ * 512 - 1: j_ * 512])
            V(lambda e, j_=j_, ini=ini: e.tensor_tensor_scan(
                s.VSTK[96:112, j_ * 512:(j_ + 1) * 512],
                s.T3[96:112, j_ * 512:(j_ + 1) * 512],
                s.T1[96:112, 0:512], ini, ALU.add, ALU.add),
              w=[("dT", s.c_consts)])
        S_(lambda e: e.activation(s.T3[32:48, :], s.T3[32:48, :], AF.Sqrt),
           w=[("Vm", s.v_proj)])
        sv = p.val("Sm")
        V(lambda e: e.tensor_scalar(s.T3[32:48, :], s.T3[32:48, :], 1e-6,
                                    None, ALU.add), w=[("Sm", sv)])
        S_(lambda e: e.activation(s.T3[32:48, :], s.T3[32:48, :], AF.Ln),
           w=[("Vm", p.val("Vm"))])
        sv = p.val("Sm")
        V(lambda e: e.tensor_scalar(s.VSTK[32:48, :], s.VSTK[96:112, :], 1.0,
                                    None, ALU.mult), w=[("Sm", sv)])
        V(lambda e: e.tensor_tensor(s.VSTK[64:80, :], s.T3[32:48, :],
                                    s.VSTK[32:48, :], ALU.add))
        V(lambda e: e.tensor_scalar(s.VSTK[64:80, :], s.VSTK[64:80, :], -1.0,
                                    None, ALU.mult))
        S_(lambda e: e.activation(s.T3[64:80, :], s.T3[64:80, :], AF.Sqrt),
           w=[("Vm", s.v_proj)])
        sv = p.val("Sm")
        V(lambda e: e.tensor_scalar(s.T3[64:80, :], s.T3[64:80, :], 1e-6,
                                    None, ALU.add), w=[("Sm", sv)])
        V(lambda e: e.reciprocal(s.T3[0:16, :], s.T3[64:80, :]))
        vv = p.val("Vm")
        S_(lambda e: e.activation(s.T3[64:80, :], s.T3[64:80, :], AF.Ln),
           w=[("Vm", vv)])
        s_lnsk = p.val("Sm")
        V(lambda e: e.tensor_scalar(s.T3[96:112, :], s.T3[64:80, :], 1.0,
                                    None, ALU.mult), w=[("Sm", s_lnsk)])
        V(lambda e: e.tensor_tensor(s.VSTK[0:16, :], s.VSTK[96:112, :],
                                    s.T3[96:112, :], ALU.subtract))
        S_(lambda e: e.activation(s.T1[96:112, :], s.T1[32:48, :], AF.Ln),
           w=[("Vm", p.val("Vm"))])
        s_lnb = p.val("Sm")
        V(lambda e: e.tensor_tensor(s.VSTK[32:48, :], s.T1[96:112, :],
                                    s.T3[96:112, :], ALU.subtract),
          w=[("Sm", s_lnb)])
        V(lambda e: e.tensor_scalar(s.T3[32:48, :], s.VSTK[96:112, :], 1.0,
                                    None, ALU.mult))
        V(lambda e: e.tensor_tensor(s.VSTK[32:48, :], s.VSTK[32:48, :],
                                    s.T3[32:48, :], ALU.subtract))
        s.v_rows = p.val("Vm")
        s.s_rows = p.val("Sm")
        s._qs_sp0 = p.val("dSp0")
        s._qs_sp1 = p.val("dSp1")

    def finish(self):
        p = self.P
        p.barrier_wait(SYN, "dT", p.val("dT"))
        p.barrier_wait(SYN, "dWs", p.val("dWs"))
        p.barrier_wait(SYN, "dUp0", p.val("dUp0"))
        p.barrier_wait(SYN, "dUp1", p.val("dUp1"))
        p.barrier_wait(GPS, "dG2", p.val("dG2"))
        p.barrier_wait(SYN, "dDn", p.val("dDn"))
        p.barrier_wait(SYN, "dX", p.val("dX"))
        p.barrier_wait(SYN, "dHs", p.val("dHs"))
        p.barrier_wait(SYN, "dHb", p.val("dHb"))
        p.barrier_wait(SYN, "dX10", p.val("dX10"))
        p.barrier_wait(SYN, "dX11", p.val("dX11"))
        p.barrier_wait(SYN, "dF0", p.val("dF0"))
        p.barrier_wait(SYN, "dF1", p.val("dF1"))
        p.barrier_wait(SYN, "dSp0", p.val("dSp0"))
        p.barrier_wait(SYN, "dSp1", p.val("dSp1"))
        p.barrier_wait(SYN, "dSl0", p.val("dSl0"))
        p.barrier_wait(SYN, "dSl1", p.val("dSl1"))
        p.barrier_wait(SYN, "dF0", p.val("dF0"))
        p.barrier_wait(SYN, "dF1", p.val("dF1"))
        p.barrier_wait(SYN, "dX10", p.val("dX10"))
        p.barrier_wait(SYN, "dX11", p.val("dX11"))
        p.barrier_wait(SYN, "dZ0", p.val("dZ0"))
        p.barrier_wait(SYN, "dZ1", p.val("dZ1"))
        p.barrier_wait(SYN, "dMc0", p.val("dMc0"))
        p.barrier_wait(SYN, "dMc1", p.val("dMc1"))
        p.barrier_wait(SYN, "dQ20", p.val("dQ20"))
        p.barrier_wait(SYN, "dQ21", p.val("dQ21"))
        p.barrier_wait(SYN, "dO10", p.val("dO10"))
        p.barrier_wait(SYN, "dO11", p.val("dO11"))
        p.barrier_wait(GPS, "dG", p.val("dG"))
        p.barrier_wait(GPS, "Gm", p.val("Gm"))
        p.barrier_wait(GPS, "dW", p.val("dW"))
        p.barrier_wait(TEN, "Tm", p.val("Tm"))
        p.barrier_wait(VEC, "Vm", p.val("Vm"))
        p.barrier_wait(SCA, "Sm", p.val("Sm"))

    def build(self):
        with self.nc.Block() as block:
            self.P.replay(block)
        self.st.close()
        return self.nc


    SC_SLAB = 0
    SC_WS = 6144
    SC_KBAR = 14336
    SC_MC = 14848
    SC_Q2 = 15360
    SC_Z = 15872
    SC_O1 = 16896

    def emit_scan(self):
        s, p = self, self.P
        p.op(VEC, lambda e: e.memset(s.SLOC[:, :], 0.0), inc="Vm")
        for h in range(NH):
            p.op(VEC, lambda e, h=h: e.tensor_scalar(
                s.MCH[:, h * HD:(h + 1) * HD], s.IDENTB[0:64, 0:64], 1.0,
                None, ALU.mult), waits=[("dT", s.c_consts)], inc="Vm")
        slab_free = [s._ab_tm, s._ab_tm]
        dslab = [0, 0]
        ws_free = [s.v_rows, s.v_rows]
        stg_free = 0
        gc_free = 0
        v_chain = p.val("Vm")
        t_chain = 0
        pe_free = s._ab_v
        pf_free = 0
        pg_free = 0
        pt_free = 0
        pbig_free = [p.val("Vm")] * 4
        spl = {"z": [0, 0], "mc": [0, 0], "q2": [0, 0], "o1": [0, 0]}
        for c in range(NCH):
            cs = slice(c * CH, (c + 1) * CH)
            par = c % 2
            slab = s.p16(s.SC_SLAB + par * 3072, 3072)
            for i_, src in enumerate((s.qhat_s, s.ksil_s, s.vsil_s)):
                p.dma(SYN, slab[:, i_ * 1024:(i_ + 1) * 1024]
                      .rearrange("p (f c) -> p f c", c=CH),
                      src[c].rearrange("(f p) c -> p f c", p=128),
                      waits=[("dSp0", s._qs_sp0), ("dSp1", s._qs_sp1),
                             ("Tm", slab_free[par])],
                      inc="dSl%d" % par)
            dslab[par] = p.val("dSl%d" % par)
            p.dma(SYN, s.STGF[0:1, 0:2048].rearrange("o (u c) -> o u c", c=CH),
                  s.VSTK[32:48, cs], waits=[("Vm", s.v_rows),
                                            ("Tm", stg_free)], inc="dT")
            p.dma(SYN, s.STATS[0:1, 0:2048].rearrange("o (u c) -> o u c", c=CH),
                  s.VSTK[64:80, cs], inc="dT")
            if c == 0:
                p.dma(SYN, s.STGF[64:65, 0:16], s.ZROW[:, 0:1], inc="dT")
            else:
                p.dma(SYN, s.STGF[64:65, 0:16],
                      s.VSTK[96:112, c * CH - 1: c * CH], inc="dT")
            p.dma(SYN, s.STGF[64:65, 16:32],
                  s.VSTK[96:112, c * CH + CH - 1: c * CH + CH], inc="dT")
            dstg = p.val("dT")
            CT = s.p32(0, 128)
            CT2 = s.p32(128, 128)
            p.op(TEN, lambda e, cs=cs: e.matmul(
                s.PF[0:128, 0:128], s.VSTK[:, cs], s.IDENT[:, :],
                is_transpose=True, start=True, stop=True),
                waits=[("Vm", s.v_rows), ("Vm", gc_free), ("Vm", pf_free)],
                inc="Tm")
            tm = p.val("Tm")
            p.op(VEC, lambda e, CT=CT: e.tensor_scalar(
                CT, s.PF[0:128, 0:128], 1.0, None, ALU.mult),
                waits=[("Tm", tm)], inc="Vm")
            p.op(TEN, lambda e, cs=cs: e.matmul(
                s.PF[0:128, 0:128], s.T1[:, cs], s.IDENT[:, :],
                is_transpose=True, start=True, stop=True),
                waits=[("Vm", p.val("Vm"))], inc="Tm")
            tm = p.val("Tm")
            p.op(VEC, lambda e, CT2=CT2: e.tensor_scalar(
                CT2, s.PF[0:128, 0:128], 1.0, None, ALU.mult),
                waits=[("Tm", tm)], inc="Vm")
            pf_free = p.val("Vm")
            vct = p.val("Vm")
            p.op(TEN, lambda e: e.matmul(
                s.PE[0:128, 0:16], s.ONES4[64:65, 0:128], s.STGF[64:65, 0:16],
                start=True, stop=True),
                waits=[("dT", dstg), ("Vm", pe_free)], inc="Tm")
            p.op(TEN, lambda e: e.matmul(
                s.PE[0:128, 16:32], s.ONES4[64:65, 0:128],
                s.STGF[64:65, 16:32], start=True, stop=True), inc="Tm")
            tm = p.val("Tm")
            GC = s.GCOLS
            p.op(VEC, lambda e, CT=CT: e.tensor_tensor(
                GC[:, 0:16], s.PE[0:128, 0:16], CT[:, 96:112], ALU.subtract),
                waits=[("Tm", tm), ("Vm", vct), ("Vm", gc_free)], inc="Vm")
            p.op(VEC, lambda e, CT=CT: e.tensor_tensor(
                GC[:, 16:32], CT[:, 64:80], s.PE[0:128, 0:16], ALU.add),
                inc="Vm")
            p.op(VEC, lambda e, CT=CT: e.tensor_tensor(
                GC[:, 32:48], CT[:, 0:16], s.PE[0:128, 16:32], ALU.subtract),
                inc="Vm")
            p.op(VEC, lambda e, CT=CT: e.tensor_tensor(
                GC[:, 48:64], CT[:, 32:48], s.PE[0:128, 0:16], ALU.add),
                inc="Vm")
            vv = p.val("Vm")
            p.op(SCA, lambda e, GC=GC: e.activation(GC[:, :], GC[:, :], AF.Exp),
                 waits=[("Vm", vv)], inc="Sm")
            sgc = p.val("Sm")
            p.op(VEC, lambda e: e.tensor_tensor(
                s.STGF[64:65, 32:48], s.STGF[64:65, 0:16],
                s.STGF[64:65, 16:32], ALU.subtract),
                waits=[("dT", dstg)], inc="Vm")
            vv = p.val("Vm")
            p.op(SCA, lambda e: e.activation(
                s.STGF[64:65, 32:48], s.STGF[64:65, 32:48], AF.Exp),
                waits=[("Vm", vv)], inc="Sm")
            sv = p.val("Sm")
            p.op(TEN, lambda e: e.matmul(
                s.PE[0:64, 32:48], s.ONES4[64:65, 0:64], s.STGF[64:65, 32:48],
                start=True, stop=True), waits=[("Sm", sv)], inc="Tm")
            tm = p.val("Tm")
            p.op(VEC, lambda e: e.tensor_scalar(
                s.GCREP[:, :], s.PE[0:64, 32:48], 1.0, None, ALU.mult),
                waits=[("Tm", tm)], inc="Vm")
            pe_free = p.val("Vm")
            v_cols = p.val("Vm")
            for hq in range(4):
                wpar = (c * 4 + hq) % 2
                ws = s.p16(s.SC_WS + wpar * 4096, 4096)
                W1A = ws[:, 0:1024]
                PB_ = ws[:, 1024:2048]
                QB_ = ws[:, 2048:3072]
                RB_ = ws[:, 3072:3584]
                heads = [4 * hq + i for i in range(4)]
                stage_lim = getattr(self, "scan_stage", 99)
                if stage_lim == 0:
                    continue

                def hslabs(h):
                    base = 64 * (h % 2)
                    fc = h // 2
                    qsl = slab[base:base + 64, fc * 128: fc * 128 + 128]
                    ksl = slab[base:base + 64, 1024 + fc * 128:
                               1024 + fc * 128 + 128]
                    vsl = slab[base:base + 64, 2048 + fc * 128:
                               2048 + fc * 128 + 128]
                    return qsl, ksl, vsl

                for i, h in enumerate(heads):
                    qsl, ksl, _ = hslabs(h)
                    qq = s.PBIG[:, i * 512:(i + 1) * 512]
                    p.op(TEN, lambda e, qq=qq, ksl=ksl: e.matmul(
                        qq[:, 0:128], ksl, ksl, start=True, stop=True),
                        waits=[("dSl%d" % par, dslab[par]),
                               ("Vm", pbig_free[i])])
                    if stage_lim == 10:
                        p.op(TEN, lambda e, qq=qq, ksl=ksl, qsl=qsl: e.matmul(
                            qq[:, 128:256], ksl, qsl, start=True, stop=True),
                            inc="Tm")
                        continue
                    p.op(TEN, lambda e, qq=qq, ksl=ksl, qsl=qsl: e.matmul(
                        qq[:, 128:256], ksl, qsl, start=True, stop=True))
                    p.op(TEN, lambda e, qq=qq, h=h: e.matmul(
                        qq[:, 256:384], s.ONES4[0:1, 0:128],
                        s.STGF[0:1, h * 128: h * 128 + 128],
                        start=True, stop=True), waits=[("dT", dstg)])
                    p.op(TEN, lambda e, qq=qq, h=h: e.matmul(
                        qq[:, 384:512], s.ONES4[0:1, 0:128],
                        s.STATS[0:1, h * 128: h * 128 + 128],
                        start=True, stop=True), inc="Tm")
                stg_free = p.val("Tm")
                tgram = p.val("Tm")
                if stage_lim in (10, 11):
                    continue
                EArg = s.p32(256, 1024)
                for i, h in enumerate(heads):
                    p.op(VEC, lambda e, i=i, h=h, CT=CT: e.tensor_scalar(
                        EArg[:, i * 256:(i + 1) * 256],
                        s.PBIG[:, i * 512 + 256: i * 512 + 512],
                        CT[:, h:h + 1], None, ALU.add),
                        waits=[("Tm", tgram), ("Sm", sgc)], inc="Vm")
                p.op(VEC, lambda e, EArg=EArg: e.tensor_tensor(
                    EArg.rearrange("p (i c) -> p i c", c=256),
                    EArg.rearrange("p (i c) -> p i c", c=256),
                    s.TRI2.rearrange("p (o c) -> p o c", o=1)
                    .broadcast_to([128, 4, 256]), ALU.add), inc="Vm")
                vv = p.val("Vm")
                p.op(SCA, lambda e, EArg=EArg: e.activation(
                    EArg, EArg, AF.Exp), waits=[("Vm", vv)], inc="Sm")
                se = p.val("Sm")
                if stage_lim == 12:
                    continue
                p.op(VEC, lambda e, W1A=W1A, EArg=EArg: e.tensor_tensor(
                    W1A.rearrange("p (i c) -> p i c", c=256),
                    s.PBIG.rearrange("p (i c) -> p i c", c=512)[:, :, 0:256],
                    EArg.rearrange("p (i c) -> p i c", c=256), ALU.mult),
                    waits=[("Sm", se), ("Vm", ws_free[wpar])], inc="Vm")
                vW1A = p.val("Vm")
                pbig_free = [p.val("Vm")] * 4
                if stage_lim == 1:
                    continue
                for f2 in range(2):
                    fc = heads[2 * f2] // 2
                    vfull = slab[0:128, 2048 + fc * 128: 2048 + fc * 128 + 128]
                    kfull = slab[0:128, 1024 + fc * 128: 1024 + fc * 128 + 128]
                    p.op(TEN, lambda e, f2=f2, vfull=vfull: e.matmul(
                        s.PT[:, f2 * 256: f2 * 256 + 128], vfull,
                        s.IDENTB[:, :], is_transpose=True, start=True,
                        stop=True),
                        waits=([("Vm", pt_free)] if f2 == 0 else ()))
                    p.op(TEN, lambda e, f2=f2, kfull=kfull: e.matmul(
                        s.PT[:, f2 * 256 + 128: f2 * 256 + 256], kfull,
                        s.IDENTB[:, :], is_transpose=True, start=True,
                        stop=True),
                        inc=("Tm" if f2 == 1 else None))
                tpt = p.val("Tm")
                kbar = s.p16(s.SC_KBAR + wpar * 256, 256)
                for i, h in enumerate(heads):
                    vcol = (i // 2) * 256 + (h % 2) * 64
                    kcol = (i // 2) * 256 + 128 + (h % 2) * 64
                    p.op(VEC, lambda e, i=i, h=h, RB_=RB_, CT2=CT2, vcol=vcol:
                         e.tensor_scalar(
                             RB_[:, i * 128: i * 128 + 64],
                             s.PT[:, vcol: vcol + 64],
                             CT2[:, 32 + h: 33 + h], None, ALU.mult),
                         waits=[("Tm", tpt)] if i == 0 else (), inc="Vm")
                    p.op(VEC, lambda e, i=i, h=h, RB_=RB_, kcol=kcol:
                         e.tensor_scalar(
                             RB_[:, i * 128 + 64: i * 128 + 128],
                             s.PT[:, kcol: kcol + 64],
                             s.GCOLS[:, 48 + h: 49 + h], -1.0, ALU.mult,
                             ALU.mult),
                         inc="Vm")
                    p.op(VEC, lambda e, i=i, h=h, kbar=kbar, kcol=kcol:
                         e.tensor_scalar(
                             kbar[:, i * 64:(i + 1) * 64],
                             s.PT[:, kcol: kcol + 64],
                             s.GCOLS[:, 32 + h: 33 + h], None, ALU.mult),
                         inc="Vm")
                pt_free = p.val("Vm")
                vR0 = p.val("Vm")
                for i in range(4):
                    p.op(TEN, lambda e, i=i, W1A=W1A: e.matmul(
                        s.PT[:, i * 128:(i + 1) * 128],
                        W1A[:, i * 256: i * 256 + 128], s.IDENTB[:, :],
                        is_transpose=True, start=True, stop=True),
                        waits=([("Vm", vW1A), ("Vm", pt_free)] if i == 0
                               else ()),
                        inc=("Tm" if i == 3 else None))
                tq0 = p.val("Tm")
                p.op(VEC, lambda e, QB_=QB_: e.tensor_scalar(
                    QB_[:, 0:512], s.PT[:, 0:1024:2][:, 0:512] if False else
                    s.PT[:, 0:512], 1.0, None, ALU.mult),
                    waits=[("Tm", tq0)], inc="Vm")
                pt_free = p.val("Vm")
                vq0 = p.val("Vm")
                if stage_lim == 2:
                    continue
                vr = vR0
                vpq = vq0
                sq_wait = 0
                pg_free_sm = 0
                for j in range(7):
                    for i in range(4):
                        lsl = (W1A[:, i * 256: i * 256 + 128] if j == 0 else
                               PB_[:, (j % 2) * 512 + i * 128:
                                   (j % 2) * 512 + i * 128 + 128])
                        p.op(TEN, lambda e, i=i, lsl=lsl, RB_=RB_: e.matmul(
                            s.PE[:, i * 128:(i + 1) * 128], lsl,
                            RB_[:, i * 128:(i + 1) * 128],
                            start=True, stop=True),
                            waits=([("Vm", vr), ("Vm", vpq),
                                    ("Vm", pe_free)] if i == 0 else ()),
                            inc=("Tm" if i == 3 else None))
                    ta = p.val("Tm")
                    op_ = ALU.subtract if j == 0 else ALU.add
                    p.op(VEC, lambda e, RB_=RB_, op_=op_: e.tensor_tensor(
                        RB_, RB_, s.PE[:, 0:512], op_),
                        waits=[("Tm", ta)], inc="Vm")
                    vr = p.val("Vm")
                    pe_free = vr
                    if j < 6:
                        for i in range(4):
                            psl = (W1A[:, i * 256: i * 256 + 128] if j == 0
                                   else PB_[:, (j % 2) * 512 + i * 128:
                                            (j % 2) * 512 + i * 128 + 128])
                            qsl2 = QB_[:, (j % 2) * 512 + i * 128:
                                       (j % 2) * 512 + i * 128 + 128]
                            p.op(TEN, lambda e, i=i, psl=psl, qsl2=qsl2:
                                 e.matmul(s.PF[:, i * 128:(i + 1) * 128],
                                          qsl2, psl, start=True, stop=True),
                                 waits=([("Vm", pf_free), ("Sm", sq_wait)]
                                        if i == 0 else ()),
                                 inc=("Tm" if (i == 3 and j == 5) else None))
                            if j < 5:
                                p.op(TEN, lambda e, i=i, psl=psl, qsl2=qsl2:
                                     e.matmul(s.PG[:, i * 128:(i + 1) * 128],
                                              psl, qsl2, start=True,
                                              stop=True),
                                     waits=([("Vm", pg_free)] if i == 0
                                            else ()),
                                     inc=("Tm" if i == 3 else None))
                        tsq = p.val("Tm")
                        np_ = PB_[:, ((j + 1) % 2) * 512:
                                  ((j + 1) % 2) * 512 + 512]
                        p.op(VEC, lambda e, np_=np_: e.tensor_scalar(
                            np_, s.PF[:, 0:512], 1.0, None, ALU.mult),
                            waits=[("Tm", tsq)], inc="Vm")
                        pf_free = p.val("Vm")
                        vpq = p.val("Vm")
                        if j < 5:
                            nq_ = QB_[:, ((j + 1) % 2) * 512:
                                      ((j + 1) % 2) * 512 + 512]
                            p.op(SCA, lambda e, nq_=nq_: e.activation(
                                nq_, s.PG[:, 0:512], AF.Identity),
                                waits=[("Tm", tsq)], inc="Sm")
                            sq_wait = p.val("Sm")
                            pg_free_sm = sq_wait
                if stage_lim == 3:
                    continue
                for f2 in range(2):
                    fc = heads[2 * f2] // 2
                    qfull = slab[0:128, fc * 128: fc * 128 + 128]
                    p.op(TEN, lambda e, f2=f2, qfull=qfull: e.matmul(
                        s.PT[:, f2 * 128:(f2 + 1) * 128], qfull,
                        s.IDENTB[:, :], is_transpose=True, start=True,
                        stop=True),
                        waits=([("Vm", pt_free)] if f2 == 0 else ()),
                        inc=("Tm" if f2 == 1 else None))
                tqz = p.val("Tm")
                GQ = s.p32(1280, 256)
                for i, h in enumerate(heads):
                    qcol = (i // 2) * 128 + (h % 2) * 64
                    p.op(VEC, lambda e, i=i, h=h, GQ=GQ, qcol=qcol:
                         e.tensor_scalar(
                             GQ[:, i * 64:(i + 1) * 64],
                             s.PT[:, qcol: qcol + 64],
                             s.GCOLS[:, 16 + h: 17 + h], None, ALU.mult),
                         waits=([("Tm", tqz)] if i == 0 else ()), inc="Vm")
                pt_free = p.val("Vm")
                vgq = p.val("Vm")
                for i in range(4):
                    p.op(TEN, lambda e, i=i, GQ=GQ: e.matmul(
                        s.PG[0:64, i * 128:(i + 1) * 128], GQ[:, i * 64:(i + 1) * 64],
                        s.IDENT[:, :], is_transpose=True, start=True,
                        stop=False), waits=([("Vm", vgq), ("Vm", pg_free),
                                             ("Sm", pg_free_sm)]
                                            if i == 0 else ()))
                    p.op(TEN, lambda e, i=i, RB_=RB_, W1A=W1A: e.matmul(
                        s.PG[0:64, i * 128:(i + 1) * 128],
                        RB_[:, i * 128 + 64: i * 128 + 128],
                        W1A[:, i * 256 + 128: i * 256 + 256],
                        start=False, stop=True),
                        inc=("Tm" if i == 3 else None))
                tz = p.val("Tm")
                zb = s.p16(s.SC_Z + wpar * 512, 512)
                p.op(VEC, lambda e, zb=zb: e.tensor_scalar(
                    zb[0:64, :], s.PG[0:64, 0:512], 1.0, None, ALU.mult),
                    waits=[("Tm", tz), ("dZ%d" % wpar, spl["z"][wpar])],
                    inc="Vm")
                vz = p.val("Vm")
                pg_free = vz
                p.dma(SYN, s.z_s[c, 4 * hq:4 * hq + 4]
                      .rearrange("i p c -> p i c"),
                      zb[0:64, :].rearrange("p (i c) -> p i c", c=CH),
                      waits=[("Vm", vz)], inc="dZ%d" % wpar)
                spl["z"][wpar] = p.val("dZ%d" % wpar)
                if stage_lim == 4:
                    continue
                for i in range(4):
                    p.op(TEN, lambda e, i=i, RB_=RB_, kbar=kbar: e.matmul(
                        s.PE[0:64, i * 64 + 0:(i + 1) * 64],
                        RB_[:, i * 128 + 64: i * 128 + 128],
                        kbar[:, i * 64:(i + 1) * 64],
                        start=True, stop=True),
                        waits=([("Vm", pe_free)] if i == 0 else ()))
                    p.op(TEN, lambda e, i=i, RB_=RB_, kbar=kbar: e.matmul(
                        s.PE[0:64, 256 + i * 64: 256 + (i + 1) * 64],
                        kbar[:, i * 64:(i + 1) * 64],
                        RB_[:, i * 128: i * 128 + 64],
                        start=True, stop=True),
                        inc=("Tm" if i == 3 else None))
                tmc = p.val("Tm")
                mcb = s.p16(s.SC_MC + wpar * 256, 256)
                q2b = s.p16(s.SC_Q2 + wpar * 256, 256)
                for i, h in enumerate(heads):
                    p.op(VEC, lambda e, i=i, h=h, mcb=mcb:
                         e.scalar_tensor_tensor(
                             mcb[0:64, i * 64:(i + 1) * 64],
                             s.IDENTB[0:64, 0:64], s.GCREP[:, h:h + 1],
                             s.PE[0:64, i * 64:(i + 1) * 64],
                             ALU.mult, ALU.add),
                         waits=([("Tm", tmc), ("Vm", v_cols),
                                 ("dMc%d" % wpar, spl["mc"][wpar])]
                                if i == 0 else ()),
                         inc="Vm")
                p.op(VEC, lambda e, q2b=q2b: e.tensor_scalar(
                    q2b[0:64, :], s.PE[0:64, 256:512], 1.0, None, ALU.mult),
                    waits=[("dQ2%d" % wpar, spl["q2"][wpar])], inc="Vm")
                vmc = p.val("Vm")
                pe_free = vmc
                p.dma(SYN, s.mc_s[c, 4 * hq:4 * hq + 4]
                      .rearrange("i p c -> p i c"),
                      mcb[0:64, :].rearrange("p (i c) -> p i c", c=HD),
                      waits=[("Vm", vmc)], inc="dMc%d" % wpar)
                spl["mc"][wpar] = p.val("dMc%d" % wpar)
                p.dma(SYN, s.q2_s[c, 4 * hq:4 * hq + 4]
                      .rearrange("i p c -> p i c"),
                      q2b[0:64, :].rearrange("p (i c) -> p i c", c=HD),
                      waits=[("Vm", vmc)], inc="dQ2%d" % wpar)
                spl["q2"][wpar] = p.val("dQ2%d" % wpar)
                if stage_lim == 5:
                    continue
                for i in range(4):
                    p.op(TEN, lambda e, i=i, W1A=W1A, RB_=RB_: e.matmul(
                        s.PF[:, i * 64: i * 64 + 64],
                        W1A[:, i * 256 + 128: i * 256 + 256],
                        RB_[:, i * 128: i * 128 + 64],
                        start=True, stop=True),
                        waits=([("Vm", pf_free)] if i == 0 else ()),
                        inc=("Tm" if i == 3 else None))
                to1 = p.val("Tm")
                o1b = s.p16(s.SC_O1 + wpar * 1024, 1024)
                p.op(VEC, lambda e, o1b=o1b: e.tensor_scalar(
                    o1b[:, 0:256], s.PF[:, 0:256], 1.0, None, ALU.mult),
                    waits=[("Tm", to1), ("dO1%d" % wpar, spl["o1"][wpar])],
                    inc="Vm")
                vo1 = p.val("Vm")
                pf_free = vo1
                p.dma(SYN, s.o1_s[c, 4 * hq:4 * hq + 4]
                      .rearrange("i p c -> p i c"),
                      o1b[:, 0:256].rearrange("p (i c) -> p i c", c=HD),
                      waits=[("Vm", vo1)], inc="dO1%d" % wpar)
                spl["o1"][wpar] = p.val("dO1%d" % wpar)
                for i, h in enumerate(heads):
                    p.op(TEN, lambda e, i=i, h=h, mcb=mcb: e.matmul(
                        s.PG[0:64, i * 64:(i + 1) * 64],
                        mcb[0:64, i * 64:(i + 1) * 64],
                        s.SLOC[:, h * HD:(h + 1) * HD],
                        start=True, stop=True),
                        waits=([("Vm", vmc), ("Vm", v_chain),
                                ("Vm", pg_free)] if i == 0 else ()))
                    p.op(TEN, lambda e, i=i, h=h, mcb=mcb: e.matmul(
                        s.PG[0:64, 256 + i * 64: 256 + (i + 1) * 64],
                        mcb[0:64, i * 64:(i + 1) * 64],
                        s.MCH[:, h * HD:(h + 1) * HD],
                        start=True, stop=True),
                        inc=("Tm" if i == 3 else None))
                tch = p.val("Tm")
                p.op(VEC, lambda e, q2b=q2b, hq=hq: e.tensor_tensor(
                    s.SLOC[:, 4 * hq * HD: (4 * hq + 4) * HD],
                    q2b[0:64, :], s.PG[0:64, 0:256], ALU.add),
                    waits=[("Tm", tch)], inc="Vm")
                p.op(VEC, lambda e, hq=hq: e.tensor_scalar(
                    s.MCH[:, 4 * hq * HD: (4 * hq + 4) * HD],
                    s.PG[0:64, 256:512], 1.0, None, ALU.mult), inc="Vm")
                v_chain = p.val("Vm")
                pg_free = v_chain
                ws_free[wpar] = p.val("Vm")
            slab_free[par] = p.val("Tm")
            gc_free = p.val("Vm")
        s.t_scan = p.val("Tm")
        s.v_scan = p.val("Vm")
        s.spl = spl

    P2_OFM = 0
    P2_MC = 16384
    P2_Q2 = 18432
    P2_Z = 20480
    P2_O1 = 24576
    P2_FG = 26624
    P2_MT = 30720
    EN_WO = 16384
    EN_OG = 33280
    EN_XWO = 34304
    EN_X1 = 38400

    def emit_fold(self):
        s, p = self, self.P
        p.dma(SYN, s.ag2_in.rearrange("p (h two c) -> p h two c", two=2, c=HD)
              [:, :, 0, :], s.MCH[:, :].rearrange("p (h c) -> p h c", c=HD),
              waits=[("Vm", s.v_scan)], inc="dT")
        p.dma(SYN, s.ag2_in.rearrange("p (h two c) -> p h two c", two=2, c=HD)
              [:, :, 1, :], s.SLOC[:, :].rearrange("p (h c) -> p h c", c=HD),
              inc="dT")
        d2 = p.val("dT")
        p.op(GPS, lambda e: e.collective_compute(
            "AllGather", ALU.bypass, replica_groups=[list(range(self.ncores))],
            ins=[s.ag2_in.opt()], outs=[s.ag2_out.opt()]),
            waits=[("dT", d2), ("cc", s._cc_done)], inc="cc")
        s._cc_done = p.val("cc")
        p.op(VEC, lambda e: e.memset(s.SS[:, :], 0.0), inc="Vm")
        fg_d = [0, 0]
        fg_free = [0, 0]
        fg_vfree = [0, 0]
        v_ss = p.val("Vm")
        pe_free = 0
        mt_tmfree = 0
        for j in range(min(7, self.ncores - 1) if self.ncores > 1 else 0):
            par = j % 2
            fg = s.p16(s.P2_FG + par * 2048, 2048)
            p.dma(SYN, fg[0:64, :].rearrange("p (h c) -> p h c", c=2 * HD),
                  s.ag2_out[j * HD:(j + 1) * HD, :]
                  .rearrange("p (h c) -> p h c", c=2 * HD),
                  waits=[("cc", s._cc_done), ("Tm", fg_free[par]),
                         ("Vm", fg_vfree[par])],
                  inc="dF%d" % par)
            fg_d[par] = p.val("dF%d" % par)
            mt = s.p16(s.P2_MT, 1024)
            p.op(VEC, lambda e, fg=fg, j=j: e.scalar_tensor_tensor(
                mt[0:64, :].rearrange("p (h c) -> p h c", c=HD),
                fg[0:64, :].rearrange("p (h c) -> p h c", c=2 * HD)[:, :, 0:HD],
                s.FOLDA[:, j:j + 1],
                s.FOLDB.rearrange("p (j c) -> p j c", c=HD)
                [:, j:j + 1, :].broadcast_to([64, NH, HD]),
                ALU.mult, ALU.add),
                waits=[("dF%d" % par, fg_d[par]), ("Tm", mt_tmfree)],
                inc="Vm")
            vmt = p.val("Vm")
            for h in range(NH):
                p.op(TEN, lambda e, h=h, mt=mt: e.matmul(
                    s.PT[0:64, h * 64:(h + 1) * 64],
                    mt[0:64, h * 64:(h + 1) * 64], s.IDENTB[0:64, 0:64],
                    is_transpose=True, start=True, stop=True),
                    waits=([("Vm", vmt)] if h == 0 else ()),
                    inc=("Tm" if h == NH - 1 else None))
            tmt = p.val("Tm")
            mtf = s.p16(s.P2_MT, 1024)
            p.op(VEC, lambda e, mtf=mtf: e.tensor_scalar(
                mtf[0:64, :], s.PT[0:64, 0:1024], 1.0, None, ALU.mult),
                waits=[("Tm", tmt)], inc="Vm")
            vmtf = p.val("Vm")
            for h in range(NH):
                p.op(TEN, lambda e, h=h, mtf=mtf: e.matmul(
                    s.PE[0:64, h * 64:(h + 1) * 64] if h < 8 else
                    s.PF[0:64, (h - 8) * 64:(h - 7) * 64],
                    mtf[0:64, h * 64:(h + 1) * 64],
                    s.SS[:, h * HD:(h + 1) * HD], start=True, stop=True),
                    waits=([("Vm", vmtf), ("Vm", v_ss), ("Vm", pe_free)]
                           if h == 0 else ()),
                    inc=("Tm" if h == NH - 1 else None))
            tss = p.val("Tm")
            p.op(VEC, lambda e, fg=fg, j=j: e.scalar_tensor_tensor(
                s.SS[:, 0:8 * HD].rearrange("p (h c) -> p h c", c=HD),
                fg[0:64, :].rearrange("p (h c) -> p h c", c=2 * HD)
                [:, 0:8, HD:2 * HD],
                s.FOLDA[:, j:j + 1],
                s.PE[0:64, 0:512].rearrange("p (h c) -> p h c", c=HD),
                ALU.mult, ALU.add), waits=[("Tm", tss)], inc="Vm")
            p.op(VEC, lambda e, fg=fg, j=j: e.scalar_tensor_tensor(
                s.SS[:, 8 * HD:].rearrange("p (h c) -> p h c", c=HD),
                fg[0:64, :].rearrange("p (h c) -> p h c", c=2 * HD)
                [:, 8:16, HD:2 * HD],
                s.FOLDA[:, j:j + 1],
                s.PF[0:64, 0:512].rearrange("p (h c) -> p h c", c=HD),
                ALU.mult, ALU.add), inc="Vm")
            v_ss = p.val("Vm")
            pe_free = v_ss
            mt_tmfree = tss
            fg_free[par] = p.val("Tm")
            fg_vfree[par] = p.val("Vm")
        s.v_ss = p.val("Vm")

    def emit_pass2(self):
        s, p = self, self.P
        p.op(VEC, lambda e: e.tensor_scalar(
            s.STRUE[:, :], s.SS[:, :], 1.0, None, ALU.mult),
            waits=[("Vm", s.v_ss)], inc="Vm")
        v_st = p.val("Vm")
        rings = {}
        dring = {"mc": [0, 0], "q2": [0, 0], "z": [0, 0], "o1": [0, 0]}
        rfree = {"mc": [0, 0], "q2": [0, 0], "z": [0, 0], "o1": [0, 0]}
        rvfree = {"mc": [0, 0], "q2": [0, 0], "z": [0, 0], "o1": [0, 0]}
        ofm = s.p16(s.P2_OFM, 16384)
        pe_free = 0; pf_free = 0; pt_free = 0
        v_ofm = 0
        for c in range(NCH):
            par = c % 2
            mcr = s.p16(s.P2_MC + par * 1024, 1024)
            q2r = s.p16(s.P2_Q2 + par * 1024, 1024)
            zr = s.p16(s.P2_Z + par * 2048, 2048)
            o1r = s.p16(s.P2_O1 + par * 1024, 1024)
            for nm, dst, src, w in (("mc", mcr, s.mc_s, HD),
                                    ("q2", q2r, s.q2_s, HD),
                                    ("z", zr, s.z_s, CH),
                                    ("o1", o1r, s.o1_s, HD)):
                p.dma(SYN, dst[0:128 if nm == "o1" else 64, :]
                      .rearrange("p (h c) -> p h c", c=w),
                      src[c].rearrange("h p c -> p h c"),
                      waits=[("Tm", rfree[nm][par]), ("Vm", rvfree[nm][par]),
                             ("dZ0", s.spl["z"][0]), ("dZ1", s.spl["z"][1]),
                             ("dMc0", s.spl["mc"][0]), ("dMc1", s.spl["mc"][1]),
                             ("dQ20", s.spl["q2"][0]), ("dQ21", s.spl["q2"][1]),
                             ("dO10", s.spl["o1"][0]), ("dO11", s.spl["o1"][1])],
                      inc="dSl%d" % par)
            dr = p.val("dSl%d" % par)
            for h in range(NH):
                bank = s.PBIG[:, (h % 4) * 512:(h % 4) * 512 + 512]
                p.op(TEN, lambda e, h=h, zr=zr, bank=bank: e.matmul(
                    bank[:, 0:64], zr[0:64, h * CH:(h + 1) * CH],
                    s.STRUE[:, h * HD:(h + 1) * HD], start=True, stop=True),
                    waits=[("dSl%d" % par, dr), ("Vm", v_st),
                           ("Vm", s._p2big_free[h % 4])], inc="Tm")
                t_o = p.val("Tm")
                otm = s.p32(1536, 64)
                p.op(VEC, lambda e, h=h, o1r=o1r, bank=bank, otm=otm:
                     e.tensor_tensor(otm, bank[:, 0:64],
                                     o1r[:, h * HD:(h + 1) * HD], ALU.add),
                     waits=[("Tm", t_o), ("Tm", s._otm_free)], inc="Vm")
                v_otm = p.val("Vm")
                p.op(TEN, lambda e, otm=otm: e.matmul(
                    s.PF[0:64, 0:128], otm, s.IDENT[:, :],
                    is_transpose=True, start=True, stop=True),
                    waits=[("Vm", v_otm), ("Vm", pf_free)], inc="Tm")
                s._otm_free = p.val("Tm")
                t_t = p.val("Tm")
                fc = h // 2
                rb = 64 * (h % 2)
                p.op(VEC, lambda e, fc=fc, rb=rb, c=c, ofm=ofm: e.tensor_scalar(
                    ofm[rb:rb + 64, fc * TC + c * CH: fc * TC + c * CH + CH],
                    s.PF[0:64, 0:128], 1.0, None, ALU.mult),
                    waits=[("Tm", t_t)], inc="Vm")
                pf_free = p.val("Vm")
                v_ofm = p.val("Vm")
                p.op(TEN, lambda e, h=h, mcr=mcr: e.matmul(
                    bank[:, 64:128][0:64, :] if False else
                    s.PE[0:64, (h % 8) * 64:(h % 8) * 64 + 64],
                    mcr[0:64, h * HD:(h + 1) * HD],
                    s.STRUE[:, h * HD:(h + 1) * HD], start=True, stop=True),
                    waits=[("Vm", pe_free)], inc="Tm")
                t_s = p.val("Tm")
                p.op(VEC, lambda e, h=h, q2r=q2r: e.tensor_tensor(
                    s.STRUE[:, h * HD:(h + 1) * HD],
                    s.PE[0:64, (h % 8) * 64:(h % 8) * 64 + 64],
                    q2r[0:64, h * HD:(h + 1) * HD], ALU.add),
                    waits=[("Tm", t_s)], inc="Vm")
                v_st = p.val("Vm")
                pe_free = v_st
                s._p2big_free[h % 4] = p.val("Vm")
            for nm in rfree:
                rfree[nm][par] = p.val("Tm")
                rvfree[nm][par] = p.val("Vm")
        s.v_ofm = p.val("Vm")
        scr = s.p16(s.P2_MC, 2048)
        for fc in range(NFC):
            p.op(SCA, lambda e, fc=fc, ofm=ofm, scr=scr: e.activation(
                scr, ofm[:, fc * TC:(fc + 1) * TC], AF.Identity,
                accum_out=s.GSTATS[:, fc:fc + 1]),
                waits=[("Vm", s.v_ofm)], inc="Sm")
            p.op(SCA, lambda e, fc=fc, ofm=ofm, scr=scr: e.activation(
                scr, ofm[:, fc * TC:(fc + 1) * TC], AF.Square,
                accum_out=s.GSTATS[:, 8 + fc:9 + fc]), inc="Sm")
        sg = p.val("Sm")
        for fc in range(NFC):
            p.op(TEN, lambda e, fc=fc: e.matmul(
                s.PE[0:16, 0:1], s.SELSUM32[:, fc * 16:(fc + 1) * 16],
                s.GSTATS[:, fc:fc + 1], start=(fc == 0), stop=(fc == NFC - 1)),
                waits=([("Sm", sg), ("Vm", pe_free)] if fc == 0 else ()))
            p.op(TEN, lambda e, fc=fc: e.matmul(
                s.PF[0:16, 0:1], s.SELSUM32[:, fc * 16:(fc + 1) * 16],
                s.GSTATS[:, 8 + fc:9 + fc], start=(fc == 0),
                stop=(fc == NFC - 1)),
                waits=([("Vm", pf_free)] if fc == 0 else ()),
                inc=("Tm" if fc == NFC - 1 else None))
        tg = p.val("Tm")
        p.op(VEC, lambda e: e.tensor_scalar(
            s.MRS[:, 0:1], s.PE[0:16, 0:1], 1.0, None, ALU.mult),
            waits=[("Tm", tg)], inc="Vm")
        p.op(VEC, lambda e: e.tensor_scalar(
            s.MRS[:, 1:2], s.PF[0:16, 0:1], 1.0, None, ALU.mult), inc="Vm")
        vg = p.val("Vm")
        p.dma(SYN, s.ag3_in, s.MRS[:, 0:2], waits=[("Vm", vg)], inc="dT")
        d3 = p.val("dT")
        p.op(GPS, lambda e: e.collective_compute(
            "AllGather", ALU.bypass, replica_groups=[list(range(self.ncores))],
            ins=[s.ag3_in.opt()], outs=[s.ag3_out.opt()]),
            waits=[("dT", d3), ("cc", s._cc_done)], inc="cc")
        s._cc_done = p.val("cc")
        s._p2wo_free = p.val("Tm")
        s._p2wo_sfree = p.val("Sm")

    def emit_attn_end(self):
        s, p = self, self.P
        nr = self.ncores
        gsb = s.p32(1600, 2)
        p.dma(GPS, gsb[0:nr * 16, :], s.ag3_out,
              waits=[("cc", s._cc_done)], inc="dG2")
        dg = p.val("dG2")
        p.op(TEN, lambda e: e.matmul(
            s.PE[0:16, 0:2], s.FOLD8[0:16 * nr, 0:16], gsb[0:16 * nr, :],
            start=True, stop=True), waits=[("dG2", dg)], inc="Tm")
        tg = p.val("Tm")
        N_ = float(TC * self.ncores * HD)
        V = lambda fn, w=(): p.op(VEC, fn, waits=w, inc="Vm")
        S_ = lambda fn, w=(): p.op(SCA, fn, waits=w, inc="Sm")
        V(lambda e: e.tensor_scalar(s.MRS[:, 2:4], s.PE[0:16, 0:2], 1.0 / N_,
                                    None, ALU.mult), w=[("Tm", tg)])
        V(lambda e: e.tensor_tensor(s.MRS[:, 4:5], s.MRS[:, 2:3],
                                    s.MRS[:, 2:3], ALU.mult))
        V(lambda e: e.tensor_tensor(s.MRS[:, 3:4], s.MRS[:, 3:4],
                                    s.MRS[:, 4:5], ALU.subtract))
        vv = p.val("Vm")
        S_(lambda e: e.activation(s.MRS[:, 3:4], s.MRS[:, 3:4], AF.Sqrt,
                                  bias=s.EPSC[0:16, 1:2]), w=[("Vm", vv)])
        sv = p.val("Sm")
        V(lambda e: e.reciprocal(s.MRS[:, 3:4], s.MRS[:, 3:4]), w=[("Sm", sv)])
        V(lambda e: e.tensor_tensor(s.MRS[:, 4:5], s.MRS[:, 2:3],
                                    s.MRS[:, 3:4], ALU.mult))
        vmrs = p.val("Vm")
        for fc in range(NFC):
            p.op(TEN, lambda e, fc=fc: e.matmul(
                s.PE[0:128, 4 + 2 * fc: 6 + 2 * fc],
                s.EXPAND32[:, fc * 128:(fc + 1) * 128], s.MRS[:, 3:5],
                start=True, stop=True), waits=[("Vm", vmrs)],
                inc=("Tm" if fc == NFC - 1 else None))
        te = p.val("Tm")
        for fc in range(NFC):
            V(lambda e, fc=fc: e.tensor_tensor(
                s.COEF[:, fc:fc + 1], s.GNS[:, fc:fc + 1],
                s.PE[0:128, 4 + 2 * fc: 5 + 2 * fc], ALU.mult),
              w=([("Tm", te), ("dT", s.c_consts)] if fc == 0 else ()))
            V(lambda e, fc=fc: e.tensor_tensor(
                s.COEF[:, 8 + fc: 9 + fc], s.GNS[:, fc:fc + 1],
                s.PE[0:128, 5 + 2 * fc: 6 + 2 * fc], ALU.mult))
            V(lambda e, fc=fc: e.tensor_tensor(
                s.COEF[:, 8 + fc: 9 + fc], s.GNB[:, fc:fc + 1],
                s.COEF[:, 8 + fc: 9 + fc], ALU.subtract))
        v_coef = p.val("Vm")
        wslots = [s.p16(s.EN_WO, 8192), s.p16(s.EN_WO + 8192, 8192)]
        p.dma(GPS, wslots[0].rearrange("p (kc c) -> p kc c", c=DM),
              s.wob.rearrange("(kc p) c -> p kc c", p=128),
              waits=[("dW", s.wcast_at["wo"]), ("Tm", s._p2wo_free),
                     ("Sm", s._p2wo_sfree), ("dUp0", p.val("dUp0")),
                     ("dUp1", p.val("dUp1"))],
              inc="dUp0")
        dwo = p.val("dUp0")
        ofm = s.p16(s.P2_OFM, 16384)
        xwo = s.p16(s.EN_XWO, 4096)
        dog = [0, 0]
        og_free = [0, 0]
        dx1 = [0, 0]
        x1_free = [0, 0]
        for j in range(NTT):
            js = slice(j * 512, (j + 1) * 512)
            for fc in range(NFC):
                par = fc % 2
                ogr = s.p16(s.EN_OG + par * 512, 512)
                p.dma(SYN, ogr, s.og_s[fc * 128:(fc + 1) * 128, js],
                      waits=[("dSp0", s._qs_sp0), ("dSp1", s._qs_sp1),
                             ("Vm", og_free[par])], inc="dO1%d" % par)
                dog[par] = p.val("dO1%d" % par)
                p.op(VEC, lambda e, fc=fc, j=j, ofm=ofm, xwo=xwo: e.tensor_scalar(
                    xwo[:, fc * 512:(fc + 1) * 512],
                    ofm[:, fc * TC + j * 512: fc * TC + j * 512 + 512],
                    s.COEF[:, fc:fc + 1], s.COEF[:, 8 + fc: 9 + fc],
                    ALU.mult, ALU.add),
                    waits=[("Vm", v_coef), ("Vm", s.v_ofm),
                           ("Tm", s._xwo_free)], inc="Vm")
                p.op(VEC, lambda e, fc=fc, ogr=ogr, xwo=xwo: e.tensor_tensor(
                    xwo[:, fc * 512:(fc + 1) * 512],
                    xwo[:, fc * 512:(fc + 1) * 512], ogr, ALU.mult),
                    waits=[("dO1%d" % par, dog[par])], inc="Vm")
                og_free[par] = p.val("Vm")
            v_xwo = p.val("Vm")
            for oc in range(NFC):
                q_ = s.PBIG[:, (oc % 4) * 512:(oc % 4) * 512 + 512]
                for kc in range(NFC):
                    p.op(TEN, lambda e, q=q_, kc=kc, oc=oc, xwo=xwo: e.matmul(
                        q, wslots[0][:, kc * DM + oc * 128:
                                     kc * DM + oc * 128 + 128],
                        xwo[:, kc * 512:(kc + 1) * 512],
                        start=(kc == 0), stop=(kc == NFC - 1)),
                        waits=([("dUp0", dwo), ("Vm", v_xwo),
                                ("Vm", s._en_qfree[oc % 4])] if kc == 0
                               else ()),
                        inc=("Tm" if kc == NFC - 1 else None))
                tm = p.val("Tm")
                s._xwo_free = tm
                par = oc % 2
                x1r = s.p16(s.EN_X1 + par * 512, 512)
                p.dma(SYN, x1r, s.x1_s[oc * 128:(oc + 1) * 128, js],
                      waits=[("Vm", x1_free[par])], inc="dX1%d" % par)
                dx = p.val("dX1%d" % par)
                p.op(VEC, lambda e, q=q_, oc=oc, j=j, x1r=x1r:
                     e.scalar_tensor_tensor(
                         s.Xb[:, oc * TC + j * 512: oc * TC + j * 512 + 512],
                         q, s.BO[:, oc:oc + 1], x1r, ALU.add, ALU.add),
                     waits=[("Tm", tm), ("dX1%d" % par, dx)], inc="Vm")
                s._en_qfree[oc % 4] = p.val("Vm")
                x1_free[par] = p.val("Vm")
        s.v_x2 = p.val("Vm")

    def stage_barrier(self):
        p = self.P
        cnt = {x: p.val(x) for x in ("Tm", "Vm", "Sm", "Gm")}
        dma = {x: p.val(x) for x in ("dT", "dHb", "dHs", "dX",
                                     "dX10", "dX11", "dF0", "dF1",
                                     "dSp0", "dSp1", "dSl0", "dSl1",
                                     "dZ0", "dZ1", "dMc0", "dMc1",
                                     "dQ20", "dQ21", "dO10", "dO11",
                                     "dUp0", "dUp1", "dDn")}
        for eng in (TEN, VEC, SCA, GPS):
            for sem, v in cnt.items():
                p.barrier_wait(eng, sem, v)
            for sem, v in dma.items():
                p.barrier_wait(eng, sem, v)

    def emit_all(self):
        self.emit_loads()
        self.emit_ln(xwaits=[("dG", self.c_x)])
        self.emit_router(xwaits=[("dG", self.c_x)])
        self.emit_ffn(stage=1)
        self.stage_barrier()
        self.emit_attn_proj()
        self.stage_barrier()
        self.emit_scan()
        self.stage_barrier()
        self.emit_fold()
        self.stage_barrier()
        self.emit_pass2()
        self.stage_barrier()
        self.emit_attn_end()
        self.stage_barrier()
        self._lnq_free = [self.P.val("Vm")] * NTT
        self.emit_ln(xwaits=[("Vm", self.v_x2)])
        self.emit_router()
        self.emit_ffn(stage=2)
        self.finish()


_NC_CACHE = {}


def _build_full():
    if "nc" not in _NC_CACHE:
        b = Builder()
        b.emit_all()
        _NC_CACHE["nc"] = b.build()
    return _NC_CACHE["nc"]


def kernel(**inputs):
    nc = _build_full()
    in_maps = [_prep_core_inputs(inputs, core) for core in range(NCORES)]
    res = run_bass_kernel_spmd(nc, in_maps, core_ids=list(range(NCORES)))
    out = np.concatenate([np.asarray(r["y"]).T for r in res.results], axis=0)
    return out.reshape(1, T, DM).astype(np.float32)


def _prep_core_inputs(full, core):
    f32 = np.float32
    t0 = core * TC
    x = np.ascontiguousarray(full["L0_attn_x"][0, t0:t0 + TC, :].T).astype(f32)
    conv = np.concatenate([full["L0_attn_q_conv"], full["L0_attn_k_conv"],
                           full["L0_attn_v_conv"]], axis=1)
    convw = np.zeros((128, NFC * 12), f32)
    for fc in range(NFC):
        convw[:, fc * 12:(fc + 1) * 12] = conv[fc * 128:(fc + 1) * 128, :]
    bqkvg = np.zeros((128, 4 * NFC), f32)
    for i, b in enumerate(["bq", "bk", "bv", "bg"]):
        bqkvg[:, i * NFC:(i + 1) * NFC] = np.asarray(
            full["L0_attn_" + b]).reshape(NFC, 128).T
    bo_c = np.asarray(full["L0_attn_bo"]).reshape(NFC, 128).T.astype(f32)
    bab = np.stack([full["L0_attn_ba"], full["L0_attn_bb"]], axis=1).astype(f32)
    gns = full["L0_attn_gn_scale"].reshape(NFC, 128).T.astype(f32)
    gnb = full["L0_attn_gn_bias"].reshape(NFC, 128).T.astype(f32)
    dwn = full["L0_ffn_down_weights"]
    dwn_t = np.transpose(dwn, (1, 2, 0))
    down_r = np.zeros((NFC, E, 4, 128, 128), f32)
    for oc in range(NFC):
        for e_ in range(E):
            blk = dwn[e_, oc * 128:(oc + 1) * 128, :]
            for ic in range(4):
                down_r[oc, e_, ic] = blk[:, ic * 128:(ic + 1) * 128].T
    shd = full["L0_ffn_shared_down_weight"]
    shdown_r = np.zeros((NFC, 16, 128, 128), f32)
    for oc in range(NFC):
        blk = shd[oc * 128:(oc + 1) * 128, :]
        for sc in range(16):
            shdown_r[oc, sc] = blk[:, sc * 128:(sc + 1) * 128].T
    tri2 = np.zeros((128, 256), f32)
    sidx = np.arange(128)[:, None]
    tidx = np.arange(128)[None, :]
    tri2[:, 0:128] = np.where(sidx < tidx, 0.0, -60000.0)
    tri2[:, 128:256] = np.where(sidx <= tidx, 0.0, -60000.0)
    onescol = np.zeros((128, 2), f32); onescol[:, 0] = 1.0
    selsum = np.zeros((128, NFC * 16), f32)
    expand = np.zeros((16, NFC * 128), f32)
    for fc in range(NFC):
        selsum[0:64, fc * 16 + 2 * fc] = 1.0
        selsum[64:128, fc * 16 + 2 * fc + 1] = 1.0
        expand[2 * fc, fc * 128: fc * 128 + 64] = 1.0
        expand[2 * fc + 1, fc * 128 + 64: fc * 128 + 128] = 1.0
    iota8 = np.tile(np.arange(8, dtype=f32)[None, :], (128, 1))
    onehot = np.zeros((8, 1), f32)
    if core > 0:
        onehot[core - 1, 0] = 1.0
    foldA = np.zeros((64, 7), f32)
    foldA[:, :core] = 1.0
    foldB = np.zeros((64, 7 * 64), f32)
    for j in range(7):
        if j >= core:
            foldB[:, j * 64:(j + 1) * 64] = np.eye(64, dtype=f32)
    fold8 = np.zeros((128, 16), f32)
    for r_ in range(8):
        fold8[r_ * 16 + np.arange(16), np.arange(16)] = 1.0
    d = {
        "x0": x,
        "wq": full["L0_attn_Wq"].astype(f32), "wk": full["L0_attn_Wk"].astype(f32),
        "wv": full["L0_attn_Wv"].astype(f32), "wg": full["L0_attn_Wg"].astype(f32),
        "wo": full["L0_attn_Wo"].astype(f32),
        "bqkvg": bqkvg, "bo_c": bo_c, "bab": bab,
        "wa": full["L0_attn_Wa"].astype(f32), "wb": full["L0_attn_Wb"].astype(f32),
        "convw": convw, "gns": gns, "gnb": gnb,
        "gatew": np.ascontiguousarray(full["L0_ffn_gate_weight"].T).astype(f32),
        "gateb": full["L0_ffn_gate_bias"].reshape(8, 1).astype(f32),
        "upw": np.ascontiguousarray(np.transpose(
            full["L0_ffn_up_weights"], (0, 2, 1))).astype(f32),
        "downw": down_r, "shdownw": shdown_r,
        "shupw": np.ascontiguousarray(full["L0_ffn_shared_up_weight"].T).astype(f32),
        "onehot": onehot, "h0m": np.array([[0.0 if core == 0 else 1.0]], f32),
        "foldA": foldA, "foldB": foldB, "fold8": fold8,
        "ident": np.eye(128, dtype=f32),
        "identb": np.eye(128, dtype=f32).astype(BFNP),
        "tri2": tri2, "onescol": onescol.astype(BFNP), "onescol32": onescol,
        "selsum": selsum.astype(BFNP), "expand": expand.astype(BFNP),
        "selsum32": selsum, "expand32": expand,
        "iota8": iota8, "onesr": np.ones((1, 128), f32), "ones4": _ones4(),
        "epsc": np.tile(np.array([[LNEPS, GNEPS]], f32), (128, 1)),
        "gateb128": np.tile(np.asarray(full["L0_ffn_gate_bias"],
                                       dtype=f32)[None, :], (128, 1)),
        "zrow": np.zeros((16, 512), f32),
    }
    return d


def _bias_row(v, n):
    return v.reshape(1, n)


def _ones4():
    o = np.zeros((128, 128), np.float32)
    for r in (0, 32, 64, 96):
        o[r, :] = 1.0
    return o




# version 05 (final)
# speedup vs baseline: 1.0817x; 1.0690x over previous
import numpy as np
import ml_dtypes
from contextlib import ExitStack

import concourse.bass as bass
import concourse.mybir as mybir
from concourse.bass_utils import run_bass_kernel_spmd

F32 = mybir.dt.float32
BF16 = mybir.dt.bfloat16
AF = mybir.ActivationFunctionType
ALU = mybir.AluOpType
BFNP = ml_dtypes.bfloat16

NCORES = 8
T = 16384
TC = T // NCORES
DM = 1024
NH = 16
HD = 64
CH = 128
NCH = TC // CH
NFC = DM // 128
NTT = TC // 512
NTG = TC // 128
E = 8
EI = 512
SI = 2048
LNEPS = 1e-5
GNEPS = 1e-5
RSCALE = 2.5

TEN, VEC, SCA, GPS, SYN = "tensor", "vector", "scalar", "gpsimd", "sync"
ENGINES = (TEN, VEC, SCA, GPS, SYN)



def pbcast(a, n):
    ap = [list(x) for x in a.ap]
    ap[0] = [0, n]
    return bass.AP(a.tensor, a.offset, ap)

class Prog:

    def __init__(self):
        self.ops = {e: [] for e in ENGINES}
        self.count = {}
        self.handles = {}

    def val(self, sem):
        return self.count.get(sem, 0)

    def bump(self, sem, n):
        self.count[sem] = self.count.get(sem, 0) + n
        return self.count[sem]

    def op(self, engine, build, waits=(), inc=None, inc_n=1, drain=None):
        waits = tuple((s, v) for (s, v) in waits if v > 0)
        if inc is not None:
            self.bump(inc, inc_n)
        if drain is None:
            drain = engine in (VEC, SCA)

        def fn(eng, _w=waits, _b=build, _i=inc, _n=inc_n, _d=drain):
            if _d:
                eng.drain()
            for s, v in _w:
                eng.wait_ge(self.handles[s], v)
            ins = _b(eng)
            if _i is not None:
                ins.then_inc(self.handles[_i], _n)

        self.ops[engine].append(fn)

    def dma(self, engine, out, in_, waits=(), inc=None):
        self.op(engine, lambda e, o=out, i=in_: e.dma_start(out=o, in_=i),
                waits=waits, inc=inc, inc_n=16)

    def barrier_wait(self, engine, sem, value):
        if value > 0:
            self.ops[engine].append(
                lambda eng, s=sem, v=value: eng.wait_ge(self.handles[s], v))

    def replay(self, block):
        for name in ENGINES:
            def run(eng, n=name):
                for fn in self.ops[n]:
                    fn(eng)
            getattr(block, name)(run)


class Builder:
    def __init__(self, taps=(), ncores=NCORES):
        self.ncores = ncores
        self.nc = bass.Bass("TRN2", target_bir_lowering=False)
        self.P = Prog()
        self.st = ExitStack()
        self.taps = {}
        self.want_taps = set(taps)
        self._declare()

    def din(self, name, shape, dt=F32):
        return self.nc.dram_tensor(name, shape, dt, kind="ExternalInput").ap()

    def dint(self, name, shape, dt):
        return self.nc.dram_tensor(name, shape, dt).ap()

    def sbuf(self, name, shape, dt):
        return self.st.enter_context(self.nc.sbuf_tensor(name, shape, dt))

    def psum(self, name, shape, dt=F32):
        return self.st.enter_context(self.nc.psum_tensor(name, shape, dt))

    def tap(self, name, ap, shape, dt, waits):
        if name not in self.want_taps:
            return
        t = self.nc.dram_tensor("tap_" + name, shape, dt,
                                kind="ExternalOutput").ap()
        self.taps[name] = t
        self.P.dma(SYN, t, ap,
                   waits=list(waits) + [("dT", self.P.val("dT"))], inc="dT")

    def _declare(self):
        nc, P = self.nc, self.P
        din, dint, sb, ps = self.din, self.dint, self.sbuf, self.psum
        self.x0 = din("x0", [DM, TC])
        self.wq = din("wq", [DM, DM]); self.wk = din("wk", [DM, DM])
        self.wv = din("wv", [DM, DM]); self.wg = din("wg", [DM, DM])
        self.wo = din("wo", [DM, DM])
        self.bqkvg = din("bqkvg", [128, 4 * NFC])
        self.bo_c = din("bo_c", [128, NFC])
        self.bab = din("bab", [16, 2])
        self.wa = din("wa", [DM, NH]); self.wb = din("wb", [DM, NH])
        self.convw = din("convw", [128, NFC * 12])
        self.gns = din("gns", [128, NFC]); self.gnb = din("gnb", [128, NFC])
        self.gatew = din("gatew", [DM, E]); self.gateb = din("gateb", [8, 1])
        self.gateb128 = din("gateb128", [128, 8])
        self.upw = din("upw", [E, DM, EI])
        self.downw = din("downw", [NFC, E, 4, 128, 128])
        self.shupw = din("shupw", [DM, SI])
        self.shdownw = din("shdownw", [NFC, 16, 128, 128])
        self.onehot = din("onehot", [8, 1])
        self.h0m = din("h0m", [1, 1])
        self.foldA = din("foldA", [64, 7])
        self.foldB = din("foldB", [64, 7 * 64])
        self.fold8 = din("fold8", [128, 16])
        self.ident = din("ident", [128, 128])
        self.identb = din("identb", [128, 128], BF16)
        self.tri2 = din("tri2", [128, 256])
        self.onescol = din("onescol", [128, 2], BF16)
        self.onescol32 = din("onescol32", [128, 2])
        self.selsum = din("selsum", [128, NFC * 16], BF16)
        self.selsum32 = din("selsum32", [128, NFC * 16])
        self.expand = din("expand", [16, NFC * 128], BF16)
        self.expand32 = din("expand32", [16, NFC * 128])
        self.iota8 = din("iota8", [128, 8])
        self.onesr = din("onesr", [1, 128])
        self.ones4 = din("ones4", [128, 128])
        self.zrow = din("zrow", [16, 512])
        self.epsc = din("epsc", [128, 2])
        self.y = nc.dram_tensor("y", [DM, TC], F32, kind="ExternalOutput").ap()

        self.upb = dint("upb", [E, DM, EI], BF16)
        self.downb = dint("downb", [NFC, E, 4, 128, 128], BF16)
        self.shupb = dint("shupb", [DM, SI], BF16)
        self.shdownb = dint("shdownb", [NFC, 16, 128, 128], BF16)
        self.wqb = dint("wqb", [DM, DM], BF16)
        self.wkb = dint("wkb", [DM, DM], BF16)
        self.wvb = dint("wvb", [DM, DM], BF16)
        self.wgb = dint("wgb", [DM, DM], BF16)
        self.wob = dint("wob", [DM, DM], BF16)
        self.gatewb = dint("gatewb", [DM, E], BF16)
        self.wab = dint("wab", [DM, NH], BF16)
        self.wbb = dint("wbb", [DM, NH], BF16)
        self.qhat_s = dint("qhat_s", [NCH, DM, CH], BF16)
        self.ksil_s = dint("ksil_s", [NCH, DM, CH], BF16)
        self.vsil_s = dint("vsil_s", [NCH, DM, CH], BF16)
        self.og_s = dint("og_s", [DM, TC], BF16)
        self.x1_s = dint("x1_s", [DM, TC], BF16)
        self.z_s = dint("z_s", [NCH, NH, HD, CH], BF16)
        self.o1_s = dint("o1_s", [NCH, NH, CH, HD], BF16)
        self.mc_s = dint("mc_s", [NCH, NH, HD, HD], BF16)
        self.q2_s = dint("q2_s", [NCH, NH, HD, HD], BF16)
        self.hal_in = dint("hal_in", [DM, 3], BF16)
        self.halo_b = dint("halo_b", [NFC, 1, 384], BF16)
        self.hal_out = dint("hal_out", [self.ncores * DM, 3], BF16)
        self.ag2_in = dint("ag2_in", [HD, NH * 2 * HD], BF16)
        self.ag2_out = dint("ag2_out", [self.ncores * HD, NH * 2 * HD], BF16)
        self.ag3_in = dint("ag3_in", [NH, 2], F32)
        self.ag3_out = dint("ag3_out", [self.ncores * NH, 2], F32)

        self.Xb = sb("Xb", [128, NFC * TC], BF16)
        self.T1 = sb("T1", [128, TC], F32)
        self.T3 = sb("T3", [128, TC], F32)
        self.VSTK = sb("VSTK", [128, TC], F32)
        self.STATS = sb("STATS", [128, 2048], F32)
        self.STGF = sb("STGF", [128, 2048], F32)
        self.RB = sb("RB", [128, TC], BF16)
        self.CONVW = sb("CONVW", [128, NFC * 12], F32)
        self.GNS = sb("GNS", [128, NFC], F32)
        self.GNB = sb("GNB", [128, NFC], F32)
        self.BQKVG = sb("BQKVG", [128, 4 * NFC], F32)
        self.BO = sb("BO", [128, NFC], F32)
        self.BAB = sb("BAB", [16, 2], F32)
        self.IDENT = sb("IDENT", [128, 128], F32)
        self.IDENTB = sb("IDENTB", [128, 128], BF16)
        self.TRI2 = sb("TRI2", [128, 256], F32)
        self.ONESC = sb("ONESC", [128, 2], BF16)
        self.ONESC32 = sb("ONESC32", [128, 2], F32)
        self.SELSUM = sb("SELSUM", [128, NFC * 16], BF16)
        self.SELSUM32 = sb("SELSUM32", [128, NFC * 16], F32)
        self.EXPAND = sb("EXPAND", [16, NFC * 128], BF16)
        self.EXPAND32 = sb("EXPAND32", [16, NFC * 128], F32)
        self.IOTA8 = sb("IOTA8", [128, 8], F32)
        self.ONESR = sb("ONESR", [1, 128], F32)
        self.ONES4 = sb("ONES4", [128, 128], F32)
        self.ZROW = sb("ZROW", [16, 512], F32)
        self.EPSC = sb("EPSC", [128, 2], F32)
        self.ONEHOT = sb("ONEHOT", [8, 1], BF16)
        self.H0M = sb("H0M", [1, 1], F32)
        self.FOLDA = sb("FOLDA", [64, 7], F32)
        self.FOLDB = sb("FOLDB", [64, 7 * 64], F32)
        self.FOLD8 = sb("FOLD8", [128, 16], F32)
        self.GCOLS = sb("GCOLS", [128, 4 * NH], F32)
        self.GCREP = sb("GCREP", [64, NH], F32)
        self.SLOC = sb("SLOC", [64, NH * HD], BF16)
        self.MCH = sb("MCH", [64, NH * HD], BF16)
        self.SS = sb("SS", [64, NH * HD], BF16)
        self.STRUE = sb("STRUE", [64, NH * HD], BF16)
        self.GSTATS = sb("GSTATS", [128, 16], F32)
        self.MRS = sb("MRS", [16, 8], F32)
        self.COEF = sb("COEF", [128, 2 * NFC], F32)
        self.R2TM = sb("R2TM", [128, NTG], F32)
        self.GATEB = sb("GATEB", [8, 1], F32)
        self.GATEB128 = sb("GATEB128", [128, 8], F32)
        self.YSTG = sb("YSTG", [128, 1024], F32)
        self.WSUMS = sb("WSUMS", [128, 4160], BF16)
        self.MROWB = sb("MROWB", [128, 2048], BF16)
        self.POOL16 = sb("POOL16", [128, 40512], BF16)
        self.POOL32 = sb("POOL32", [128, 2048], F32)

        self.PBIG = ps("PBIG", [128, 2048])
        self.PE = ps("PE", [128, 512]); self.PF = ps("PF", [128, 512])
        self.PG = ps("PG", [128, 512])
        self.PT = ps("PT", [128, 1024], BF16)

        for s in ("dT", "dG", "dG2", "dW", "dWs", "dUp0", "dUp1", "dDn", "dX", "dHs", "dHb", "dX10", "dX11", "dF0", "dF1", "dSp0", "dSp1", "dSl0", "dSl1", "dZ0", "dZ1", "dMc0", "dMc1", "dQ20", "dQ21", "dO10", "dO11", "Tm", "Vm", "Sm", "Gm", "cc"):
            P.handles[s] = self.st.enter_context(nc.semaphore(s))
        self._rb_free = 0
        self._r2_free = 0
        self._lnq_free = [0, 0, 0, 0]
        self._ffn_q_free = [0, 0, 0, 0]
        self._x1dma = 0
        self._ydma = 0
        self._stg_free = 0
        self._stg_dma = 0
        self._cc_done = 0
        self._hrow_v = 0
        self._hbounce_d = 0
        self._sq_free = 0
        self._ssq_v = 0
        self._ab_v = 0
        self._ab_s = 0
        self._spill_d = 0
        self._pbig_sfree = [0, 0, 0, 0]
        self._pbig_vfree = [0, 0, 0, 0]
        self._p2big_free = [0, 0, 0, 0]
        self._otm_free = 0
        self._p2wo_free = 0
        self._p2wo_sfree = 0
        self._xwo_free = 0
        self._en_qfree = [0, 0, 0, 0]

    def p16(self, off, width):
        assert off >= 0 and off + width <= 40512, (off, width)
        return self.POOL16[:, off:off + width]

    def p32(self, off, width):
        assert off >= 0 and off + width <= 2048, (off, width)
        return self.POOL32[:, off:off + width]

    def emit_loads(self):
        s, p = self, self.P
        for dst, src in [
            (s.IDENT, s.ident), (s.TRI2, s.tri2), (s.ONESC32, s.onescol32),
            (s.IOTA8, s.iota8), (s.H0M, s.h0m), (s.FOLDA, s.foldA), (s.FOLDB, s.foldB),
            (s.FOLD8, s.fold8),
            (s.GNS, s.gns), (s.GNB, s.gnb), (s.CONVW, s.convw),
            (s.SELSUM32, s.selsum32), (s.EXPAND32, s.expand32),
            (s.BQKVG, s.bqkvg), (s.BO, s.bo_c), (s.BAB, s.bab),
            (s.ONESR, s.onesr), (s.GATEB, s.gateb), (s.EPSC, s.epsc),
            (s.GATEB128, s.gateb128), (s.ZROW, s.zrow), (s.ONES4, s.ones4),
        ]:
            p.dma(SYN, dst[:], src, inc="dT")
        for dst, src in [(s.IDENTB, s.identb), (s.ONESC, s.onescol),
                         (s.SELSUM, s.selsum), (s.EXPAND, s.expand)]:
            p.dma(SYN, dst[:], src, inc="dT")
        p.dma(GPS, s.ONEHOT[:], s.onehot, inc="dG")
        for fc in range(NFC):
            p.dma(GPS, s.Xb[:, fc * TC:(fc + 1) * TC],
                  s.x0[fc * 128:(fc + 1) * 128, :], inc="dG")
        s.c_consts = p.val("dT")
        s.c_x = p.val("dG")
        for eng in (TEN, VEC, SCA):
            p.barrier_wait(eng, "dT", s.c_consts)
        p.barrier_wait(TEN, "dG", s.c_x)
        s.wcast_at = {}
        for nm, src, dst in [
            ("gate", s.gatew, s.gatewb), ("shup", s.shupw, s.shupb),
            ("up", s.upw, s.upb), ("down", s.downw, s.downb),
            ("shdown", s.shdownw, s.shdownb),
            ("wq", s.wq, s.wqb), ("wk", s.wk, s.wkb), ("wv", s.wv, s.wvb),
            ("wg", s.wg, s.wgb), ("wa", s.wa, s.wab), ("wb", s.wb, s.wbb),
            ("wo", s.wo, s.wob),
        ]:
            p.dma(GPS, dst, src, waits=[("dW", p.val("dW"))], inc="dW")
            s.wcast_at[nm] = p.val("dW")

        def wsum(dst_row, wsrc, ncols, wait_w):
            for oc in range(0, ncols, 512):
                ow = min(512, ncols - oc)
                for kc in range(NFC):
                    buf = s.p16(kc * 512, 512)
                    p.dma(SYN, buf[:, 0:ow],
                          wsrc[kc * 128:(kc + 1) * 128, oc:oc + ow],
                          waits=[("dW", wait_w),
                                 ("Tm", s._wsum_tm)], inc="dWs")
                dt = p.val("dWs")
                for kc in range(NFC):
                    buf = s.p16(kc * 512, 512)
                    p.op(TEN, lambda e, b=buf, ow=ow, k=kc: e.matmul(
                        s.PG[0:1, 0:ow], s.ONESC[:, 0:1], b[:, 0:ow],
                        start=(k == 0), stop=(k == NFC - 1)),
                        waits=([("dWs", dt), ("Vm", s._wsum_vm)]
                               if kc == 0 else ()),
                        inc=("Tm" if kc == NFC - 1 else None))
                s._wsum_tm = p.val("Tm")
                p.op(VEC, lambda e, d=dst_row, oc=oc, ow=ow: e.tensor_scalar(
                    d[:, oc:oc + ow], s.PG[0:1, 0:ow], -1.0, None, ALU.mult),
                    waits=[("Tm", s._wsum_tm)], inc="Vm")
                s._wsum_vm = p.val("Vm")

        s._wsum_tm = 0
        s._wsum_vm = 0
        wsum(s.WSUMS[32:33, 2048:2056], s.gatewb, E, s.wcast_at["gate"])
        wsum(s.WSUMS[32:33, 0:2048], s.shupb, SI, s.wcast_at["shup"])
        for e_ in range(E):
            wsum(s.WSUMS[0:1, e_ * 512:(e_ + 1) * 512],
                 s.upb[e_], EI, s.wcast_at["up"])
        for i_, nm in enumerate(("wq", "wk", "wv", "wg")):
            wsum(s.WSUMS[64:65, i_ * DM:(i_ + 1) * DM],
                 getattr(s, nm + "b"), DM, s.wcast_at[nm])
        wsum(s.WSUMS[64:65, 4 * DM: 4 * DM + 16], s.wab, NH,
             s.wcast_at["wa"])
        wsum(s.WSUMS[64:65, 4 * DM + 16: 4 * DM + 32], s.wbb, NH,
             s.wcast_at["wb"])
        s.v_wsum = p.val("Vm")
        s.t_wsum = p.val("Tm")

    LN_SQ = 24576

    def emit_ln(self, xwaits=()):
        s, p = self, self.P
        sq_last_use = [0, 0]
        for j in range(NTT):
            js = slice(j * 512, (j + 1) * 512)
            q = s.PBIG[:, js]
            for fc in range(NFC):
                xs = s.Xb[:, fc * TC + j * 512: fc * TC + j * 512 + 512]
                bi = fc % 2
                sq = s.p16(s.LN_SQ + bi * 512, 512)
                w = [("Tm", sq_last_use[bi])]
                if fc == 0:
                    w = w + list(xwaits) + [("Vm", s._lnq_free[j])]
                p.op(SCA, lambda e, x=xs, qq=sq: e.activation(qq[:], x, AF.Square),
                     waits=w, inc="Sm")
                sm = p.val("Sm")
                p.op(TEN, lambda e, x=xs, fc=fc, q=q: e.matmul(
                    q[0:1, :], s.ONESC[:, 0:1], x,
                    start=(fc == 0), stop=(fc == NFC - 1)),
                    waits=(list(xwaits) + [("Vm", s._lnq_free[j])]) if fc == 0 else ())
                p.op(TEN, lambda e, sq=sq, fc=fc, q=q: e.matmul(
                    q[32:33, :], s.ONESC[:, 0:1], sq[:],
                    start=(fc == 0), stop=(fc == NFC - 1)),
                    waits=[("Sm", sm)], inc="Tm")
                sq_last_use[bi] = p.val("Tm")
            tm = p.val("Tm")
            ST = s.STATS
            p.op(VEC, lambda e, js=js, q=q: e.tensor_scalar(
                ST[0:1, js], q[0:1, :], 1.0 / DM, None,
                ALU.mult), waits=[("Tm", tm)], inc="Vm")
            p.op(VEC, lambda e, js=js: e.tensor_scalar(
                s.MROWB[0:1, js], ST[0:1, js], 1.0, None, ALU.mult), inc="Vm")
            p.op(VEC, lambda e, js=js: e.tensor_scalar(
                s.MROWB[32:33, js], ST[0:1, js], 1.0, None, ALU.mult), inc="Vm")
            p.op(VEC, lambda e, js=js: e.tensor_scalar(
                s.MROWB[64:65, js], ST[0:1, js], 1.0, None, ALU.mult), inc="Vm")
            p.op(VEC, lambda e, js=js: e.tensor_tensor(
                ST[32:33, js], ST[0:1, js], ST[0:1, js], ALU.mult), inc="Vm")
            p.op(VEC, lambda e, js=js, q=q: e.scalar_tensor_tensor(
                ST[64:65, js], q[32:33, :], 1.0 / DM, ST[32:33, js],
                ALU.mult, ALU.subtract), inc="Vm")
            s._lnq_free[j] = p.val("Vm")
            vv = p.val("Vm")
            p.op(SCA, lambda e, js=js: e.activation(
                ST[64:65, js], ST[64:65, js], AF.Sqrt, bias=s.EPSC[0:1, 0:1]),
                waits=[("Vm", vv)], inc="Sm")
            sv = p.val("Sm")
            p.op(VEC, lambda e, js=js: e.reciprocal(
                ST[64:65, js], ST[64:65, js]), waits=[("Sm", sv)], inc="Vm")
            p.op(VEC, lambda e, js=js: e.tensor_scalar(
                s.MROWB[96:97, js], ST[64:65, js], 1.0, None, ALU.mult), inc="Vm")
            p.op(VEC, lambda e, js=js: e.tensor_tensor(
                ST[32:33, js], ST[64:65, js], ST[64:65, js], ALU.mult), inc="Vm")
            vv = p.val("Vm")
            p.op(TEN, lambda e, js=js: e.matmul(
                s.PF[0:128, 0:512], s.ONES4[64:65, 0:128], ST[64:65, js],
                start=True, stop=True),
                waits=[("Vm", vv), ("Vm", s._r2_free)], inc="Tm")
            tm_rb = p.val("Tm")
            p.op(VEC, lambda e, js=js: e.tensor_scalar(
                s.RB[:, js], s.PF[0:128, 0:512], 1.0, None, ALU.mult),
                waits=[("Tm", tm_rb)], inc="Vm")
            vv_rb = p.val("Vm")
            for gsub in range(4):
                g = 4 * j + gsub
                p.op(TEN, lambda e, j=j, gs=gsub: e.matmul(
                    s.PF[0:128, 0:1],
                    s.STATS[32:33, j * 512 + gs * 128: j * 512 + gs * 128 + 128],
                    s.IDENT[32:33, 32:33], is_transpose=True, start=True,
                    stop=True), waits=[("Vm", vv_rb), ("Vm", s._r2_free)],
                    inc="Tm")
                tm3 = p.val("Tm")
                p.op(VEC, lambda e, g=g: e.tensor_scalar(
                    s.R2TM[:, g:g + 1], s.PF[0:128, 0:1], 1.0,
                    None, ALU.mult), waits=[("Tm", tm3)], inc="Vm")
                s._r2_free = p.val("Vm")
        s.v_ln = p.val("Vm")
        s.s_ln = p.val("Sm")
        s.t_ln = p.val("Tm")
        s.g_ln = p.val("Gm")

    FF_R2CW = 0
    FF_R2SH = 16384
    FF_UP = 24576
    FF_DOWN = 32768
    FF_GATE = 38912

    def emit_router(self, xwaits=()):
        s, p = self, self.P
        gate_sb = s.p16(s.FF_GATE, 64)
        p.dma(GPS, gate_sb.rearrange("p (kc c) -> p kc c", c=8),
              s.gatewb.rearrange("(kc p) c -> p kc c", p=128),
              waits=[("dW", s.wcast_at["gate"])], inc="dG2")
        dgate = p.val("dG2")
        for j in range(NTT):
            q = s.PBIG[0:8, j * 512:(j + 1) * 512]
            xw = [("dG2", dgate), ("Vm", s._lnq_free[j])] + list(xwaits)
            for kc in range(NFC):
                xs = s.Xb[:, kc * TC + j * 512: kc * TC + j * 512 + 512]
                p.op(TEN, lambda e, q=q, xs=xs, kc=kc: e.matmul(
                    q, gate_sb[:, kc * 8:(kc + 1) * 8], xs,
                    start=(kc == 0), stop=False),
                    waits=(xw if kc == 0 else ()))
            p.op(TEN, lambda e, q=q, j=j: e.matmul(
                q, s.WSUMS[32:33, 2048:2056],
                s.MROWB[32:33, j * 512:(j + 1) * 512],
                start=False, stop=True),
                waits=[("Vm", s.v_wsum), ("Vm", s.v_ln)],
                inc="Tm")
            tm = p.val("Tm")
            p.op(VEC, lambda e, q=q, j=j: e.tensor_tensor(
                s.T1[64:72, j * 512:(j + 1) * 512], q,
                s.RB[0:8, j * 512:(j + 1) * 512], ALU.mult),
                waits=[("Tm", tm)], inc="Vm")
        vv = p.val("Vm")
        s._lnq_free = [p.val("Vm")] * NTT
        p.op(SCA, lambda e: e.activation(
            s.T1[64:72, :], s.T1[64:72, :],
            AF.Sigmoid), waits=[("Vm", vv)], inc="Sm")
        sv = p.val("Sm")
        RTM = self.p32(0, 256).rearrange("p (g k) -> p g k", k=16)
        tfree = [0, 0]
        for g in range(NTG):
            bank = s.PE if g % 2 == 0 else s.PF
            p.op(TEN, lambda e, g=g, bank=bank: e.matmul(
                bank[0:128, 0:8],
                s.T1[64:72, g * 128:(g + 1) * 128],
                s.IDENT[64:72, 64:72], is_transpose=True, start=True, stop=True),
                waits=[("Sm", sv), ("Vm", tfree[g % 2])], inc="Tm")
            tm = p.val("Tm")
            p.op(VEC, lambda e, g=g, bank=bank: e.tensor_scalar(
                self.p32(0, 256)[:, g * 16:g * 16 + 8], bank[0:128, 0:8],
                1.0, None, ALU.mult), waits=[("Tm", tm)], inc="Vm")
            p.op(VEC, lambda e, g=g: e.tensor_tensor(
                self.p32(0, 256)[:, g * 16 + 8:g * 16 + 16],
                self.p32(0, 256)[:, g * 16:g * 16 + 8],
                s.GATEB128[:, :], ALU.add), inc="Vm")
            tfree[g % 2] = p.val("Vm")
        SCR = 256

        def nxt(w):
            nonlocal SCR
            r = self.p32(SCR, w)
            SCR += w
            return r
        M1A = nxt(16); M2A = nxt(16); M1B = nxt(16); M2B = nxt(16)
        GSA = nxt(16); GSB = nxt(16); SELA = nxt(16); SELB = nxt(16)
        TMP64 = nxt(64)
        MSK = nxt(128); TMP128 = nxt(128)
        I1 = nxt(16); I2 = nxt(16); W1R = nxt(16); W2R = nxt(16)
        REC = nxt(16); SCL = nxt(16); IND1 = nxt(128); IND2 = nxt(128)
        IOTB4 = nxt(4); IOTB8 = nxt(8); CWTM = nxt(128)
        v_rt = p.val("Vm")
        V = lambda fn: p.op(VEC, fn, inc="Vm")

        def b16(ap, k):
            return ap.rearrange("p (g o) -> p g o", o=1).broadcast_to([128, 16, k])

        def g3(ap, k):
            return ap.rearrange("p (g k) -> p g k", k=k)

        V(lambda e: e.tensor_scalar(IOTB4, s.IOTA8[:, 0:4], 16.0, None, ALU.add))
        V(lambda e: e.tensor_scalar(IOTB8, s.IOTA8[:, 0:8], 16.0, None, ALU.add))
        sfc3 = g3(self.p32(0, 256), 16)
        for (lo, M1, M2, GS) in ((8, M1A, M2A, GSA), (12, M1B, M2B, GSB)):
            gsl = self.p32(0, 256).rearrange("p (g k) -> p g k", k=16)[:, :, lo:lo + 4]
            V(lambda e, gsl=gsl, M1=M1: e.tensor_reduce(
                M1, gsl, mybir.AxisListType.X, ALU.max))
            V(lambda e, gsl=gsl, M1=M1: e.tensor_tensor(
                g3(TMP64, 4), gsl, b16(M1, 4), ALU.is_equal))
            V(lambda e: e.scalar_tensor_tensor(
                g3(TMP64, 4), g3(TMP64, 4), -16.0,
                IOTB4.rearrange("p (o k) -> p o k", o=1).broadcast_to([128, 16, 4]),
                ALU.mult, ALU.add))
            V(lambda e, M2=M2: e.tensor_reduce(
                M2, g3(TMP64, 4), mybir.AxisListType.X, ALU.min))
            V(lambda e, M2=M2: e.tensor_tensor(
                g3(TMP64, 4),
                s.IOTA8[:, 0:4].rearrange("p (o k) -> p o k", o=1).broadcast_to([128, 16, 4]),
                b16(M2, 4), ALU.is_equal))
            V(lambda e, gsl=gsl: e.scalar_tensor_tensor(
                g3(TMP64, 4), g3(TMP64, 4), -1e9, gsl, ALU.mult, ALU.add))
            V(lambda e, M2=M2: e.tensor_reduce(
                M2, g3(TMP64, 4), mybir.AxisListType.X, ALU.max))
            V(lambda e, M1=M1, M2=M2, GS=GS: e.tensor_tensor(
                GS, M1, M2, ALU.add))
        V(lambda e: e.tensor_tensor(SELA, GSA, GSB, ALU.is_ge))
        V(lambda e: e.tensor_tensor(SELB, GSA, GSB, ALU.is_lt))
        V(lambda e: e.tensor_tensor(
            g3(MSK, 8)[:, :, 0:4], sfc3[:, :, 8:12], b16(SELA, 4), ALU.mult))
        V(lambda e: e.tensor_tensor(
            g3(MSK, 8)[:, :, 4:8], sfc3[:, :, 12:16], b16(SELB, 4), ALU.mult))
        V(lambda e: e.tensor_reduce(I1, g3(MSK, 8), mybir.AxisListType.X, ALU.max))
        V(lambda e: e.tensor_tensor(g3(TMP128, 8), g3(MSK, 8), b16(I1, 8), ALU.is_equal))
        V(lambda e: e.scalar_tensor_tensor(
            g3(TMP128, 8), g3(TMP128, 8), -16.0,
            IOTB8.rearrange("p (o k) -> p o k", o=1).broadcast_to([128, 16, 8]),
            ALU.mult, ALU.add))
        V(lambda e: e.tensor_reduce(I1, g3(TMP128, 8), mybir.AxisListType.X, ALU.min))
        V(lambda e: e.tensor_tensor(
            g3(IND1, 8),
            s.IOTA8.rearrange("p (o k) -> p o k", o=1).broadcast_to([128, 16, 8]),
            b16(I1, 8), ALU.is_equal))
        V(lambda e: e.scalar_tensor_tensor(
            g3(TMP128, 8), g3(IND1, 8), -1e9, g3(MSK, 8), ALU.mult, ALU.add))
        V(lambda e: e.tensor_reduce(I2, g3(TMP128, 8), mybir.AxisListType.X, ALU.max))
        V(lambda e: e.tensor_tensor(g3(MSK, 8), g3(TMP128, 8), b16(I2, 8), ALU.is_equal))
        V(lambda e: e.scalar_tensor_tensor(
            g3(MSK, 8), g3(MSK, 8), -16.0,
            IOTB8.rearrange("p (o k) -> p o k", o=1).broadcast_to([128, 16, 8]),
            ALU.mult, ALU.add))
        V(lambda e: e.tensor_reduce(I2, g3(MSK, 8), mybir.AxisListType.X, ALU.min))
        V(lambda e: e.tensor_tensor(
            g3(IND2, 8),
            s.IOTA8.rearrange("p (o k) -> p o k", o=1).broadcast_to([128, 16, 8]),
            b16(I2, 8), ALU.is_equal))
        V(lambda e: e.tensor_tensor(g3(TMP128, 8), g3(IND1, 8), sfc3[:, :, 0:8], ALU.mult))
        V(lambda e: e.tensor_reduce(W1R, g3(TMP128, 8), mybir.AxisListType.X, ALU.add))
        V(lambda e: e.tensor_tensor(g3(TMP128, 8), g3(IND2, 8), sfc3[:, :, 0:8], ALU.mult))
        V(lambda e: e.tensor_reduce(W2R, g3(TMP128, 8), mybir.AxisListType.X, ALU.add))
        V(lambda e: e.tensor_tensor(REC, W1R, W2R, ALU.add))
        V(lambda e: e.tensor_scalar(REC, REC, 1e-20, None, ALU.add))
        V(lambda e: e.reciprocal(REC, REC))
        V(lambda e: e.tensor_tensor(SCL, REC, s.R2TM[:, :], ALU.mult))
        V(lambda e: e.tensor_scalar(SCL, SCL, RSCALE, None, ALU.mult))
        V(lambda e: e.tensor_tensor(g3(CWTM, 8), g3(IND1, 8), b16(W1R, 8), ALU.mult))
        V(lambda e: e.tensor_tensor(g3(TMP128, 8), g3(IND2, 8), b16(W2R, 8), ALU.mult))
        V(lambda e: e.tensor_tensor(g3(CWTM, 8), g3(CWTM, 8), g3(TMP128, 8), ALU.add))
        V(lambda e: e.tensor_tensor(g3(CWTM, 8), g3(CWTM, 8), b16(SCL, 8), ALU.mult))
        vcw = p.val("Vm")
        tfree = [p.val("Vm"), p.val("Vm")]
        for g in range(NTG):
            bank = s.PE if g % 2 == 0 else s.PF
            p.op(TEN, lambda e, g=g, bank=bank: e.matmul(
                bank[0:8, 0:128], CWTM[:, g * 8:(g + 1) * 8],
                s.IDENT[0:128, 0:128], is_transpose=True, start=True,
                stop=True), waits=[("Vm", vcw), ("Vm", tfree[g % 2])],
                inc="Tm")
            tm = p.val("Tm")
            p.op(VEC, lambda e, g=g, bank=bank: e.tensor_scalar(
                s.T3[96:104, g * 128:(g + 1) * 128],
                bank[0:8, 0:128], 1.0, None, ALU.mult),
                waits=[("Tm", tm)], inc="Vm")
            tfree[g % 2] = p.val("Vm")
        s.v_router = p.val("Vm")

    def emit_ffn(self, stage):
        s, p = self, self.P
        up_free = [self.t_ln, self.t_ln]
        down_free = [0, 0]
        pbig_free = [p.val("Sm")] * 4
        cwb_free = [0, 0]
        dup = [0, 0]
        ddown = [0, 0]
        x_written = []
        for j in range(NTT):
            js = slice(j * 512, (j + 1) * 512)
            mrow = s.MROWB[0:1, j * 512:(j + 1) * 512]
            mrow32 = s.MROWB[32:33, j * 512:(j + 1) * 512]
            for scb in range(4):
                for sci in range(4):
                    sc = 4 * scb + sci
                    slot = sci % 2
                    buf = s.p16(s.FF_UP + slot * 4096, 1024)
                    p.dma(GPS, buf.rearrange("p (kc c) -> p kc c", c=128),
                          s.shupb.rearrange("(kc p) c -> p kc c", p=128)
                          [:, :, sc * 128:(sc + 1) * 128],
                          waits=[("dW", s.wcast_at["shup"]),
                                 ("Tm", up_free[slot])],
                          inc="dUp%d" % slot)
                    dup[slot] = p.val("dUp%d" % slot)
                    q = s.PBIG[:, sci * 512:(sci + 1) * 512]
                    for kc in range(NFC):
                        xs = s.Xb[:, kc * TC + j * 512: kc * TC + j * 512 + 512]
                        p.op(TEN, lambda e, q=q, b=buf, kc=kc, xs=xs: e.matmul(
                            q, b[:, kc * 128:(kc + 1) * 128], xs,
                            start=(kc == 0), stop=False),
                            waits=([("dUp%d" % slot, dup[slot]),
                                    ("Sm", pbig_free[sci]),
                                    ("Vm", s._ffn_q_free[sci])]
                                   if kc == 0 else ()))
                    p.op(TEN, lambda e, q=q, sc=sc, m32=mrow32: e.matmul(
                        q, s.WSUMS[32:33, sc * 128:(sc + 1) * 128],
                        m32, start=False, stop=True),
                        waits=[("Vm", s.v_wsum)],
                        inc="Tm")
                    up_free[slot] = p.val("Tm")
                tm = p.val("Tm")
                rsl = s.p16(s.FF_R2SH + scb * 2048, 2048)
                p.op(SCA, lambda e, rsl=rsl: e.activation(
                    rsl, s.PBIG[:, 0:2048], AF.Relu), waits=[("Tm", tm)],
                    inc="Sm")
                pbig_free = [p.val("Sm")] * 4
                sm = p.val("Sm")
                p.op(VEC, lambda e, rsl=rsl, j=j: e.tensor_tensor(
                    rsl.rearrange("p (sc c) -> p sc c", c=512),
                    rsl.rearrange("p (sc c) -> p sc c", c=512),
                    s.RB[:, j * 512:(j + 1) * 512]
                    .rearrange("p (o c) -> p o c", o=1)
                    .broadcast_to([128, 4, 512]), ALU.mult),
                    waits=[("Sm", sm)], inc="Vm")
                p.op(VEC, lambda e, rsl=rsl: e.tensor_tensor(
                    rsl, rsl, rsl, ALU.mult), inc="Vm")
            v_r2sh = p.val("Vm")
            for r_ in range(2):
                p.dma(SYN, s.STGF[32 * r_:32 * r_ + 1, 0:2048]
                      .rearrange("o (u c) -> o u c", c=512),
                      s.T3[96 + 4 * r_:100 + 4 * r_, js],
                      waits=[("Vm", s.v_router), ("Tm", s._stg_free)],
                      inc="dT")
            s._stg_dma = p.val("dT")
            for e_ in range(E):
                slot = e_ % 2
                buf = s.p16(s.FF_UP + slot * 4096, 4096)
                p.dma(GPS, buf.rearrange("p (kc c) -> p kc c", c=EI),
                      s.upb[e_].rearrange("(kc p) c -> p kc c", p=128),
                      waits=[("dW", s.wcast_at["up"]),
                             ("Tm", up_free[slot])], inc="dUp%d" % slot)
                dup[slot] = p.val("dUp%d" % slot)
                cbank = s.PE if e_ % 2 == 0 else s.PF
                p.op(TEN, lambda e, cbank=cbank, e_=e_: e.matmul(
                    cbank[:, 0:512],
                    s.ONES4[32 * (e_ // 4):32 * (e_ // 4) + 1, 0:128],
                    s.STGF[32 * (e_ // 4):32 * (e_ // 4) + 1,
                           (e_ % 4) * 512:(e_ % 4) * 512 + 512],
                    start=True, stop=True),
                    waits=[("dT", s._stg_dma), ("Vm", cwb_free[e_ % 2])],
                    inc="Tm")
                tcb = p.val("Tm")
                cb = s.p16(s.FF_GATE + 64 + (e_ % 2) * 512, 512)
                p.op(VEC, lambda e, cb=cb, cbank=cbank: e.tensor_scalar(
                    cb, cbank[:, 0:512], 1.0, None, ALU.mult),
                    waits=[("Tm", tcb)], inc="Vm")
                t_cwb = p.val("Vm")
                s._stg_free = p.val("Tm")
                for ic in range(4):
                    q = s.PBIG[:, ic * 512:(ic + 1) * 512]
                    for kc in range(NFC):
                        xs = s.Xb[:, kc * TC + j * 512: kc * TC + j * 512 + 512]
                        p.op(TEN, lambda e, q=q, b=buf, kc=kc, ic=ic, xs=xs:
                             e.matmul(q, b[:, kc * EI + ic * 128:
                                           kc * EI + ic * 128 + 128], xs,
                                      start=(kc == 0), stop=False),
                             waits=([("dUp%d" % slot, dup[slot]),
                                     ("Sm", pbig_free[ic]),
                                     ("Vm", s._ffn_q_free[ic])]
                                    if kc == 0 else ()))
                    p.op(TEN, lambda e, q=q, e_=e_, ic=ic, mrow=mrow: e.matmul(
                        q, s.WSUMS[0:1, e_ * 512 + ic * 128:
                                   e_ * 512 + ic * 128 + 128],
                        mrow, start=False, stop=True),
                        inc=("Tm" if ic == 3 else None))
                up_free[slot] = p.val("Tm")
                tm = p.val("Tm")
                rsl = s.p16(s.FF_R2CW + e_ * 2048, 2048)
                p.op(SCA, lambda e, rsl=rsl: e.activation(
                    rsl, s.PBIG[:, 0:2048], AF.Relu), waits=[("Tm", tm)],
                    inc="Sm")
                pbig_free = [p.val("Sm")] * 4
                sm = p.val("Sm")
                t1 = s.p32(0, 2048)
                p.op(VEC, lambda e, rsl=rsl, cb=cb: e.tensor_tensor(
                    t1.rearrange("p (ic c) -> p ic c", c=512),
                    rsl.rearrange("p (ic c) -> p ic c", c=512),
                    cb.rearrange("p (o c) -> p o c", o=1)
                    .broadcast_to([128, 4, 512]), ALU.mult),
                    waits=[("Sm", sm), ("Vm", t_cwb)], inc="Vm")
                cwb_free[e_ % 2] = p.val("Vm")
                p.op(VEC, lambda e, rsl=rsl, t1=t1: e.tensor_tensor(
                    rsl, t1, rsl, ALU.mult), inc="Vm")
            v_r2cw = p.val("Vm")
            down_free[1] = max(down_free[1], p.val("Tm"))
            for oc in range(NFC):
                slot = oc % 2
                dsem = "dDn" if slot == 0 else "dF0"
                dbuf = s.p16(s.FF_DOWN if slot == 0 else s.FF_UP, 6144)
                p.dma(GPS, dbuf[:, 0:4096].rearrange("p (u c) -> p u c", c=128),
                      s.downb[oc].rearrange("e i p c -> p (e i) c"),
                      waits=[("dW", s.wcast_at["down"]),
                             ("Tm", down_free[slot])], inc=dsem)
                p.dma(GPS, dbuf[:, 4096:6144].rearrange("p (u c) -> p u c", c=128),
                      s.shdownb[oc].rearrange("u p c -> p u c"),
                      waits=[("dW", s.wcast_at["shdown"])], inc=dsem)
                ddown[slot] = p.val(dsem)
                q = s.PBIG[:, (oc % 4) * 512:(oc % 4) * 512 + 512]
                first = True
                for e_ in range(E):
                    for ic in range(4):
                        rsl = s.p16(s.FF_R2CW + e_ * 2048 + ic * 512, 512)
                        w = []
                        if first:
                            w = [(dsem, ddown[slot]), ("Vm", v_r2cw),
                                 ("Vm", s._ffn_q_free[oc % 4])]
                        p.op(TEN, lambda e, q=q, d=dbuf, e_=e_, ic=ic, rsl=rsl:
                             e.matmul(q, d[:, (e_ * 4 + ic) * 128:
                                           (e_ * 4 + ic) * 128 + 128], rsl,
                                      start=(e_ == 0 and ic == 0), stop=False),
                             waits=w)
                        first = False
                for sc in range(16):
                    rsl = s.p16(s.FF_R2SH + sc * 512, 512)
                    p.op(TEN, lambda e, q=q, d=dbuf, sc=sc, rsl=rsl: e.matmul(
                        q, d[:, 4096 + sc * 128: 4096 + sc * 128 + 128], rsl,
                        start=False, stop=(sc == 15)),
                        waits=([("Vm", v_r2sh)] if sc == 0 else ()),
                        inc=("Tm" if sc == 15 else None))
                down_free[slot] = p.val("Tm")
                up_free = [p.val("Tm")] * 2
                tm = p.val("Tm")
                xs = s.Xb[:, oc * TC + j * 512: oc * TC + j * 512 + 512]
                if stage == 1:
                    ot = s.p16(s.FF_GATE + 1088, 512)
                    p.op(VEC, lambda e, q=q, xs=xs, ot=ot:
                         e.scalar_tensor_tensor(ot, q, 0.5, xs, ALU.mult,
                                                ALU.add),
                         waits=[("Tm", tm), ("dX", s._x1dma)], inc="Vm")
                    s._ffn_q_free[oc % 4] = p.val("Vm")
                    vq = p.val("Vm")
                    p.dma(SYN, s.x1_s[oc * 128:(oc + 1) * 128, js], ot,
                          waits=[("Vm", vq)], inc="dX")
                    s._x1dma = p.val("dX")
                else:
                    ot = s.YSTG[:, (oc % 2) * 512:(oc % 2) * 512 + 512]
                    p.op(VEC, lambda e, q=q, xs=xs, ot=ot:
                         e.scalar_tensor_tensor(ot, q, 0.5, xs, ALU.mult,
                                                ALU.add),
                         waits=[("Tm", tm), ("dX", s._ydma)], inc="Vm")
                    s._ffn_q_free[oc % 4] = p.val("Vm")
                    vq = p.val("Vm")
                    p.dma(SYN, s.y[oc * 128:(oc + 1) * 128, js], ot,
                          waits=[("Vm", vq)], inc="dX")
                    s._ydma = p.val("dX")
        s.v_ffn = p.val("Vm")
        s.d_ffn = p.val("dT")

    AT_W = 0
    AT_PRE = 16384
    AT_CONV = 32800
    AT_SQ = 36896
    AT_HALG = 38944

    def emit_attn_proj(self):
        s, p = self, self.P
        p.op(VEC, lambda e: e.memset(s.VSTK[:, :], 0.0), inc="Vm")
        p.op(VEC, lambda e: e.memset(s.T1[:, :], 0.0), inc="Vm")
        for fc in range(NFC):
            p.dma(GPS, s.Xb[:, fc * TC:(fc + 1) * TC],
                  s.x1_s[fc * 128:(fc + 1) * 128, :],
                  waits=[("dX", s._x1dma), ("Vm", s.v_ffn)], inc="dG2")
        s.c_x1 = p.val("dG2")
        self.emit_ln(xwaits=[("dG2", s.c_x1)])

        s._pbig_sfree = [p.val("Sm")] * NTT
        s._pbig_vfree = [p.val("Vm")] * NTT
        wslots = [s.p16(s.AT_W, 8192), s.p16(s.AT_W + 8192, 8192)]
        wsem = ["dUp0", "dUp1"]
        wfree = [p.val("Tm"), p.val("Tm")]
        dw = [0, 0]
        projs = [("q", s.wqb, 0, s.qhat_s), ("k", s.wkb, 1, s.ksil_s),
                 ("v", s.wvb, 2, s.vsil_s), ("g", s.wgb, 3, None)]
        pre = s.p16(s.AT_PRE, 16416)
        prev_spill = {0: 0, 1: 0}
        s._d_spill = 0
        s._v_sumsq = 0
        s._halg_free = 0
        conv_free = [0, 0]
        sumsq_rows = {"q": 32, "k": 64}
        for pi, (pname, wdram, pidx, spill) in enumerate(projs):
            slot = pi % 2
            wv = wslots[slot]
            p.dma(GPS, wv.rearrange("p (kc c) -> p kc c", c=DM),
                  wdram.rearrange("(kc p) c -> p kc c", p=128),
                  waits=[("dW", s.wcast_at["w" + pname]),
                         ("Tm", wfree[slot])], inc=wsem[slot])
            dw[slot] = p.val(wsem[slot])
            if pname in ("q", "k"):
                r0 = sumsq_rows[pname]
                p.op(VEC, lambda e, r0=r0: e.memset(s.T3[r0:r0 + 16, :], 0.0),
                     inc="Vm")
            for oc in range(NFC):
                for j in range(NTT):
                    q_ = s.PBIG[:, j * 512:(j + 1) * 512]
                    for kc in range(NFC):
                        xs = s.Xb[:, kc * TC + j * 512: kc * TC + j * 512 + 512]
                        p.op(TEN, lambda e, q=q_, w=wv, kc=kc, oc=oc, xs=xs:
                             e.matmul(q, w[:, kc * DM + oc * 128:
                                           kc * DM + oc * 128 + 128], xs,
                                      start=(kc == 0), stop=False),
                             waits=([("dG2", s.c_x1), (wsem[slot], dw[slot]),
                                     ("Sm", s._pbig_sfree[j]),
                                     ("Vm", s._pbig_vfree[j])]
                                    if kc == 0 else ()))
                    p.op(TEN, lambda e, q=q_, oc=oc, j=j, pidx=pidx: e.matmul(
                        q, s.WSUMS[64:65, pidx * DM + oc * 128:
                                   pidx * DM + oc * 128 + 128],
                        s.MROWB[64:65, j * 512:(j + 1) * 512],
                        start=False, stop=True),
                        waits=[("Vm", s.v_wsum)],
                        inc=("Tm" if j == NTT - 1 else None))
                tm = p.val("Tm")
                tbuf = s.p16(s.AT_SQ, 2048)
                p.op(VEC, lambda e, tbuf=tbuf: e.tensor_tensor(
                    tbuf.rearrange("p (j c) -> p j c", c=512),
                    s.PBIG.rearrange("p (j c) -> p j c", c=512),
                    s.RB.rearrange("p (j c) -> p j c", c=512),
                    ALU.mult), waits=[("Tm", tm), ("Vm", s.v_ln),
                                      ("Tm", s._sq_free)], inc="Vm")
                s._pbig_vfree = [p.val("Vm")] * NTT
                p.op(VEC, lambda e, tbuf=tbuf, oc=oc, pidx=pidx: e.tensor_scalar(
                    pre[:, oc * 2052 + 3: oc * 2052 + 2051],
                    tbuf[:, 0:2048], s.BQKVG[:, pidx * NFC + oc:
                                             pidx * NFC + oc + 1],
                    None, ALU.add), inc="Vm")
                s._pre_v = p.val("Vm")
            if pname == "g":
                dhalo = p.val("dT")
                for oc in range(NFC):
                    cbuf = s.p16(s.AT_CONV + (oc % 2) * 2048, 2048)
                    p.op(SCA, lambda e, cbuf=cbuf, oc=oc: e.activation(
                        cbuf, pre[:, oc * 2052 + 3: oc * 2052 + 2051],
                        AF.Silu), waits=[("Vm", s._pre_v),
                                         ("dSp%d" % (oc % 2),
                                          conv_free[oc % 2])], inc="Sm")
                    sv = p.val("Sm")
                    p.dma(SYN, s.og_s[oc * 128:(oc + 1) * 128, :], cbuf,
                          waits=[("Sm", sv)], inc="dSp%d" % (oc % 2))
                    conv_free[oc % 2] = p.val("dSp%d" % (oc % 2))
                wfree[slot] = p.val("Tm")
                continue
            for oc in range(NFC):
                p.dma(SYN, s.hal_in[oc * 128:(oc + 1) * 128, 0:3],
                      pre[:, oc * 2052 + 2048: oc * 2052 + 2051],
                      waits=[("Vm", s._pre_v), ("cc", s._cc_done)], inc="dHs")
            dstage = p.val("dHs")
            p.op(GPS, lambda e: e.collective_compute(
                "AllGather", ALU.bypass, replica_groups=[list(range(self.ncores))],
                ins=[s.hal_in.opt()], outs=[s.hal_out.opt()]),
                waits=[("dHs", dstage)], inc="cc")
            s._cc_done = p.val("cc")
            hal_flat = s.hal_out.rearrange("(r f) c -> r (f c)", r=self.ncores)
            for oc in range(NFC):
                hg = s.p16(s.AT_HALG, 512)
                p.dma(GPS, hg[0:self.ncores, 0:384], hal_flat[:, oc * 384:(oc + 1) * 384],
                      waits=[("cc", s._cc_done), ("Tm", s._halg_free)],
                      inc="dG2")
                dh = p.val("dG2")
                p.op(TEN, lambda e, hg=hg: e.matmul(
                    s.PE[0:1, 0:384], s.ONEHOT[0:self.ncores, 0:1],
                    hg[0:self.ncores, 0:384],
                    start=True, stop=True),
                    waits=[("dG2", dh), ("Vm", s._hrow_v)], inc="Tm")
                s._halg_free = p.val("Tm")
                th = p.val("Tm")
                hrow = s.p16(s.AT_HALG + 512, 512)
                p.op(VEC, lambda e, hrow=hrow: e.tensor_scalar(
                    hrow[0:1, 0:384], s.PE[0:1, 0:384], s.H0M[0:1, 0:1], None,
                    ALU.mult), waits=[("Tm", th), ("dHb", s._hbounce_d)],
                    inc="Vm")
                vh = p.val("Vm")
                s._hrow_v = vh
                p.dma(SYN, s.halo_b[oc], hrow[0:1, 0:384],
                      waits=[("Vm", vh)], inc="dHb")
                db = p.val("dHb")
                p.dma(SYN, pre[:, oc * 2052: oc * 2052 + 3],
                      s.halo_b[oc].rearrange("o (q c) -> (o q) c", c=3),
                      waits=[("dHb", db)], inc="dHb")
                s._hbounce_d = p.val("dHb")
            dhalo = p.val("dHb")
            for oc in range(NFC):
                cbuf = s.p16(s.AT_CONV + (oc % 2) * 2048, 2048)
                for tap in range(4):
                    wcol = s.CONVW[:, oc * 12 + pidx * 4 + tap:
                                   oc * 12 + pidx * 4 + tap + 1]
                    if tap == 0:
                        p.op(VEC, lambda e, cbuf=cbuf, oc=oc, w=wcol:
                             e.tensor_scalar(cbuf, pre[:, oc * 2052:
                                                       oc * 2052 + 2048],
                                             w, None, ALU.mult),
                             waits=[("dHb", dhalo),
                                    ("dSp%d" % (oc % 2), conv_free[oc % 2])],
                             inc="Vm")
                    else:
                        p.op(VEC, lambda e, cbuf=cbuf, oc=oc, tap=tap, w=wcol:
                             e.scalar_tensor_tensor(
                                 cbuf, pre[:, oc * 2052 + tap:
                                           oc * 2052 + 2048 + tap], w,
                                 cbuf, ALU.mult, ALU.add), inc="Vm")
                p.op(SCA, lambda e, cbuf=cbuf: e.activation(
                    cbuf, cbuf, AF.Silu), waits=[("Vm", p.val("Vm"))], inc="Sm")
                sv = p.val("Sm")
                p.dma(SYN, spill.rearrange("n (f p) c -> p n f c", p=128)
                      [:, :, oc, :], cbuf.rearrange("p (n c) -> p n c", c=CH),
                      waits=[("Sm", sv)], inc="dSp%d" % (oc % 2))
                if pname in ("q", "k"):
                    sq = s.p16(s.AT_SQ, 2048)
                    p.op(VEC, lambda e, cbuf=cbuf, sq=sq: e.tensor_tensor(
                        sq, cbuf, cbuf, ALU.mult),
                        waits=[("Sm", sv), ("Tm", s._sq_free)], inc="Vm")
                    vsq = p.val("Vm")
                    r0 = sumsq_rows[pname]
                    for j in range(NTT):
                        p.op(TEN, lambda e, sq=sq, j=j, oc=oc: e.matmul(
                            s.PF[0:16, 0:512],
                            s.SELSUM[:, oc * 16:(oc + 1) * 16],
                            sq[:, j * 512:(j + 1) * 512],
                            start=True, stop=True),
                            waits=[("Vm", vsq), ("Vm", s._ssq_v)], inc="Tm")
                        s._sq_free = p.val("Tm")
                        tms = p.val("Tm")
                        p.op(VEC, lambda e, r0=r0, j=j: e.tensor_tensor(
                            s.T3[r0:r0 + 16, j * 512:(j + 1) * 512],
                            s.T3[r0:r0 + 16, j * 512:(j + 1) * 512],
                            s.PF[0:16, 0:512], ALU.add),
                            waits=[("Tm", tms)], inc="Vm")
                        s._ssq_v = p.val("Vm")
                conv_free[oc % 2] = p.val("dSp%d" % (oc % 2))
            wfree[slot] = p.val("Tm")
        s.v_proj = p.val("Vm")

        wab_sb = s.p16(s.AT_W, 256)
        p.dma(GPS, wab_sb[:, 0:128].rearrange("p (kc c) -> p kc c", c=16),
              s.wab.rearrange("(kc p) c -> p kc c", p=128),
              waits=[("dW", s.wcast_at["wa"]), ("Tm", wfree[0])], inc="dUp0")
        p.dma(GPS, wab_sb[:, 128:256].rearrange("p (kc c) -> p kc c", c=16),
              s.wbb.rearrange("(kc p) c -> p kc c", p=128),
              waits=[("dW", s.wcast_at["wb"])], inc="dUp0")
        dab = p.val("dUp0")
        for ab in range(2):
            dst = s.T1[0:16, :] if ab == 0 else s.T1[32:48, :]
            for j in range(NTT):
                q_ = s.PE[0:16, 0:512]
                for kc in range(NFC):
                    xs = s.Xb[:, kc * TC + j * 512: kc * TC + j * 512 + 512]
                    p.op(TEN, lambda e, q=q_, kc=kc, ab=ab, xs=xs: e.matmul(
                        q, wab_sb[:, ab * 128 + kc * 16: ab * 128 + kc * 16 + 16],
                        xs, start=(kc == 0), stop=False),
                        waits=([("dUp0", dab), ("Vm", s._ab_v)] if kc == 0 else ()))
                p.op(TEN, lambda e, q=q_, ab=ab, j=j: e.matmul(
                    q, s.WSUMS[64:65, 4 * DM + ab * 16: 4 * DM + ab * 16 + 16],
                    s.MROWB[64:65, j * 512:(j + 1) * 512],
                    start=False, stop=True), inc="Tm")
                tm = p.val("Tm")
                p.op(VEC, lambda e, j=j: e.tensor_tensor(
                    s.T3[96:112, j * 512:(j + 1) * 512], q_,
                    s.RB[0:16, j * 512:(j + 1) * 512], ALU.mult),
                    waits=[("Tm", tm), ("Sm", s._ab_s)], inc="Vm")
                s._ab_v = p.val("Vm")
                vv = p.val("Vm")
                p.op(SCA, lambda e, dst=dst, j=j, ab=ab: e.activation(
                    dst[:, j * 512:(j + 1) * 512],
                    s.T3[96:112, j * 512:(j + 1) * 512], AF.Sigmoid,
                    bias=s.BAB[:, ab:ab + 1]), waits=[("Vm", vv)], inc="Sm")
                s._ab_s = p.val("Sm")
        s.s_ab = p.val("Sm")
        s._ab_tm = p.val("Tm")

        V = lambda fn, w=(): p.op(VEC, fn, waits=w, inc="Vm")
        S_ = lambda fn, w=(): p.op(SCA, fn, waits=w, inc="Sm")
        S_(lambda e: e.activation(s.T3[96:112, :], s.T1[0:16, :], AF.Ln),
           w=[("Sm", s.s_ab), ("Vm", p.val("Vm"))])
        sv = p.val("Sm")
        V(lambda e: e.tensor_scalar(s.T3[96:112, :], s.T3[96:112, :], -1.0,
                                    None, ALU.mult), w=[("Sm", sv)])
        for j_ in range(NTT):
            ini = (0.0 if j_ == 0 else
                   s.VSTK[96:112, j_ * 512 - 1: j_ * 512])
            V(lambda e, j_=j_, ini=ini: e.tensor_tensor_scan(
                s.VSTK[96:112, j_ * 512:(j_ + 1) * 512],
                s.T3[96:112, j_ * 512:(j_ + 1) * 512],
                s.T1[96:112, 0:512], ini, ALU.add, ALU.add),
              w=[("dT", s.c_consts)])
        S_(lambda e: e.activation(s.T3[32:48, :], s.T3[32:48, :], AF.Sqrt),
           w=[("Vm", s.v_proj)])
        sv = p.val("Sm")
        V(lambda e: e.tensor_scalar(s.T3[32:48, :], s.T3[32:48, :], 1e-6,
                                    None, ALU.add), w=[("Sm", sv)])
        S_(lambda e: e.activation(s.T3[32:48, :], s.T3[32:48, :], AF.Ln),
           w=[("Vm", p.val("Vm"))])
        sv = p.val("Sm")
        V(lambda e: e.tensor_scalar(s.VSTK[32:48, :], s.VSTK[96:112, :], 1.0,
                                    None, ALU.mult), w=[("Sm", sv)])
        V(lambda e: e.tensor_tensor(s.VSTK[64:80, :], s.T3[32:48, :],
                                    s.VSTK[32:48, :], ALU.add))
        V(lambda e: e.tensor_scalar(s.VSTK[64:80, :], s.VSTK[64:80, :], -1.0,
                                    None, ALU.mult))
        S_(lambda e: e.activation(s.T3[64:80, :], s.T3[64:80, :], AF.Sqrt),
           w=[("Vm", s.v_proj)])
        sv = p.val("Sm")
        V(lambda e: e.tensor_scalar(s.T3[64:80, :], s.T3[64:80, :], 1e-6,
                                    None, ALU.add), w=[("Sm", sv)])
        V(lambda e: e.reciprocal(s.T3[0:16, :], s.T3[64:80, :]))
        vv = p.val("Vm")
        S_(lambda e: e.activation(s.T3[64:80, :], s.T3[64:80, :], AF.Ln),
           w=[("Vm", vv)])
        s_lnsk = p.val("Sm")
        V(lambda e: e.tensor_scalar(s.T3[96:112, :], s.T3[64:80, :], 1.0,
                                    None, ALU.mult), w=[("Sm", s_lnsk)])
        V(lambda e: e.tensor_tensor(s.VSTK[0:16, :], s.VSTK[96:112, :],
                                    s.T3[96:112, :], ALU.subtract))
        S_(lambda e: e.activation(s.T1[96:112, :], s.T1[32:48, :], AF.Ln),
           w=[("Vm", p.val("Vm"))])
        s_lnb = p.val("Sm")
        V(lambda e: e.tensor_tensor(s.VSTK[32:48, :], s.T1[96:112, :],
                                    s.T3[96:112, :], ALU.subtract),
          w=[("Sm", s_lnb)])
        V(lambda e: e.tensor_scalar(s.T3[32:48, :], s.VSTK[96:112, :], 1.0,
                                    None, ALU.mult))
        V(lambda e: e.tensor_tensor(s.VSTK[32:48, :], s.VSTK[32:48, :],
                                    s.T3[32:48, :], ALU.subtract))
        s.v_rows = p.val("Vm")
        s.s_rows = p.val("Sm")
        s._qs_sp0 = p.val("dSp0")
        s._qs_sp1 = p.val("dSp1")

    def finish(self):
        p = self.P
        p.barrier_wait(SYN, "dT", p.val("dT"))
        p.barrier_wait(SYN, "dWs", p.val("dWs"))
        p.barrier_wait(SYN, "dUp0", p.val("dUp0"))
        p.barrier_wait(SYN, "dUp1", p.val("dUp1"))
        p.barrier_wait(GPS, "dG2", p.val("dG2"))
        p.barrier_wait(SYN, "dDn", p.val("dDn"))
        p.barrier_wait(SYN, "dX", p.val("dX"))
        p.barrier_wait(SYN, "dHs", p.val("dHs"))
        p.barrier_wait(SYN, "dHb", p.val("dHb"))
        p.barrier_wait(SYN, "dX10", p.val("dX10"))
        p.barrier_wait(SYN, "dX11", p.val("dX11"))
        p.barrier_wait(SYN, "dF0", p.val("dF0"))
        p.barrier_wait(SYN, "dF1", p.val("dF1"))
        p.barrier_wait(SYN, "dSp0", p.val("dSp0"))
        p.barrier_wait(SYN, "dSp1", p.val("dSp1"))
        p.barrier_wait(SYN, "dSl0", p.val("dSl0"))
        p.barrier_wait(SYN, "dSl1", p.val("dSl1"))
        p.barrier_wait(SYN, "dF0", p.val("dF0"))
        p.barrier_wait(SYN, "dF1", p.val("dF1"))
        p.barrier_wait(SYN, "dX10", p.val("dX10"))
        p.barrier_wait(SYN, "dX11", p.val("dX11"))
        p.barrier_wait(SYN, "dZ0", p.val("dZ0"))
        p.barrier_wait(SYN, "dZ1", p.val("dZ1"))
        p.barrier_wait(SYN, "dMc0", p.val("dMc0"))
        p.barrier_wait(SYN, "dMc1", p.val("dMc1"))
        p.barrier_wait(SYN, "dQ20", p.val("dQ20"))
        p.barrier_wait(SYN, "dQ21", p.val("dQ21"))
        p.barrier_wait(SYN, "dO10", p.val("dO10"))
        p.barrier_wait(SYN, "dO11", p.val("dO11"))
        p.barrier_wait(GPS, "dG", p.val("dG"))
        p.barrier_wait(GPS, "Gm", p.val("Gm"))
        p.barrier_wait(GPS, "dW", p.val("dW"))
        p.barrier_wait(TEN, "Tm", p.val("Tm"))
        p.barrier_wait(VEC, "Vm", p.val("Vm"))
        p.barrier_wait(SCA, "Sm", p.val("Sm"))

    def build(self):
        with self.nc.Block() as block:
            self.P.replay(block)
        self.st.close()
        return self.nc


    SC_SLAB = 0
    SC_WS = 6144
    SC_KBAR = 14336
    SC_MC = 14848
    SC_Q2 = 15360
    SC_Z = 15872
    SC_O1 = 16896

    def emit_scan(self):
        s, p = self, self.P
        p.op(VEC, lambda e: e.memset(s.SLOC[:, :], 0.0), inc="Vm")
        for h in range(NH):
            p.op(VEC, lambda e, h=h: e.tensor_scalar(
                s.MCH[:, h * HD:(h + 1) * HD], s.IDENTB[0:64, 0:64], 1.0,
                None, ALU.mult), waits=[("dT", s.c_consts)], inc="Vm")
        slab_free = [s._ab_tm, s._ab_tm]
        dslab = [0, 0]
        ws_free = [s.v_rows, s.v_rows]
        stg_free = 0
        gc_free = 0
        v_chain = p.val("Vm")
        t_chain = 0
        pe_free = s._ab_v
        pf_free = 0
        pg_free = 0
        pt_free = 0
        pbig_free = [p.val("Vm")] * 4
        spl = {"z": [0, 0], "mc": [0, 0], "q2": [0, 0], "o1": [0, 0]}
        for c in range(NCH):
            cs = slice(c * CH, (c + 1) * CH)
            par = c % 2
            slab = s.p16(s.SC_SLAB + par * 3072, 3072)
            for i_, src in enumerate((s.qhat_s, s.ksil_s, s.vsil_s)):
                p.dma(SYN, slab[:, i_ * 1024:(i_ + 1) * 1024]
                      .rearrange("p (f c) -> p f c", c=CH),
                      src[c].rearrange("(f p) c -> p f c", p=128),
                      waits=[("dSp0", s._qs_sp0), ("dSp1", s._qs_sp1),
                             ("Tm", slab_free[par])],
                      inc="dSl%d" % par)
            dslab[par] = p.val("dSl%d" % par)
            p.dma(SYN, s.STGF[0:1, 0:2048].rearrange("o (u c) -> o u c", c=CH),
                  s.VSTK[32:48, cs], waits=[("Vm", s.v_rows),
                                            ("Tm", stg_free)], inc="dT")
            p.dma(SYN, s.STATS[0:1, 0:2048].rearrange("o (u c) -> o u c", c=CH),
                  s.VSTK[64:80, cs], inc="dT")
            if c == 0:
                p.dma(SYN, s.STGF[64:65, 0:16], s.ZROW[:, 0:1], inc="dT")
            else:
                p.dma(SYN, s.STGF[64:65, 0:16],
                      s.VSTK[96:112, c * CH - 1: c * CH], inc="dT")
            p.dma(SYN, s.STGF[64:65, 16:32],
                  s.VSTK[96:112, c * CH + CH - 1: c * CH + CH], inc="dT")
            dstg = p.val("dT")
            CT = s.p32(0, 128)
            CT2 = s.p32(128, 128)
            p.op(TEN, lambda e, cs=cs: e.matmul(
                s.PF[0:128, 0:128], s.VSTK[:, cs], s.IDENT[:, :],
                is_transpose=True, start=True, stop=True),
                waits=[("Vm", s.v_rows), ("Vm", gc_free), ("Vm", pf_free)],
                inc="Tm")
            tm = p.val("Tm")
            p.op(VEC, lambda e, CT=CT: e.tensor_scalar(
                CT, s.PF[0:128, 0:128], 1.0, None, ALU.mult),
                waits=[("Tm", tm)], inc="Vm")
            p.op(TEN, lambda e, cs=cs: e.matmul(
                s.PF[0:128, 0:128], s.T1[:, cs], s.IDENT[:, :],
                is_transpose=True, start=True, stop=True),
                waits=[("Vm", p.val("Vm"))], inc="Tm")
            tm = p.val("Tm")
            p.op(VEC, lambda e, CT2=CT2: e.tensor_scalar(
                CT2, s.PF[0:128, 0:128], 1.0, None, ALU.mult),
                waits=[("Tm", tm)], inc="Vm")
            pf_free = p.val("Vm")
            vct = p.val("Vm")
            p.op(TEN, lambda e: e.matmul(
                s.PE[0:128, 0:16], s.ONES4[64:65, 0:128], s.STGF[64:65, 0:16],
                start=True, stop=True),
                waits=[("dT", dstg), ("Vm", pe_free)], inc="Tm")
            p.op(TEN, lambda e: e.matmul(
                s.PE[0:128, 16:32], s.ONES4[64:65, 0:128],
                s.STGF[64:65, 16:32], start=True, stop=True), inc="Tm")
            tm = p.val("Tm")
            GC = s.GCOLS
            p.op(VEC, lambda e, CT=CT: e.tensor_tensor(
                GC[:, 0:16], s.PE[0:128, 0:16], CT[:, 96:112], ALU.subtract),
                waits=[("Tm", tm), ("Vm", vct), ("Vm", gc_free)], inc="Vm")
            p.op(VEC, lambda e, CT=CT: e.tensor_tensor(
                GC[:, 16:32], CT[:, 64:80], s.PE[0:128, 0:16], ALU.add),
                inc="Vm")
            p.op(VEC, lambda e, CT=CT: e.tensor_tensor(
                GC[:, 32:48], CT[:, 0:16], s.PE[0:128, 16:32], ALU.subtract),
                inc="Vm")
            p.op(VEC, lambda e, CT=CT: e.tensor_tensor(
                GC[:, 48:64], CT[:, 32:48], s.PE[0:128, 0:16], ALU.add),
                inc="Vm")
            vv = p.val("Vm")
            p.op(SCA, lambda e, GC=GC: e.activation(GC[:, :], GC[:, :], AF.Exp),
                 waits=[("Vm", vv)], inc="Sm")
            sgc = p.val("Sm")
            p.op(VEC, lambda e: e.tensor_tensor(
                s.STGF[64:65, 32:48], s.STGF[64:65, 0:16],
                s.STGF[64:65, 16:32], ALU.subtract),
                waits=[("dT", dstg)], inc="Vm")
            vv = p.val("Vm")
            p.op(SCA, lambda e: e.activation(
                s.STGF[64:65, 32:48], s.STGF[64:65, 32:48], AF.Exp),
                waits=[("Vm", vv)], inc="Sm")
            sv = p.val("Sm")
            p.op(TEN, lambda e: e.matmul(
                s.PE[0:64, 32:48], s.ONES4[64:65, 0:64], s.STGF[64:65, 32:48],
                start=True, stop=True), waits=[("Sm", sv)], inc="Tm")
            tm = p.val("Tm")
            p.op(VEC, lambda e: e.tensor_scalar(
                s.GCREP[:, :], s.PE[0:64, 32:48], 1.0, None, ALU.mult),
                waits=[("Tm", tm)], inc="Vm")
            pe_free = p.val("Vm")
            v_cols = p.val("Vm")
            for hq in range(4):
                wpar = (c * 4 + hq) % 2
                ws = s.p16(s.SC_WS + wpar * 4096, 4096)
                W1A = ws[:, 0:1024]
                PB_ = ws[:, 1024:2048]
                QB_ = ws[:, 2048:3072]
                RB_ = ws[:, 3072:3584]
                heads = [4 * hq + i for i in range(4)]
                stage_lim = getattr(self, "scan_stage", 99)
                if stage_lim == 0:
                    continue

                def hslabs(h):
                    base = 64 * (h % 2)
                    fc = h // 2
                    qsl = slab[base:base + 64, fc * 128: fc * 128 + 128]
                    ksl = slab[base:base + 64, 1024 + fc * 128:
                               1024 + fc * 128 + 128]
                    vsl = slab[base:base + 64, 2048 + fc * 128:
                               2048 + fc * 128 + 128]
                    return qsl, ksl, vsl

                for i, h in enumerate(heads):
                    qsl, ksl, _ = hslabs(h)
                    qq = s.PBIG[:, i * 512:(i + 1) * 512]
                    p.op(TEN, lambda e, qq=qq, ksl=ksl: e.matmul(
                        qq[:, 0:128], ksl, ksl, start=True, stop=True),
                        waits=[("dSl%d" % par, dslab[par]),
                               ("Vm", pbig_free[i])])
                    if stage_lim == 10:
                        p.op(TEN, lambda e, qq=qq, ksl=ksl, qsl=qsl: e.matmul(
                            qq[:, 128:256], ksl, qsl, start=True, stop=True),
                            inc="Tm")
                        continue
                    p.op(TEN, lambda e, qq=qq, ksl=ksl, qsl=qsl: e.matmul(
                        qq[:, 128:256], ksl, qsl, start=True, stop=True))
                    p.op(TEN, lambda e, qq=qq, h=h: e.matmul(
                        qq[:, 256:384], s.ONES4[0:1, 0:128],
                        s.STGF[0:1, h * 128: h * 128 + 128],
                        start=True, stop=True), waits=[("dT", dstg)])
                    p.op(TEN, lambda e, qq=qq, h=h: e.matmul(
                        qq[:, 384:512], s.ONES4[0:1, 0:128],
                        s.STATS[0:1, h * 128: h * 128 + 128],
                        start=True, stop=True), inc="Tm")
                stg_free = p.val("Tm")
                tgram = p.val("Tm")
                if stage_lim in (10, 11):
                    continue
                EArg = s.p32(256, 1024)
                for i, h in enumerate(heads):
                    p.op(VEC, lambda e, i=i, h=h, CT=CT: e.tensor_scalar(
                        EArg[:, i * 256:(i + 1) * 256],
                        s.PBIG[:, i * 512 + 256: i * 512 + 512],
                        CT[:, h:h + 1], None, ALU.add),
                        waits=[("Tm", tgram), ("Sm", sgc)], inc="Vm")
                p.op(VEC, lambda e, EArg=EArg: e.tensor_tensor(
                    EArg.rearrange("p (i c) -> p i c", c=256),
                    EArg.rearrange("p (i c) -> p i c", c=256),
                    s.TRI2.rearrange("p (o c) -> p o c", o=1)
                    .broadcast_to([128, 4, 256]), ALU.add), inc="Vm")
                vv = p.val("Vm")
                p.op(SCA, lambda e, EArg=EArg: e.activation(
                    EArg, EArg, AF.Exp), waits=[("Vm", vv)], inc="Sm")
                se = p.val("Sm")
                if stage_lim == 12:
                    continue
                p.op(VEC, lambda e, W1A=W1A, EArg=EArg: e.tensor_tensor(
                    W1A.rearrange("p (i c) -> p i c", c=256),
                    s.PBIG.rearrange("p (i c) -> p i c", c=512)[:, :, 0:256],
                    EArg.rearrange("p (i c) -> p i c", c=256), ALU.mult),
                    waits=[("Sm", se), ("Vm", ws_free[wpar])], inc="Vm")
                vW1A = p.val("Vm")
                pbig_free = [p.val("Vm")] * 4
                if stage_lim == 1:
                    continue
                for f2 in range(2):
                    fc = heads[2 * f2] // 2
                    vfull = slab[0:128, 2048 + fc * 128: 2048 + fc * 128 + 128]
                    kfull = slab[0:128, 1024 + fc * 128: 1024 + fc * 128 + 128]
                    p.op(TEN, lambda e, f2=f2, vfull=vfull: e.matmul(
                        s.PT[:, f2 * 256: f2 * 256 + 128], vfull,
                        s.IDENTB[:, :], is_transpose=True, start=True,
                        stop=True),
                        waits=([("Vm", pt_free)] if f2 == 0 else ()))
                    p.op(TEN, lambda e, f2=f2, kfull=kfull: e.matmul(
                        s.PT[:, f2 * 256 + 128: f2 * 256 + 256], kfull,
                        s.IDENTB[:, :], is_transpose=True, start=True,
                        stop=True),
                        inc=("Tm" if f2 == 1 else None))
                tpt = p.val("Tm")
                kbar = s.p16(s.SC_KBAR + wpar * 256, 256)
                for i, h in enumerate(heads):
                    vcol = (i // 2) * 256 + (h % 2) * 64
                    kcol = (i // 2) * 256 + 128 + (h % 2) * 64
                    p.op(VEC, lambda e, i=i, h=h, RB_=RB_, CT2=CT2, vcol=vcol:
                         e.tensor_scalar(
                             RB_[:, i * 128: i * 128 + 64],
                             s.PT[:, vcol: vcol + 64],
                             CT2[:, 32 + h: 33 + h], None, ALU.mult),
                         waits=[("Tm", tpt)] if i == 0 else (), inc="Vm")
                    p.op(VEC, lambda e, i=i, h=h, RB_=RB_, kcol=kcol:
                         e.tensor_scalar(
                             RB_[:, i * 128 + 64: i * 128 + 128],
                             s.PT[:, kcol: kcol + 64],
                             s.GCOLS[:, 48 + h: 49 + h], -1.0, ALU.mult,
                             ALU.mult),
                         inc="Vm")
                    p.op(VEC, lambda e, i=i, h=h, kbar=kbar, kcol=kcol:
                         e.tensor_scalar(
                             kbar[:, i * 64:(i + 1) * 64],
                             s.PT[:, kcol: kcol + 64],
                             s.GCOLS[:, 32 + h: 33 + h], None, ALU.mult),
                         inc="Vm")
                pt_free = p.val("Vm")
                vR0 = p.val("Vm")
                for i in range(4):
                    p.op(TEN, lambda e, i=i, W1A=W1A: e.matmul(
                        s.PT[:, i * 128:(i + 1) * 128],
                        W1A[:, i * 256: i * 256 + 128], s.IDENTB[:, :],
                        is_transpose=True, start=True, stop=True),
                        waits=([("Vm", vW1A), ("Vm", pt_free)] if i == 0
                               else ()),
                        inc=("Tm" if i == 3 else None))
                tq0 = p.val("Tm")
                p.op(VEC, lambda e, QB_=QB_: e.tensor_scalar(
                    QB_[:, 0:512], s.PT[:, 0:1024:2][:, 0:512] if False else
                    s.PT[:, 0:512], 1.0, None, ALU.mult),
                    waits=[("Tm", tq0)], inc="Vm")
                pt_free = p.val("Vm")
                vq0 = p.val("Vm")
                if stage_lim == 2:
                    continue
                vr = vR0
                vpq = vq0
                sq_wait = 0
                pg_free_sm = 0
                for j in range(7):
                    for i in range(4):
                        lsl = (W1A[:, i * 256: i * 256 + 128] if j == 0 else
                               PB_[:, (j % 2) * 512 + i * 128:
                                   (j % 2) * 512 + i * 128 + 128])
                        p.op(TEN, lambda e, i=i, lsl=lsl, RB_=RB_: e.matmul(
                            s.PE[:, i * 128:(i + 1) * 128], lsl,
                            RB_[:, i * 128:(i + 1) * 128],
                            start=True, stop=True),
                            waits=([("Vm", vr), ("Vm", vpq),
                                    ("Vm", pe_free)] if i == 0 else ()),
                            inc=("Tm" if i == 3 else None))
                    ta = p.val("Tm")
                    op_ = ALU.subtract if j == 0 else ALU.add
                    p.op(VEC, lambda e, RB_=RB_, op_=op_: e.tensor_tensor(
                        RB_, RB_, s.PE[:, 0:512], op_),
                        waits=[("Tm", ta)], inc="Vm")
                    vr = p.val("Vm")
                    pe_free = vr
                    if j < 6:
                        for i in range(4):
                            psl = (W1A[:, i * 256: i * 256 + 128] if j == 0
                                   else PB_[:, (j % 2) * 512 + i * 128:
                                            (j % 2) * 512 + i * 128 + 128])
                            qsl2 = QB_[:, (j % 2) * 512 + i * 128:
                                       (j % 2) * 512 + i * 128 + 128]
                            p.op(TEN, lambda e, i=i, psl=psl, qsl2=qsl2:
                                 e.matmul(s.PF[:, i * 128:(i + 1) * 128],
                                          qsl2, psl, start=True, stop=True),
                                 waits=([("Vm", pf_free), ("Sm", sq_wait)]
                                        if i == 0 else ()),
                                 inc=("Tm" if (i == 3 and j == 5) else None))
                            if j < 5:
                                p.op(TEN, lambda e, i=i, psl=psl, qsl2=qsl2:
                                     e.matmul(s.PG[:, i * 128:(i + 1) * 128],
                                              psl, qsl2, start=True,
                                              stop=True),
                                     waits=([("Vm", pg_free)] if i == 0
                                            else ()),
                                     inc=("Tm" if i == 3 else None))
                        tsq = p.val("Tm")
                        np_ = PB_[:, ((j + 1) % 2) * 512:
                                  ((j + 1) % 2) * 512 + 512]
                        p.op(VEC, lambda e, np_=np_: e.tensor_scalar(
                            np_, s.PF[:, 0:512], 1.0, None, ALU.mult),
                            waits=[("Tm", tsq)], inc="Vm")
                        pf_free = p.val("Vm")
                        vpq = p.val("Vm")
                        if j < 5:
                            nq_ = QB_[:, ((j + 1) % 2) * 512:
                                      ((j + 1) % 2) * 512 + 512]
                            p.op(SCA, lambda e, nq_=nq_: e.activation(
                                nq_, s.PG[:, 0:512], AF.Identity),
                                waits=[("Tm", tsq)], inc="Sm")
                            sq_wait = p.val("Sm")
                            pg_free_sm = sq_wait
                if stage_lim == 3:
                    continue
                for f2 in range(2):
                    fc = heads[2 * f2] // 2
                    qfull = slab[0:128, fc * 128: fc * 128 + 128]
                    p.op(TEN, lambda e, f2=f2, qfull=qfull: e.matmul(
                        s.PT[:, f2 * 128:(f2 + 1) * 128], qfull,
                        s.IDENTB[:, :], is_transpose=True, start=True,
                        stop=True),
                        waits=([("Vm", pt_free)] if f2 == 0 else ()),
                        inc=("Tm" if f2 == 1 else None))
                tqz = p.val("Tm")
                GQ = s.p32(1280, 256)
                for i, h in enumerate(heads):
                    qcol = (i // 2) * 128 + (h % 2) * 64
                    p.op(VEC, lambda e, i=i, h=h, GQ=GQ, qcol=qcol:
                         e.tensor_scalar(
                             GQ[:, i * 64:(i + 1) * 64],
                             s.PT[:, qcol: qcol + 64],
                             s.GCOLS[:, 16 + h: 17 + h], None, ALU.mult),
                         waits=([("Tm", tqz)] if i == 0 else ()), inc="Vm")
                pt_free = p.val("Vm")
                vgq = p.val("Vm")
                for i in range(4):
                    p.op(TEN, lambda e, i=i, GQ=GQ: e.matmul(
                        s.PG[0:64, i * 128:(i + 1) * 128], GQ[:, i * 64:(i + 1) * 64],
                        s.IDENT[:, :], is_transpose=True, start=True,
                        stop=False), waits=([("Vm", vgq), ("Vm", pg_free),
                                             ("Sm", pg_free_sm)]
                                            if i == 0 else ()))
                    p.op(TEN, lambda e, i=i, RB_=RB_, W1A=W1A: e.matmul(
                        s.PG[0:64, i * 128:(i + 1) * 128],
                        RB_[:, i * 128 + 64: i * 128 + 128],
                        W1A[:, i * 256 + 128: i * 256 + 256],
                        start=False, stop=True),
                        inc=("Tm" if i == 3 else None))
                tz = p.val("Tm")
                zb = s.p16(s.SC_Z + wpar * 512, 512)
                p.op(VEC, lambda e, zb=zb: e.tensor_scalar(
                    zb[0:64, :], s.PG[0:64, 0:512], 1.0, None, ALU.mult),
                    waits=[("Tm", tz), ("dZ%d" % wpar, spl["z"][wpar])],
                    inc="Vm")
                vz = p.val("Vm")
                pg_free = vz
                p.dma(SYN, s.z_s[c, 4 * hq:4 * hq + 4]
                      .rearrange("i p c -> p i c"),
                      zb[0:64, :].rearrange("p (i c) -> p i c", c=CH),
                      waits=[("Vm", vz)], inc="dZ%d" % wpar)
                spl["z"][wpar] = p.val("dZ%d" % wpar)
                if stage_lim == 4:
                    continue
                for i in range(4):
                    p.op(TEN, lambda e, i=i, RB_=RB_, kbar=kbar: e.matmul(
                        s.PE[0:64, i * 64 + 0:(i + 1) * 64],
                        RB_[:, i * 128 + 64: i * 128 + 128],
                        kbar[:, i * 64:(i + 1) * 64],
                        start=True, stop=True),
                        waits=([("Vm", pe_free)] if i == 0 else ()))
                    p.op(TEN, lambda e, i=i, RB_=RB_, kbar=kbar: e.matmul(
                        s.PE[0:64, 256 + i * 64: 256 + (i + 1) * 64],
                        kbar[:, i * 64:(i + 1) * 64],
                        RB_[:, i * 128: i * 128 + 64],
                        start=True, stop=True),
                        inc=("Tm" if i == 3 else None))
                tmc = p.val("Tm")
                mcb = s.p16(s.SC_MC + wpar * 256, 256)
                q2b = s.p16(s.SC_Q2 + wpar * 256, 256)
                for i, h in enumerate(heads):
                    p.op(VEC, lambda e, i=i, h=h, mcb=mcb:
                         e.scalar_tensor_tensor(
                             mcb[0:64, i * 64:(i + 1) * 64],
                             s.IDENTB[0:64, 0:64], s.GCREP[:, h:h + 1],
                             s.PE[0:64, i * 64:(i + 1) * 64],
                             ALU.mult, ALU.add),
                         waits=([("Tm", tmc), ("Vm", v_cols),
                                 ("dMc%d" % wpar, spl["mc"][wpar])]
                                if i == 0 else ()),
                         inc="Vm")
                p.op(VEC, lambda e, q2b=q2b: e.tensor_scalar(
                    q2b[0:64, :], s.PE[0:64, 256:512], 1.0, None, ALU.mult),
                    waits=[("dQ2%d" % wpar, spl["q2"][wpar])], inc="Vm")
                vmc = p.val("Vm")
                pe_free = vmc
                p.dma(SYN, s.mc_s[c, 4 * hq:4 * hq + 4]
                      .rearrange("i p c -> p i c"),
                      mcb[0:64, :].rearrange("p (i c) -> p i c", c=HD),
                      waits=[("Vm", vmc)], inc="dMc%d" % wpar)
                spl["mc"][wpar] = p.val("dMc%d" % wpar)
                p.dma(SYN, s.q2_s[c, 4 * hq:4 * hq + 4]
                      .rearrange("i p c -> p i c"),
                      q2b[0:64, :].rearrange("p (i c) -> p i c", c=HD),
                      waits=[("Vm", vmc)], inc="dQ2%d" % wpar)
                spl["q2"][wpar] = p.val("dQ2%d" % wpar)
                if stage_lim == 5:
                    continue
                for i in range(4):
                    p.op(TEN, lambda e, i=i, W1A=W1A, RB_=RB_: e.matmul(
                        s.PF[:, i * 64: i * 64 + 64],
                        W1A[:, i * 256 + 128: i * 256 + 256],
                        RB_[:, i * 128: i * 128 + 64],
                        start=True, stop=True),
                        waits=([("Vm", pf_free)] if i == 0 else ()),
                        inc=("Tm" if i == 3 else None))
                to1 = p.val("Tm")
                o1b = s.p16(s.SC_O1 + wpar * 1024, 1024)
                p.op(VEC, lambda e, o1b=o1b: e.tensor_scalar(
                    o1b[:, 0:256], s.PF[:, 0:256], 1.0, None, ALU.mult),
                    waits=[("Tm", to1), ("dO1%d" % wpar, spl["o1"][wpar])],
                    inc="Vm")
                vo1 = p.val("Vm")
                pf_free = vo1
                p.dma(SYN, s.o1_s[c, 4 * hq:4 * hq + 4]
                      .rearrange("i p c -> p i c"),
                      o1b[:, 0:256].rearrange("p (i c) -> p i c", c=HD),
                      waits=[("Vm", vo1)], inc="dO1%d" % wpar)
                spl["o1"][wpar] = p.val("dO1%d" % wpar)
                for i, h in enumerate(heads):
                    p.op(TEN, lambda e, i=i, h=h, mcb=mcb: e.matmul(
                        s.PG[0:64, i * 64:(i + 1) * 64],
                        mcb[0:64, i * 64:(i + 1) * 64],
                        s.SLOC[:, h * HD:(h + 1) * HD],
                        start=True, stop=True),
                        waits=([("Vm", vmc), ("Vm", v_chain),
                                ("Vm", pg_free)] if i == 0 else ()))
                    p.op(TEN, lambda e, i=i, h=h, mcb=mcb: e.matmul(
                        s.PG[0:64, 256 + i * 64: 256 + (i + 1) * 64],
                        mcb[0:64, i * 64:(i + 1) * 64],
                        s.MCH[:, h * HD:(h + 1) * HD],
                        start=True, stop=True),
                        inc=("Tm" if i == 3 else None))
                tch = p.val("Tm")
                p.op(VEC, lambda e, q2b=q2b, hq=hq: e.tensor_tensor(
                    s.SLOC[:, 4 * hq * HD: (4 * hq + 4) * HD],
                    q2b[0:64, :], s.PG[0:64, 0:256], ALU.add),
                    waits=[("Tm", tch)], inc="Vm")
                p.op(VEC, lambda e, hq=hq: e.tensor_scalar(
                    s.MCH[:, 4 * hq * HD: (4 * hq + 4) * HD],
                    s.PG[0:64, 256:512], 1.0, None, ALU.mult), inc="Vm")
                v_chain = p.val("Vm")
                pg_free = v_chain
                ws_free[wpar] = p.val("Vm")
            slab_free[par] = p.val("Tm")
            gc_free = p.val("Vm")
        s.t_scan = p.val("Tm")
        s.v_scan = p.val("Vm")
        s.spl = spl

    P2_OFM = 0
    P2_MC = 16384
    P2_Q2 = 18432
    P2_Z = 20480
    P2_O1 = 24576
    P2_FG = 26624
    P2_MT = 30720
    EN_WO = 16384
    EN_OG = 33280
    EN_XWO = 34304
    EN_X1 = 38400

    def emit_fold(self):
        s, p = self, self.P
        p.dma(SYN, s.ag2_in.rearrange("p (h two c) -> p h two c", two=2, c=HD)
              [:, :, 0, :], s.MCH[:, :].rearrange("p (h c) -> p h c", c=HD),
              waits=[("Vm", s.v_scan)], inc="dT")
        p.dma(SYN, s.ag2_in.rearrange("p (h two c) -> p h two c", two=2, c=HD)
              [:, :, 1, :], s.SLOC[:, :].rearrange("p (h c) -> p h c", c=HD),
              inc="dT")
        d2 = p.val("dT")
        p.op(GPS, lambda e: e.collective_compute(
            "AllGather", ALU.bypass, replica_groups=[list(range(self.ncores))],
            ins=[s.ag2_in.opt()], outs=[s.ag2_out.opt()]),
            waits=[("dT", d2), ("cc", s._cc_done)], inc="cc")
        s._cc_done = p.val("cc")
        p.op(VEC, lambda e: e.memset(s.SS[:, :], 0.0), inc="Vm")
        fg_d = [0, 0]
        fg_free = [0, 0]
        fg_vfree = [0, 0]
        v_ss = p.val("Vm")
        pe_free = 0
        mt_tmfree = 0
        for j in range(min(7, self.ncores - 1) if self.ncores > 1 else 0):
            par = j % 2
            fg = s.p16(s.P2_FG + par * 2048, 2048)
            p.dma(SYN, fg[0:64, :].rearrange("p (h c) -> p h c", c=2 * HD),
                  s.ag2_out[j * HD:(j + 1) * HD, :]
                  .rearrange("p (h c) -> p h c", c=2 * HD),
                  waits=[("cc", s._cc_done), ("Tm", fg_free[par]),
                         ("Vm", fg_vfree[par])],
                  inc="dF%d" % par)
            fg_d[par] = p.val("dF%d" % par)
            mt = s.p16(s.P2_MT, 1024)
            p.op(VEC, lambda e, fg=fg, j=j: e.scalar_tensor_tensor(
                mt[0:64, :].rearrange("p (h c) -> p h c", c=HD),
                fg[0:64, :].rearrange("p (h c) -> p h c", c=2 * HD)[:, :, 0:HD],
                s.FOLDA[:, j:j + 1],
                s.FOLDB.rearrange("p (j c) -> p j c", c=HD)
                [:, j:j + 1, :].broadcast_to([64, NH, HD]),
                ALU.mult, ALU.add),
                waits=[("dF%d" % par, fg_d[par]), ("Tm", mt_tmfree)],
                inc="Vm")
            vmt = p.val("Vm")
            for h in range(NH):
                p.op(TEN, lambda e, h=h, mt=mt: e.matmul(
                    s.PT[0:64, h * 64:(h + 1) * 64],
                    mt[0:64, h * 64:(h + 1) * 64], s.IDENTB[0:64, 0:64],
                    is_transpose=True, start=True, stop=True),
                    waits=([("Vm", vmt)] if h == 0 else ()),
                    inc=("Tm" if h == NH - 1 else None))
            tmt = p.val("Tm")
            mtf = s.p16(s.P2_MT, 1024)
            p.op(VEC, lambda e, mtf=mtf: e.tensor_scalar(
                mtf[0:64, :], s.PT[0:64, 0:1024], 1.0, None, ALU.mult),
                waits=[("Tm", tmt)], inc="Vm")
            vmtf = p.val("Vm")
            for h in range(NH):
                p.op(TEN, lambda e, h=h, mtf=mtf: e.matmul(
                    s.PE[0:64, h * 64:(h + 1) * 64] if h < 8 else
                    s.PF[0:64, (h - 8) * 64:(h - 7) * 64],
                    mtf[0:64, h * 64:(h + 1) * 64],
                    s.SS[:, h * HD:(h + 1) * HD], start=True, stop=True),
                    waits=([("Vm", vmtf), ("Vm", v_ss), ("Vm", pe_free)]
                           if h == 0 else ()),
                    inc=("Tm" if h == NH - 1 else None))
            tss = p.val("Tm")
            p.op(VEC, lambda e, fg=fg, j=j: e.scalar_tensor_tensor(
                s.SS[:, 0:8 * HD].rearrange("p (h c) -> p h c", c=HD),
                fg[0:64, :].rearrange("p (h c) -> p h c", c=2 * HD)
                [:, 0:8, HD:2 * HD],
                s.FOLDA[:, j:j + 1],
                s.PE[0:64, 0:512].rearrange("p (h c) -> p h c", c=HD),
                ALU.mult, ALU.add), waits=[("Tm", tss)], inc="Vm")
            p.op(VEC, lambda e, fg=fg, j=j: e.scalar_tensor_tensor(
                s.SS[:, 8 * HD:].rearrange("p (h c) -> p h c", c=HD),
                fg[0:64, :].rearrange("p (h c) -> p h c", c=2 * HD)
                [:, 8:16, HD:2 * HD],
                s.FOLDA[:, j:j + 1],
                s.PF[0:64, 0:512].rearrange("p (h c) -> p h c", c=HD),
                ALU.mult, ALU.add), inc="Vm")
            v_ss = p.val("Vm")
            pe_free = v_ss
            mt_tmfree = tss
            fg_free[par] = p.val("Tm")
            fg_vfree[par] = p.val("Vm")
        s.v_ss = p.val("Vm")

    def emit_pass2(self):
        s, p = self, self.P
        p.op(VEC, lambda e: e.tensor_scalar(
            s.STRUE[:, :], s.SS[:, :], 1.0, None, ALU.mult),
            waits=[("Vm", s.v_ss)], inc="Vm")
        v_st = p.val("Vm")
        rings = {}
        dring = {"mc": [0, 0], "q2": [0, 0], "z": [0, 0], "o1": [0, 0]}
        rfree = {"mc": [0, 0], "q2": [0, 0], "z": [0, 0], "o1": [0, 0]}
        rvfree = {"mc": [0, 0], "q2": [0, 0], "z": [0, 0], "o1": [0, 0]}
        ofm = s.p16(s.P2_OFM, 16384)
        pe_free = 0; pf_free = 0; pt_free = 0
        v_ofm = 0
        for c in range(NCH):
            par = c % 2
            mcr = s.p16(s.P2_MC + par * 1024, 1024)
            q2r = s.p16(s.P2_Q2 + par * 1024, 1024)
            zr = s.p16(s.P2_Z + par * 2048, 2048)
            o1r = s.p16(s.P2_O1 + par * 1024, 1024)
            for nm, dst, src, w in (("mc", mcr, s.mc_s, HD),
                                    ("q2", q2r, s.q2_s, HD),
                                    ("z", zr, s.z_s, CH),
                                    ("o1", o1r, s.o1_s, HD)):
                p.dma(SYN, dst[0:128 if nm == "o1" else 64, :]
                      .rearrange("p (h c) -> p h c", c=w),
                      src[c].rearrange("h p c -> p h c"),
                      waits=[("Tm", rfree[nm][par]), ("Vm", rvfree[nm][par]),
                             ("dZ0", s.spl["z"][0]), ("dZ1", s.spl["z"][1]),
                             ("dMc0", s.spl["mc"][0]), ("dMc1", s.spl["mc"][1]),
                             ("dQ20", s.spl["q2"][0]), ("dQ21", s.spl["q2"][1]),
                             ("dO10", s.spl["o1"][0]), ("dO11", s.spl["o1"][1])],
                      inc="dSl%d" % par)
            dr = p.val("dSl%d" % par)
            for h in range(NH):
                bank = s.PBIG[:, (h % 4) * 512:(h % 4) * 512 + 512]
                p.op(TEN, lambda e, h=h, zr=zr, bank=bank: e.matmul(
                    bank[:, 0:64], zr[0:64, h * CH:(h + 1) * CH],
                    s.STRUE[:, h * HD:(h + 1) * HD], start=True, stop=True),
                    waits=[("dSl%d" % par, dr), ("Vm", v_st),
                           ("Vm", s._p2big_free[h % 4])], inc="Tm")
                t_o = p.val("Tm")
                otm = s.p32(1536, 64)
                p.op(VEC, lambda e, h=h, o1r=o1r, bank=bank, otm=otm:
                     e.tensor_tensor(otm, bank[:, 0:64],
                                     o1r[:, h * HD:(h + 1) * HD], ALU.add),
                     waits=[("Tm", t_o), ("Tm", s._otm_free)], inc="Vm")
                v_otm = p.val("Vm")
                p.op(TEN, lambda e, otm=otm: e.matmul(
                    s.PF[0:64, 0:128], otm, s.IDENT[:, :],
                    is_transpose=True, start=True, stop=True),
                    waits=[("Vm", v_otm), ("Vm", pf_free)], inc="Tm")
                s._otm_free = p.val("Tm")
                t_t = p.val("Tm")
                fc = h // 2
                rb = 64 * (h % 2)
                p.op(VEC, lambda e, fc=fc, rb=rb, c=c, ofm=ofm: e.tensor_scalar(
                    ofm[rb:rb + 64, fc * TC + c * CH: fc * TC + c * CH + CH],
                    s.PF[0:64, 0:128], 1.0, None, ALU.mult),
                    waits=[("Tm", t_t)], inc="Vm")
                pf_free = p.val("Vm")
                v_ofm = p.val("Vm")
                p.op(TEN, lambda e, h=h, mcr=mcr: e.matmul(
                    bank[:, 64:128][0:64, :] if False else
                    s.PE[0:64, (h % 8) * 64:(h % 8) * 64 + 64],
                    mcr[0:64, h * HD:(h + 1) * HD],
                    s.STRUE[:, h * HD:(h + 1) * HD], start=True, stop=True),
                    waits=[("Vm", pe_free)], inc="Tm")
                t_s = p.val("Tm")
                p.op(VEC, lambda e, h=h, q2r=q2r: e.tensor_tensor(
                    s.STRUE[:, h * HD:(h + 1) * HD],
                    s.PE[0:64, (h % 8) * 64:(h % 8) * 64 + 64],
                    q2r[0:64, h * HD:(h + 1) * HD], ALU.add),
                    waits=[("Tm", t_s)], inc="Vm")
                v_st = p.val("Vm")
                pe_free = v_st
                s._p2big_free[h % 4] = p.val("Vm")
            for nm in rfree:
                rfree[nm][par] = p.val("Tm")
                rvfree[nm][par] = p.val("Vm")
        s.v_ofm = p.val("Vm")
        scr = s.p16(s.P2_MC, 2048)
        for fc in range(NFC):
            p.op(SCA, lambda e, fc=fc, ofm=ofm, scr=scr: e.activation(
                scr, ofm[:, fc * TC:(fc + 1) * TC], AF.Identity,
                accum_out=s.GSTATS[:, fc:fc + 1]),
                waits=[("Vm", s.v_ofm)], inc="Sm")
            p.op(SCA, lambda e, fc=fc, ofm=ofm, scr=scr: e.activation(
                scr, ofm[:, fc * TC:(fc + 1) * TC], AF.Square,
                accum_out=s.GSTATS[:, 8 + fc:9 + fc]), inc="Sm")
        sg = p.val("Sm")
        for fc in range(NFC):
            p.op(TEN, lambda e, fc=fc: e.matmul(
                s.PE[0:16, 0:1], s.SELSUM32[:, fc * 16:(fc + 1) * 16],
                s.GSTATS[:, fc:fc + 1], start=(fc == 0), stop=(fc == NFC - 1)),
                waits=([("Sm", sg), ("Vm", pe_free)] if fc == 0 else ()))
            p.op(TEN, lambda e, fc=fc: e.matmul(
                s.PF[0:16, 0:1], s.SELSUM32[:, fc * 16:(fc + 1) * 16],
                s.GSTATS[:, 8 + fc:9 + fc], start=(fc == 0),
                stop=(fc == NFC - 1)),
                waits=([("Vm", pf_free)] if fc == 0 else ()),
                inc=("Tm" if fc == NFC - 1 else None))
        tg = p.val("Tm")
        p.op(VEC, lambda e: e.tensor_scalar(
            s.MRS[:, 0:1], s.PE[0:16, 0:1], 1.0, None, ALU.mult),
            waits=[("Tm", tg)], inc="Vm")
        p.op(VEC, lambda e: e.tensor_scalar(
            s.MRS[:, 1:2], s.PF[0:16, 0:1], 1.0, None, ALU.mult), inc="Vm")
        vg = p.val("Vm")
        p.dma(SYN, s.ag3_in, s.MRS[:, 0:2], waits=[("Vm", vg)], inc="dT")
        d3 = p.val("dT")
        p.op(GPS, lambda e: e.collective_compute(
            "AllGather", ALU.bypass, replica_groups=[list(range(self.ncores))],
            ins=[s.ag3_in.opt()], outs=[s.ag3_out.opt()]),
            waits=[("dT", d3), ("cc", s._cc_done)], inc="cc")
        s._cc_done = p.val("cc")
        s._p2wo_free = p.val("Tm")
        s._p2wo_sfree = p.val("Sm")

    def emit_attn_end(self):
        s, p = self, self.P
        nr = self.ncores
        gsb = s.p32(1600, 2)
        p.dma(GPS, gsb[0:nr * 16, :], s.ag3_out,
              waits=[("cc", s._cc_done)], inc="dG2")
        dg = p.val("dG2")
        p.op(TEN, lambda e: e.matmul(
            s.PE[0:16, 0:2], s.FOLD8[0:16 * nr, 0:16], gsb[0:16 * nr, :],
            start=True, stop=True), waits=[("dG2", dg)], inc="Tm")
        tg = p.val("Tm")
        N_ = float(TC * self.ncores * HD)
        V = lambda fn, w=(): p.op(VEC, fn, waits=w, inc="Vm")
        S_ = lambda fn, w=(): p.op(SCA, fn, waits=w, inc="Sm")
        V(lambda e: e.tensor_scalar(s.MRS[:, 2:4], s.PE[0:16, 0:2], 1.0 / N_,
                                    None, ALU.mult), w=[("Tm", tg)])
        V(lambda e: e.tensor_tensor(s.MRS[:, 4:5], s.MRS[:, 2:3],
                                    s.MRS[:, 2:3], ALU.mult))
        V(lambda e: e.tensor_tensor(s.MRS[:, 3:4], s.MRS[:, 3:4],
                                    s.MRS[:, 4:5], ALU.subtract))
        vv = p.val("Vm")
        S_(lambda e: e.activation(s.MRS[:, 3:4], s.MRS[:, 3:4], AF.Sqrt,
                                  bias=s.EPSC[0:16, 1:2]), w=[("Vm", vv)])
        sv = p.val("Sm")
        V(lambda e: e.reciprocal(s.MRS[:, 3:4], s.MRS[:, 3:4]), w=[("Sm", sv)])
        V(lambda e: e.tensor_tensor(s.MRS[:, 4:5], s.MRS[:, 2:3],
                                    s.MRS[:, 3:4], ALU.mult))
        vmrs = p.val("Vm")
        for fc in range(NFC):
            p.op(TEN, lambda e, fc=fc: e.matmul(
                s.PE[0:128, 4 + 2 * fc: 6 + 2 * fc],
                s.EXPAND32[:, fc * 128:(fc + 1) * 128], s.MRS[:, 3:5],
                start=True, stop=True), waits=[("Vm", vmrs)],
                inc=("Tm" if fc == NFC - 1 else None))
        te = p.val("Tm")
        for fc in range(NFC):
            V(lambda e, fc=fc: e.tensor_tensor(
                s.COEF[:, fc:fc + 1], s.GNS[:, fc:fc + 1],
                s.PE[0:128, 4 + 2 * fc: 5 + 2 * fc], ALU.mult),
              w=([("Tm", te), ("dT", s.c_consts)] if fc == 0 else ()))
            V(lambda e, fc=fc: e.tensor_tensor(
                s.COEF[:, 8 + fc: 9 + fc], s.GNS[:, fc:fc + 1],
                s.PE[0:128, 5 + 2 * fc: 6 + 2 * fc], ALU.mult))
            V(lambda e, fc=fc: e.tensor_tensor(
                s.COEF[:, 8 + fc: 9 + fc], s.GNB[:, fc:fc + 1],
                s.COEF[:, 8 + fc: 9 + fc], ALU.subtract))
        v_coef = p.val("Vm")
        wslots = [s.p16(s.EN_WO, 8192), s.p16(s.EN_WO + 8192, 8192)]
        p.dma(GPS, wslots[0].rearrange("p (kc c) -> p kc c", c=DM),
              s.wob.rearrange("(kc p) c -> p kc c", p=128),
              waits=[("dW", s.wcast_at["wo"]), ("Tm", s._p2wo_free),
                     ("Sm", s._p2wo_sfree), ("dUp0", p.val("dUp0")),
                     ("dUp1", p.val("dUp1"))],
              inc="dUp0")
        dwo = p.val("dUp0")
        ofm = s.p16(s.P2_OFM, 16384)
        xwo = s.p16(s.EN_XWO, 4096)
        dog = [0, 0]
        og_free = [0, 0]
        dx1 = [0, 0]
        x1_free = [0, 0]
        for j in range(NTT):
            js = slice(j * 512, (j + 1) * 512)
            for fc in range(NFC):
                par = fc % 2
                ogr = s.p16(s.EN_OG + par * 512, 512)
                p.dma(SYN, ogr, s.og_s[fc * 128:(fc + 1) * 128, js],
                      waits=[("dSp0", s._qs_sp0), ("dSp1", s._qs_sp1),
                             ("Vm", og_free[par])], inc="dO1%d" % par)
                dog[par] = p.val("dO1%d" % par)
                p.op(VEC, lambda e, fc=fc, j=j, ofm=ofm, xwo=xwo: e.tensor_scalar(
                    xwo[:, fc * 512:(fc + 1) * 512],
                    ofm[:, fc * TC + j * 512: fc * TC + j * 512 + 512],
                    s.COEF[:, fc:fc + 1], s.COEF[:, 8 + fc: 9 + fc],
                    ALU.mult, ALU.add),
                    waits=[("Vm", v_coef), ("Vm", s.v_ofm),
                           ("Tm", s._xwo_free)], inc="Vm")
                p.op(VEC, lambda e, fc=fc, ogr=ogr, xwo=xwo: e.tensor_tensor(
                    xwo[:, fc * 512:(fc + 1) * 512],
                    xwo[:, fc * 512:(fc + 1) * 512], ogr, ALU.mult),
                    waits=[("dO1%d" % par, dog[par])], inc="Vm")
                og_free[par] = p.val("Vm")
            v_xwo = p.val("Vm")
            for oc in range(NFC):
                q_ = s.PBIG[:, (oc % 4) * 512:(oc % 4) * 512 + 512]
                for kc in range(NFC):
                    p.op(TEN, lambda e, q=q_, kc=kc, oc=oc, xwo=xwo: e.matmul(
                        q, wslots[0][:, kc * DM + oc * 128:
                                     kc * DM + oc * 128 + 128],
                        xwo[:, kc * 512:(kc + 1) * 512],
                        start=(kc == 0), stop=(kc == NFC - 1)),
                        waits=([("dUp0", dwo), ("Vm", v_xwo),
                                ("Vm", s._en_qfree[oc % 4])] if kc == 0
                               else ()),
                        inc=("Tm" if kc == NFC - 1 else None))
                tm = p.val("Tm")
                s._xwo_free = tm
                par = oc % 2
                x1r = s.p16(s.EN_X1 + par * 512, 512)
                p.dma(SYN, x1r, s.x1_s[oc * 128:(oc + 1) * 128, js],
                      waits=[("Vm", x1_free[par])], inc="dX1%d" % par)
                dx = p.val("dX1%d" % par)
                p.op(VEC, lambda e, q=q_, oc=oc, j=j, x1r=x1r:
                     e.scalar_tensor_tensor(
                         s.Xb[:, oc * TC + j * 512: oc * TC + j * 512 + 512],
                         q, s.BO[:, oc:oc + 1], x1r, ALU.add, ALU.add),
                     waits=[("Tm", tm), ("dX1%d" % par, dx)], inc="Vm")
                s._en_qfree[oc % 4] = p.val("Vm")
                x1_free[par] = p.val("Vm")
        s.v_x2 = p.val("Vm")

    def stage_barrier(self):
        p = self.P
        cnt = {x: p.val(x) for x in ("Tm", "Vm", "Sm", "Gm")}
        dma = {x: p.val(x) for x in ("dT", "dHb", "dHs", "dX",
                                     "dX10", "dX11", "dF0", "dF1",
                                     "dSp0", "dSp1", "dSl0", "dSl1",
                                     "dZ0", "dZ1", "dMc0", "dMc1",
                                     "dQ20", "dQ21", "dO10", "dO11",
                                     "dUp0", "dUp1", "dDn")}
        for eng in (TEN, VEC, SCA, GPS):
            for sem, v in cnt.items():
                p.barrier_wait(eng, sem, v)
            for sem, v in dma.items():
                p.barrier_wait(eng, sem, v)

    def emit_all(self):
        self.emit_loads()
        self.emit_ln(xwaits=[("dG", self.c_x)])
        self.emit_router(xwaits=[("dG", self.c_x)])
        self.emit_ffn(stage=1)
        self.stage_barrier()
        self.emit_attn_proj()
        self.stage_barrier()
        self.emit_scan()
        self.stage_barrier()
        self.emit_fold()
        self.stage_barrier()
        self.emit_pass2()
        self.stage_barrier()
        self.emit_attn_end()
        self.stage_barrier()
        self._lnq_free = [self.P.val("Vm")] * NTT
        self.emit_ln(xwaits=[("Vm", self.v_x2)])
        self.emit_router()
        self.emit_ffn(stage=2)
        self.finish()


_NC_CACHE = {}


def _build_full():
    if "nc" not in _NC_CACHE:
        b = Builder()
        b.emit_all()
        _NC_CACHE["nc"] = b.build()
    return _NC_CACHE["nc"]


def kernel(**inputs):
    nc = _build_full()
    in_maps = [_prep_core_inputs(inputs, core) for core in range(NCORES)]
    res = run_bass_kernel_spmd(nc, in_maps, core_ids=list(range(NCORES)))
    out = np.concatenate([np.asarray(r["y"]).T for r in res.results], axis=0)
    return out.reshape(1, T, DM).astype(np.float32)


def _prep_core_inputs(full, core):
    f32 = np.float32
    t0 = core * TC
    x = np.ascontiguousarray(full["L0_attn_x"][0, t0:t0 + TC, :].T).astype(f32)
    conv = np.concatenate([full["L0_attn_q_conv"], full["L0_attn_k_conv"],
                           full["L0_attn_v_conv"]], axis=1)
    convw = np.zeros((128, NFC * 12), f32)
    for fc in range(NFC):
        convw[:, fc * 12:(fc + 1) * 12] = conv[fc * 128:(fc + 1) * 128, :]
    bqkvg = np.zeros((128, 4 * NFC), f32)
    for i, b in enumerate(["bq", "bk", "bv", "bg"]):
        bqkvg[:, i * NFC:(i + 1) * NFC] = np.asarray(
            full["L0_attn_" + b]).reshape(NFC, 128).T
    bo_c = np.asarray(full["L0_attn_bo"]).reshape(NFC, 128).T.astype(f32)
    bab = np.stack([full["L0_attn_ba"], full["L0_attn_bb"]], axis=1).astype(f32)
    gns = full["L0_attn_gn_scale"].reshape(NFC, 128).T.astype(f32)
    gnb = full["L0_attn_gn_bias"].reshape(NFC, 128).T.astype(f32)
    dwn = full["L0_ffn_down_weights"]
    dwn_t = np.transpose(dwn, (1, 2, 0))
    down_r = np.zeros((NFC, E, 4, 128, 128), f32)
    for oc in range(NFC):
        for e_ in range(E):
            blk = dwn[e_, oc * 128:(oc + 1) * 128, :]
            for ic in range(4):
                down_r[oc, e_, ic] = blk[:, ic * 128:(ic + 1) * 128].T
    shd = full["L0_ffn_shared_down_weight"]
    shdown_r = np.zeros((NFC, 16, 128, 128), f32)
    for oc in range(NFC):
        blk = shd[oc * 128:(oc + 1) * 128, :]
        for sc in range(16):
            shdown_r[oc, sc] = blk[:, sc * 128:(sc + 1) * 128].T
    tri2 = np.zeros((128, 256), f32)
    sidx = np.arange(128)[:, None]
    tidx = np.arange(128)[None, :]
    tri2[:, 0:128] = np.where(sidx < tidx, 0.0, -60000.0)
    tri2[:, 128:256] = np.where(sidx <= tidx, 0.0, -60000.0)
    onescol = np.zeros((128, 2), f32); onescol[:, 0] = 1.0
    selsum = np.zeros((128, NFC * 16), f32)
    expand = np.zeros((16, NFC * 128), f32)
    for fc in range(NFC):
        selsum[0:64, fc * 16 + 2 * fc] = 1.0
        selsum[64:128, fc * 16 + 2 * fc + 1] = 1.0
        expand[2 * fc, fc * 128: fc * 128 + 64] = 1.0
        expand[2 * fc + 1, fc * 128 + 64: fc * 128 + 128] = 1.0
    iota8 = np.tile(np.arange(8, dtype=f32)[None, :], (128, 1))
    onehot = np.zeros((8, 1), f32)
    if core > 0:
        onehot[core - 1, 0] = 1.0
    foldA = np.zeros((64, 7), f32)
    foldA[:, :core] = 1.0
    foldB = np.zeros((64, 7 * 64), f32)
    for j in range(7):
        if j >= core:
            foldB[:, j * 64:(j + 1) * 64] = np.eye(64, dtype=f32)
    fold8 = np.zeros((128, 16), f32)
    for r_ in range(8):
        fold8[r_ * 16 + np.arange(16), np.arange(16)] = 1.0
    d = {
        "x0": x,
        "wq": full["L0_attn_Wq"].astype(f32), "wk": full["L0_attn_Wk"].astype(f32),
        "wv": full["L0_attn_Wv"].astype(f32), "wg": full["L0_attn_Wg"].astype(f32),
        "wo": full["L0_attn_Wo"].astype(f32),
        "bqkvg": bqkvg, "bo_c": bo_c, "bab": bab,
        "wa": full["L0_attn_Wa"].astype(f32), "wb": full["L0_attn_Wb"].astype(f32),
        "convw": convw, "gns": gns, "gnb": gnb,
        "gatew": np.ascontiguousarray(full["L0_ffn_gate_weight"].T).astype(f32),
        "gateb": full["L0_ffn_gate_bias"].reshape(8, 1).astype(f32),
        "upw": np.ascontiguousarray(np.transpose(
            full["L0_ffn_up_weights"], (0, 2, 1))).astype(f32),
        "downw": down_r, "shdownw": shdown_r,
        "shupw": np.ascontiguousarray(full["L0_ffn_shared_up_weight"].T).astype(f32),
        "onehot": onehot, "h0m": np.array([[0.0 if core == 0 else 1.0]], f32),
        "foldA": foldA, "foldB": foldB, "fold8": fold8,
        "ident": np.eye(128, dtype=f32),
        "identb": np.eye(128, dtype=f32).astype(BFNP),
        "tri2": tri2, "onescol": onescol.astype(BFNP), "onescol32": onescol,
        "selsum": selsum.astype(BFNP), "expand": expand.astype(BFNP),
        "selsum32": selsum, "expand32": expand,
        "iota8": iota8, "onesr": np.ones((1, 128), f32), "ones4": _ones4(),
        "epsc": np.tile(np.array([[LNEPS, GNEPS]], f32), (128, 1)),
        "gateb128": np.tile(np.asarray(full["L0_ffn_gate_bias"],
                                       dtype=f32)[None, :], (128, 1)),
        "zrow": np.zeros((16, 512), f32),
    }
    return d


def _bias_row(v, n):
    return v.reshape(1, n)


def _ones4():
    o = np.zeros((128, 128), np.float32)
    for r in (0, 32, 64, 96):
        o[r, :] = 1.0
    return o


